# version 16
# baseline (speedup 1.0000x reference)
import contextlib
import numpy as np
import concourse.bass as bass
import concourse.mybir as mybir
from concourse.bass_utils import run_bass_kernel_spmd

F32 = mybir.dt.float32
BF16 = mybir.dt.bfloat16
AF = mybir.ActivationFunctionType
AL = mybir.AluOpType
AX = mybir.AxisListType.X

E, T, S, H, HD = 1024, 2048, 8192, 16, 64
NJ = 8
NEXP = 8

OFF_KHT = 0
OFF_KN = 131072
OFF_QN = 262144
OFF_VB = 393216
OFF_ST = 526336
HE = 526400
SHARD = 4 * HE


class Sem:
    def __init__(self, nc, es, name):
        self.h = es.enter_context(nc.semaphore(name))
        self.n = 0

    def inc(self, instr, k=1):
        instr.then_inc(self.h, k)
        self.n += k
        return instr


class B:
    def __init__(self, nc, block, es):
        self.nc, self.block, self.es = nc, block, es
        self._sems = {}

    def sem(self, name):
        if name not in self._sems:
            self._sems[name] = Sem(self.nc, self.es, name)
        return self._sems[name]

    def fence(self, exclude=("a_cc",)):
        snap = [(s.h, s.n) for k, s in self._sems.items() if s.n > 0 and k not in exclude]
        for name in ("tensor", "vector", "scalar"):
            def f(e, snap=snap):
                for h, n in snap:
                    e.wait_ge(h, n)
            getattr(self.block, name)(f)



def emit_ln(b, t, src_jc, xh, sq, n1, nb):
    block = b.block
    PS = t["psum"]
    W = nb * 512
    ld = b.sem("ln_ld")
    ssq = b.sem("ln_sq")
    sst = b.sem("ln_st")
    srow = b.sem("ln_row")
    sbc = b.sem("ln_bc")
    snm = b.sem("ln_nm")
    ones128 = t["ones128"]
    ones1 = t["ones1x128"]
    A = t["arena"]
    R = t["rowoff"]
    murow = A[0:1, R: R + W]
    varow = A[0:1, R + 1024: R + 1024 + W]
    mu2 = A[0:1, R + 2048: R + 2048 + W]
    mub = t["mub"][:, 0:W]
    rb = t["rb"][:, 0:W]

    @block.sync
    def _(s):
        for nm in ("f_to", "a_to"):
            ss = b.sem(nm)
            s.wait_ge(ss.h, ss.n)
        s.wait_ge(snm.h, snm.n)
        s.wait_ge(sst.h, sst.n)
        for jc in range(NJ):
            ld.inc(s.dma_start(out=xh(jc), in_=src_jc(jc)), 16)
    ld_goal = ld.n

    @block.scalar
    def _(sc):
        sc.wait_ge(ld.h, ld_goal)
        for jc in range(NJ):
            ins = sc.activation(out=sq(jc), in_=xh(jc), func=AF.Square)
        ssq.inc(ins)
    ssq_goal = ssq.n

    @block.tensor
    def _(tn):
        tn.wait_ge(ld.h, ld_goal)
        for nbi in range(nb):
            for jc in range(NJ):
                ins = tn.matmul(PS[nbi % 2][0:1, 0:512], ones128[:, 0:1],
                                xh(jc)[:, nbi * 512:(nbi + 1) * 512],
                                start=(jc == 0), stop=(jc == NJ - 1))
            sst.inc(ins)
        tn.wait_ge(ssq.h, ssq_goal)
        for nbi in range(nb):
            for jc in range(NJ):
                ins = tn.matmul(PS[2 + nbi % 2][0:1, 0:512], ones128[:, 0:1],
                                sq(jc)[:, nbi * 512:(nbi + 1) * 512],
                                start=(jc == 0), stop=(jc == NJ - 1))
            sst.inc(ins)
    base = sst.n - 2 * nb

    @block.vector
    def _(v):
        for nbi in range(nb):
            v.wait_ge(sst.h, base + nbi + 1)
            v.tensor_scalar(out=murow[:, nbi * 512:(nbi + 1) * 512],
                            in0=PS[nbi % 2][0:1, 0:512], scalar1=1.0 / E,
                            scalar2=None, op0=AL.mult)
        for nbi in range(nb):
            v.wait_ge(sst.h, base + nb + nbi + 1)
            ins = v.tensor_scalar(out=varow[:, nbi * 512:(nbi + 1) * 512],
                                  in0=PS[2 + nbi % 2][0:1, 0:512], scalar1=1.0 / E,
                                  scalar2=None, op0=AL.mult)
        srow.inc(ins)
        v.wait_ge(srow.h, srow.n)
        v.tensor_tensor(out=mu2, in0=murow, in1=murow, op=AL.mult)
        ins = v.tensor_tensor(out=varow, in0=varow, in1=mu2, op=AL.subtract)
        srow.inc(ins)
        v.wait_ge(srow.h, srow.n)
        ins = v.tensor_scalar(out=varow, in0=varow, scalar1=1e-5, scalar2=None, op0=AL.add)
        srow.inc(ins)
        v.wait_ge(srow.h, srow.n)
        ins = v.reciprocal(out=varow, in_=varow)
        srow.inc(ins)
    srow_goal = srow.n

    @block.scalar
    def _(sc):
        sc.wait_ge(srow.h, srow_goal)
        ins = sc.activation(out=varow, in_=varow, func=AF.Sqrt)
        srow.inc(ins)
    srow_goal2 = srow.n

    @block.tensor
    def _(tn):
        tn.wait_ge(srow.h, srow_goal2)
        for nbi in range(nb):
            sbc.inc(tn.matmul(PS[4 + nbi % 2][:, 0:512], ones1[0:1, :],
                              murow[:, nbi * 512:(nbi + 1) * 512], start=True, stop=True))
            sbc.inc(tn.matmul(PS[6 + nbi % 2][:, 0:512], ones1[0:1, :],
                              varow[:, nbi * 512:(nbi + 1) * 512], start=True, stop=True))
    bbase = sbc.n - 2 * nb

    @block.scalar
    def _(sc):
        for nbi in range(nb):
            sc.wait_ge(sbc.h, bbase + 2 * nbi + 2)
            sc.copy(out=mub[:, nbi * 512:(nbi + 1) * 512], in_=PS[4 + nbi % 2][:, 0:512])
            ins = sc.copy(out=rb[:, nbi * 512:(nbi + 1) * 512], in_=PS[6 + nbi % 2][:, 0:512])
        sbc.inc(ins)
    sbc_goal = sbc.n

    @block.vector
    def _(v):
        v.wait_ge(sbc.h, sbc_goal)
        for jc in range(NJ):
            v.tensor_tensor(out=n1(jc), in0=xh(jc), in1=mub, op=AL.subtract)
        for jc in range(NJ):
            ins = v.tensor_tensor(out=n1(jc), in0=n1(jc), in1=rb, op=AL.mult)
        snm.inc(ins)



def emit_ffn(b, t, src_d, dst_d, w, dbg_d=None):
    block = b.block
    A = t["arena"]
    PS = t["psum"]
    M = t["fmap"]
    HT = 1024
    NB = 2
    ld_w = b.sem("f_w")
    ldh = b.sem("f_wh")
    smm = b.sem("f_mm")
    sev = b.sem("f_ev")
    scb = b.sem("f_cb")
    sra = b.sem("f_ra")
    sgu = b.sem("f_gu")
    stp = b.sem("f_tp")
    svv = b.sem("f_vv")
    sso = b.sem("f_so")
    sto = b.sem("f_to")
    ones1 = t["ones1x128"]
    half1 = t["half1x128"]
    ident = t["ident"]

    xh = lambda jc: A[:, M["XH"] + jc * HT: M["XH"] + (jc + 1) * HT]
    n1 = lambda jc: A[:, M["N1"] + jc * HT: M["N1"] + (jc + 1) * HT]
    hb = lambda fc: A[:, M["HB"] + fc * HT: M["HB"] + (fc + 1) * HT]
    racc = lambda ic: A[:, M["RACC"] + ic * HT: M["RACC"] + (ic + 1) * HT]
    cbt = A[:, M["CBT"]: M["CBT"] + HT]
    sigb = A[:, M["SIGB"]: M["SIGB"] + HT]
    gs = A[:, M["GS"]: M["GS"] + HT]
    tmp = A[:, M["TMP"]: M["TMP"] + 512]
    lg = A[0:8, M["LG"]: M["LG"] + HT]
    lgt = A[:, M["LGT"]: M["LGT"] + 8]
    cbm = A[0:8, M["CBM"]: M["CBM"] + HT]
    cbm2 = A[0:1, M["CBM2"]: M["CBM2"] + HT]
    sigrow = A[0:1, M["SIGROW"]: M["SIGROW"] + HT]
    m1c = A[:, M["SML"]: M["SML"] + 1]
    m2c = A[:, M["SML"] + 1: M["SML"] + 2]
    eqt = A[:, M["SML"] + 8: M["SML"] + 16]
    mskt = A[:, M["SML"] + 16: M["SML"] + 24]
    smet = A[:, M["SML"] + 24: M["SML"] + 32]
    smes = A[:, M["SML"] + 32: M["SML"] + 33]
    wb = lambda p: A[:, M["WB"] + p * 1024: M["WB"] + (p + 1) * 1024]

    for hh in range(2):
        c0 = hh * HT
        b.fence()
        emit_ln(b, t, lambda jc, c0=c0: src_d[jc * 128:(jc + 1) * 128, c0:c0 + HT],
                xh, hb, n1, NB)
        b.fence()

        @block.tensor
        def _(tn):
            for nbi in range(NB):
                for jc in range(NJ):
                    ins = tn.matmul(PS[nbi][0:8, 0:512], t["rw_sb"][:, jc * 8:(jc + 1) * 8],
                                    n1(jc)[:, nbi * 512:(nbi + 1) * 512],
                                    start=(jc == 0), stop=(jc == NJ - 1))
                smm.inc(ins)
        mbase = smm.n - NB

        @block.scalar
        def _(sc):
            for nbi in range(NB):
                sc.wait_ge(smm.h, mbase + nbi + 1)
                ins = sc.copy(out=lg[:, nbi * 512:(nbi + 1) * 512], in_=PS[nbi][0:8, 0:512])
            sev.inc(ins)
        lg_done = sev.n

        for tt in range(8):
            @block.tensor
            def _(tn):
                tn.wait_ge(sev.h, lg_done)
                tn.wait_ge(stp.h, stp.n)
                tn.wait_ge(scb.h, scb.n)
                stp.inc(tn.transpose(PS[4][0:128, 0:8], lg[:, tt * 128:(tt + 1) * 128],
                                     ident[0:8, 0:8]))
            tp1 = stp.n

            @block.vector
            def _(v):
                v.wait_ge(stp.h, tp1)
                ins = v.tensor_copy(out=lgt, in_=PS[4][0:128, 0:8])
                stp.inc(ins)
                v.wait_ge(stp.h, stp.n)
                ins = v.reduce_max(m1c, lgt, axis=AX)
                svv.inc(ins)
            g1 = svv.n

            @block.scalar
            def _(sc):
                sc.wait_ge(svv.h, g1)
                ins = sc.mul(m2c, m1c, -1.0)
                svv.inc(ins)
                sc.wait_ge(svv.h, svv.n)
                ins = sc.activation(out=smet, in_=lgt, func=AF.Exp, bias=m2c[:, 0:1])
                svv.inc(ins)
            g2 = svv.n

            @block.vector
            def _(v):
                v.wait_ge(svv.h, g2)
                ins = v.reduce_sum(smes, smet, axis=AX)
                svv.inc(ins)
                v.wait_ge(svv.h, svv.n)
                ins = v.reciprocal(out=smes, in_=smes)
                svv.inc(ins)
                v.wait_ge(svv.h, svv.n)
                ins = v.tensor_scalar(out=smet, in0=smet, scalar1=smes[:, 0:1], scalar2=None, op0=AL.mult)
                svv.inc(ins)
                v.wait_ge(svv.h, svv.n)
                ins = v.reduce_max(m1c, smet, axis=AX)
                svv.inc(ins)
                v.wait_ge(svv.h, svv.n)
                ins = v.tensor_scalar(out=eqt, in0=smet, scalar1=m1c[:, 0:1], scalar2=None, op0=AL.is_equal)
                svv.inc(ins)
                v.wait_ge(svv.h, svv.n)
                ins = v.scalar_tensor_tensor(out=mskt, in0=eqt, scalar=-1e9, in1=smet, op0=AL.mult, op1=AL.add)
                svv.inc(ins)
                v.wait_ge(svv.h, svv.n)
                ins = v.reduce_max(m2c, mskt, axis=AX)
                svv.inc(ins)
                v.wait_ge(svv.h, svv.n)
                ins = v.tensor_scalar(out=eqt, in0=smet, scalar1=m2c[:, 0:1], scalar2=None, op0=AL.is_ge)
                svv.inc(ins)
                v.wait_ge(svv.h, svv.n)
                ins = v.tensor_tensor(out=lgt, in0=smet, in1=eqt, op=AL.mult)
                svv.inc(ins)
            g3 = svv.n

            @block.tensor
            def _(tn):
                tn.wait_ge(svv.h, g3)
                tn.wait_ge(stp.h, stp.n)
                stp.inc(tn.transpose(PS[4][0:8, 0:128], lgt, ident[0:128, 0:128]))
            tp2 = stp.n

            @block.scalar
            def _(sc):
                sc.wait_ge(stp.h, tp2)
                ins = sc.copy(out=cbm[:, tt * 128:(tt + 1) * 128], in_=PS[4][0:8, 0:128])
                scb.inc(ins)
        cbm_ready = scb.n
        if dbg_d is not None and hh == 0:
            snm_now = b.sem("ln_nm").n
            @block.sync
            def _(s):
                s.wait_ge(b.sem("ln_nm").h, snm_now)
                s.wait_ge(scb.h, cbm_ready)
                sto.inc(s.dma_start(out=dbg_d[0:128, 0:1024], in_=n1(0)), 16)
                sto.inc(s.dma_start(out=dbg_d[0:8, 1024:2048], in_=cbm), 16)
        b.fence()

        AB = A[:, :].bitcast(BF16)
        b16 = lambda c0, w: AB[:, 2 * c0: 2 * c0 + 2 * w]
        n1hi = lambda jc: b16(M["XH"] + jc * 512, 512)
        n1lo = lambda jc: b16(M["XH"] + 4096 + jc * 512, 512)
        hshi = lambda fc: b16(M["HB"] + fc * 512, 512)
        hslo = lambda fc: b16(M["HB"] + 4096 + fc * 512, 512)
        wstage = lambda k: A[:, M["WB"] + k * 1024: M["WB"] + (k + 1) * 1024]
        whi1 = lambda p: b16(M["WB"] + 1024 + p * 512, 512)
        wlo1 = lambda p: b16(M["WB"] + 3072 + p * 512, 512)
        stemp = lambda fc: A[:, (M["TMP"] if fc % 2 == 0 else M["GS"]): (M["TMP"] if fc % 2 == 0 else M["GS"]) + 1024]
        ptemp = A[:, M["GS2"]: M["GS2"] + 1024]
        shs = b.sem("f_hs")

        snm2 = b.sem("ln_nm")
        n1r = snm2.n
        @block.vector
        def _(v):
            v.wait_ge(snm2.h, n1r)
            for jc in range(NJ):
                v.tensor_copy(out=n1hi(jc), in_=n1(jc))
            for jc in range(NJ):
                ins = v.tensor_tensor(out=n1lo(jc), in0=n1(jc), in1=n1hi(jc), op=AL.subtract)
            shs.inc(ins)
        n1split = shs.n

        scm = b.sem("f_cm")
        for e in range(NEXP):
            @block.sync
            def _(s):
                s.wait_ge(scb.h, cbm_ready)
                s.wait_ge(stp.h, stp.n)
                scm.inc(s.dma_start(out=cbm2, in_=cbm[e:e + 1, :]), 16)
            cb_ld = scm.n

            @block.tensor
            def _(tn):
                tn.wait_ge(scm.h, cb_ld)
                tn.wait_ge(scb.h, scb.n)
                tn.wait_ge(sev.h, sev.n)
                tn.wait_ge(sgu.h, sgu.n)
                for nbi in range(NB):
                    stp.inc(tn.matmul(PS[2 + nbi][:, 0:512], ones1[0:1, :],
                                      cbm2[0:1, nbi * 512:(nbi + 1) * 512],
                                      start=True, stop=True))
            cb_mm = stp.n

            @block.scalar
            def _(sc):
                sc.wait_ge(stp.h, cb_mm)
                sc.wait_ge(smm.h, smm.n)
                for nbi in range(NB):
                    ins = sc.copy(out=cbt[:, nbi * 512:(nbi + 1) * 512], in_=PS[2 + nbi][:, 0:512])
                scb.inc(ins)
            cbt_ready = scb.n

            sev_at = {}
            sgu_at = {}
            sev_base = sev.n
            sgu_base = sgu.n
            for k in range(4):
                sev_at[2 * k + 1] = sev_base + k + 1
            hbase_at = {}
            for fc in range(NJ):
                wpar = (e * NJ + fc) % 2
                pbank = (fc % 2) * 2
                @block.sync
                def _(s):
                    s.wait_ge(shs.h, shs.n)
                    ld_w.inc(s.dma_start(
                        out=wstage(0).rearrange("p (a c) -> p a c", a=8),
                        in_=w["w1"][e].rearrange("(jc p) c -> p jc c", p=128)[:, :, fc * 128:(fc + 1) * 128]), 16)
                w1_ld = ld_w.n

                @block.gpsimd
                def _(g):
                    if e * NJ + fc >= 2:
                        g.wait_ge(smm.h, smm.n - NB)
                    else:
                        for nm in ("a_to", "a_vv", "a_oe"):
                            ss = b.sem(nm)
                            g.wait_ge(ss.h, ss.n)
                        g.wait_ge(smm.h, smm.n)
                    ldh.inc(g.dma_start(
                        out=whi1(wpar).rearrange("p (a c) -> p a c", a=8),
                        in_=w["w1"][e].rearrange("(jc p) c -> p jc c", p=128)[:, :, fc * 128:(fc + 1) * 128]), 16)
                w1_hi = ldh.n

                @block.vector
                def _(v):
                    v.wait_ge(ld_w.h, w1_ld)
                    v.wait_ge(ldh.h, w1_hi)
                    if e * NJ + fc >= 2:
                        v.wait_ge(smm.h, smm.n - NB)
                    ins = v.tensor_tensor(out=wlo1(wpar), in0=wstage(0), in1=whi1(wpar), op=AL.subtract)
                    shs.inc(ins)
                w1_split = shs.n

                @block.tensor
                def _(tn):
                    tn.wait_ge(shs.h, w1_split)
                    if fc >= 2:
                        tn.wait_ge(sev.h, sev_at[fc - 1] if fc % 2 == 0 else sev_at[fc - 2])
                    else:
                        tn.wait_ge(sev.h, sev.n)
                        tn.wait_ge(scb.h, cbt_ready)
                        tn.wait_ge(sra.h, sra.n)
                    for nbi in range(NB):
                        for jc in range(NJ):
                            ins = tn.matmul(PS[pbank + nbi][:, 0:512], whi1(wpar)[:, jc * 128:(jc + 1) * 128],
                                            n1hi(jc)[:, nbi * 512:(nbi + 1) * 512],
                                            start=(jc == 0), stop=False)
                            ins = tn.matmul(PS[pbank + nbi][:, 0:512], whi1(wpar)[:, jc * 128:(jc + 1) * 128],
                                            n1lo(jc)[:, nbi * 512:(nbi + 1) * 512],
                                            start=False, stop=False)
                            ins = tn.matmul(PS[pbank + nbi][:, 0:512], wlo1(wpar)[:, jc * 128:(jc + 1) * 128],
                                            n1hi(jc)[:, nbi * 512:(nbi + 1) * 512],
                                            start=False, stop=(jc == NJ - 1))
                        smm.inc(ins)
                hbase = smm.n - NB

                if fc % 2 == 1:
                    hbase_at[fc] = hbase

                    @block.vector
                    def _(v, fc=fc):
                        v.wait_ge(sev.h, sev_at[fc])
                        v.wait_ge(scb.h, cbt_ready)
                        for f2 in (fc - 1, fc):
                            v.tensor_tensor(out=ptemp, in0=stemp(f2), in1=cbt, op=AL.mult)
                            sgu.inc(v.tensor_copy(out=hshi(f2), in_=ptemp))
                            ins = v.tensor_tensor(out=hslo(f2), in0=ptemp, in1=hshi(f2), op=AL.subtract)
                            shs.inc(ins)
                    sgu_at[fc] = sgu.n

            @block.scalar
            def _(sc):
                for k in range(4):
                    fc = 2 * k + 1
                    sc.wait_ge(sgu.h, sgu_base + 2 * (k - 1) + 2 if k >= 1 else sgu_base)
                    sc.wait_ge(smm.h, hbase_at[fc] + NB)
                    for f2 in (fc - 1, fc):
                        pb2 = (f2 % 2) * 2
                        for nbi in range(NB):
                            ins = sc.activation(out=stemp(f2)[:, nbi * 512:(nbi + 1) * 512],
                                                in_=PS[pb2 + nbi][:, 0:512], func=AF.Silu)
                    sev.inc(ins)
            assert sev.n == sev_base + 4
            h_scaled = shs.n

            sra_at = {}
            for ic in range(NJ):
                wpar = (e * NJ + ic) % 2
                pbank = 4
                @block.sync
                def _(s):
                    s.wait_ge(shs.h, shs.n)
                    ld_w.inc(s.dma_start(
                        out=wstage(0).rearrange("p (a c) -> p a c", a=8),
                        in_=w["w2"][e].rearrange("(fc p) c -> p fc c", p=128)[:, :, ic * 128:(ic + 1) * 128]), 16)
                w2_ld = ld_w.n

                @block.gpsimd
                def _(g):
                    if e * NJ + ic >= 2:
                        g.wait_ge(smm.h, smm.n - NB)
                    ldh.inc(g.dma_start(
                        out=whi1(2 + wpar).rearrange("p (a c) -> p a c", a=8),
                        in_=w["w2"][e].rearrange("(fc p) c -> p fc c", p=128)[:, :, ic * 128:(ic + 1) * 128]), 16)
                w2_hi = ldh.n

                @block.vector
                def _(v):
                    v.wait_ge(ld_w.h, w2_ld)
                    v.wait_ge(ldh.h, w2_hi)
                    if e * NJ + ic >= 2:
                        v.wait_ge(smm.h, smm.n - NB)
                    ins = v.tensor_tensor(out=wlo1(2 + wpar), in0=wstage(0), in1=whi1(2 + wpar), op=AL.subtract)
                    shs.inc(ins)
                w2_split = shs.n

                @block.tensor
                def _(tn):
                    tn.wait_ge(shs.h, max(w2_split, h_scaled))
                    tn.wait_ge(sev.h, sev.n)
                    if ic >= 1:
                        tn.wait_ge(sra.h, sra_at[ic - 1])
                    else:
                        tn.wait_ge(sra.h, sra.n)
                    for nbi in range(NB):
                        for fc in range(NJ):
                            ins = tn.matmul(PS[pbank + nbi][:, 0:512], whi1(2 + wpar)[:, fc * 128:(fc + 1) * 128],
                                            hshi(fc)[:, nbi * 512:(nbi + 1) * 512],
                                            start=(fc == 0), stop=False)
                            ins = tn.matmul(PS[pbank + nbi][:, 0:512], whi1(2 + wpar)[:, fc * 128:(fc + 1) * 128],
                                            hslo(fc)[:, nbi * 512:(nbi + 1) * 512],
                                            start=False, stop=False)
                            ins = tn.matmul(PS[pbank + nbi][:, 0:512], wlo1(2 + wpar)[:, fc * 128:(fc + 1) * 128],
                                            hshi(fc)[:, nbi * 512:(nbi + 1) * 512],
                                            start=False, stop=(fc == NJ - 1))
                        smm.inc(ins)
                rbase = smm.n - NB

                @block.vector
                def _(v):
                    for nbi in range(NB):
                        v.wait_ge(smm.h, rbase + nbi + 1)
                        sl = slice(nbi * 512, (nbi + 1) * 512)
                        if e == 0:
                            ins = v.tensor_copy(out=racc(ic)[:, sl], in_=PS[pbank + nbi][:, 0:512])
                        else:
                            ins = v.tensor_tensor(out=racc(ic)[:, sl], in0=racc(ic)[:, sl],
                                                  in1=PS[pbank + nbi][:, 0:512], op=AL.add)
                    sra.inc(ins)
                sra_at[ic] = sra.n
        @block.tensor
        def _(tn):
            for nbi in range(NB):
                for jc in range(NJ):
                    ins = tn.matmul(PS[6 + nbi][0:1, 0:512], t["sgw_sb"][:, jc:jc + 1],
                                    n1(jc)[:, nbi * 512:(nbi + 1) * 512],
                                    start=(jc == 0), stop=(jc == NJ - 1))
                smm.inc(ins)
        sgbase = smm.n - NB

        @block.scalar
        def _(sc):
            for nbi in range(NB):
                sc.wait_ge(smm.h, sgbase + nbi + 1)
                ins = sc.activation(out=sigrow[:, nbi * 512:(nbi + 1) * 512],
                                    in_=PS[6 + nbi][0:1, 0:512], func=AF.Sigmoid)
            sev.inc(ins)
        sig_done = sev.n

        @block.tensor
        def _(tn):
            tn.wait_ge(sev.h, sig_done)
            for nbi in range(NB):
                stp.inc(tn.matmul(PS[2 + nbi][:, 0:512], half1[0:1, :],
                                  sigrow[:, nbi * 512:(nbi + 1) * 512], start=True, stop=True))
        sgb2 = stp.n

        @block.scalar
        def _(sc):
            sc.wait_ge(stp.h, sgb2)
            for nbi in range(NB):
                ins = sc.copy(out=sigb[:, nbi * 512:(nbi + 1) * 512], in_=PS[2 + nbi][:, 0:512])
            sev.inc(ins)
        b.fence()

        guphi = lambda gp: b16(M["HB"] + gp * 512, 512)
        guplo = lambda gp: b16(M["HB"] + 4096 + gp * 512, 512)
        gutmp = A[:, M["TMP"]: M["TMP"] + 1024]
        gu_done_at = {}
        for gp in range(8):
            for part in range(2):
                gc = gp + 8 * part
                @block.sync
                def _(s):
                    s.wait_ge(shs.h, shs.n)
                    ld_w.inc(s.dma_start(
                        out=wstage(0).rearrange("p (a c) -> p a c", a=8),
                        in_=w["sgu"].rearrange("(jc p) c -> p jc c", p=128)[:, :, gc * 128:(gc + 1) * 128]), 16)
                g_ld = ld_w.n

                @block.gpsimd
                def _(g):
                    g.wait_ge(smm.h, smm.n)
                    ldh.inc(g.dma_start(
                        out=whi1(part).rearrange("p (a c) -> p a c", a=8),
                        in_=w["sgu"].rearrange("(jc p) c -> p jc c", p=128)[:, :, gc * 128:(gc + 1) * 128]), 16)
                g_hi = ldh.n

                @block.vector
                def _(v):
                    v.wait_ge(ld_w.h, g_ld)
                    v.wait_ge(ldh.h, g_hi)
                    v.wait_ge(smm.h, smm.n)
                    ins = v.tensor_tensor(out=wlo1(part), in0=wstage(0), in1=whi1(part), op=AL.subtract)
                    shs.inc(ins)
                g_split = shs.n

                @block.tensor
                def _(tn):
                    tn.wait_ge(shs.h, g_split)
                    if gp == 0 and part == 0:
                        tn.wait_ge(sra.h, sra.n)
                        tn.wait_ge(sev.h, sev.n)
                    elif part == 0:
                        tn.wait_ge(sgu.h, gu_done_at[gp - 1])
                    for nbi in range(NB):
                        pb = PS[0 + nbi] if part == 0 else PS[2 + nbi]
                        for jc in range(NJ):
                            tn.matmul(pb[:, 0:512], whi1(part)[:, jc * 128:(jc + 1) * 128],
                                      n1hi(jc)[:, nbi * 512:(nbi + 1) * 512],
                                      start=(jc == 0), stop=False)
                            tn.matmul(pb[:, 0:512], whi1(part)[:, jc * 128:(jc + 1) * 128],
                                      n1lo(jc)[:, nbi * 512:(nbi + 1) * 512],
                                      start=False, stop=False)
                            ins = tn.matmul(pb[:, 0:512], wlo1(part)[:, jc * 128:(jc + 1) * 128],
                                            n1hi(jc)[:, nbi * 512:(nbi + 1) * 512],
                                            start=False, stop=(jc == NJ - 1))
                        smm.inc(ins)
            gu_mm = smm.n

            @block.scalar
            def _(sc):
                sc.wait_ge(smm.h, gu_mm)
                for nbi in range(NB):
                    ins = sc.activation(out=gs[:, nbi * 512:(nbi + 1) * 512],
                                        in_=PS[nbi][:, 0:512], func=AF.Silu)
                sev.inc(ins)
            gsil = sev.n

            @block.vector
            def _(v):
                v.wait_ge(sev.h, gsil)
                for nbi in range(NB):
                    ins = v.tensor_tensor(out=gutmp[:, nbi * 512:(nbi + 1) * 512],
                                          in0=gs[:, nbi * 512:(nbi + 1) * 512],
                                          in1=PS[2 + nbi][:, 0:512], op=AL.mult)
                sgu.inc(ins)
                v.wait_ge(sgu.h, sgu.n)
                v.tensor_copy(out=guphi(gp), in_=gutmp)
                ins = v.tensor_tensor(out=guplo(gp), in0=gutmp, in1=guphi(gp), op=AL.subtract)
                shs.inc(ins)
            gu_done_at[gp] = sgu.n
        gup_done = shs.n
        b.fence()

        dso_at = {}
        sxr = b.sem("f_xr")
        xrslot = A[:, M["GS"]: M["GS"] + 1024]
        tmp2 = lambda i_: A[:, M["TMP"] + (i_ % 2) * 512: M["TMP"] + (i_ % 2) * 512 + 512]
        for ic in range(NJ):
            @block.sync
            def _(s):
                if ic > 0:
                    s.wait_ge(sso.h, dso_at[ic - 1])
                else:
                    s.wait_ge(sgu.h, sgu.n)
                    s.wait_ge(sev.h, sev.n)
                sxr.inc(s.dma_start(out=xrslot, in_=src_d[ic * 128:(ic + 1) * 128, c0:c0 + HT]), 16)

            @block.sync
            def _(s):
                s.wait_ge(shs.h, shs.n)
                ld_w.inc(s.dma_start(
                    out=wstage(0).rearrange("p (a c) -> p a c", a=8),
                    in_=w["sd"].rearrange("(gc p) c -> p gc c", p=128)[:, :, ic * 128:(ic + 1) * 128]), 16)
            d_ld = ld_w.n

            @block.gpsimd
            def _(g):
                g.wait_ge(smm.h, smm.n)
                ldh.inc(g.dma_start(
                    out=whi1(0).rearrange("p (a c) -> p a c", a=8),
                    in_=w["sd"].rearrange("(gc p) c -> p gc c", p=128)[:, :, ic * 128:(ic + 1) * 128]), 16)
            d_hi = ldh.n

            @block.vector
            def _(v):
                v.wait_ge(ld_w.h, d_ld)
                v.wait_ge(ldh.h, d_hi)
                v.wait_ge(smm.h, smm.n)
                ins = v.tensor_tensor(out=wlo1(0), in0=wstage(0), in1=whi1(0), op=AL.subtract)
                shs.inc(ins)
            d_split = shs.n

            @block.tensor
            def _(tn):
                tn.wait_ge(shs.h, max(d_split, gup_done))
                if ic >= 1:
                    tn.wait_ge(sso.h, dso_at[ic - 1])
                else:
                    tn.wait_ge(sso.h, sso.n)
                    tn.wait_ge(sgu.h, sgu.n)
                for nbi in range(NB):
                    for gc in range(NJ):
                        tn.matmul(PS[4 + nbi][:, 0:512], whi1(0)[:, gc * 128:(gc + 1) * 128],
                                  guphi(gc)[:, nbi * 512:(nbi + 1) * 512],
                                  start=(gc == 0), stop=False)
                        tn.matmul(PS[4 + nbi][:, 0:512], whi1(0)[:, gc * 128:(gc + 1) * 128],
                                  guplo(gc)[:, nbi * 512:(nbi + 1) * 512],
                                  start=False, stop=False)
                        ins = tn.matmul(PS[4 + nbi][:, 0:512], wlo1(0)[:, gc * 128:(gc + 1) * 128],
                                        guphi(gc)[:, nbi * 512:(nbi + 1) * 512],
                                        start=False, stop=(gc == NJ - 1))
                    smm.inc(ins)
            dbase = smm.n - NB

            xr_goal = sxr.n
            @block.vector
            def _(v):
                v.wait_ge(sxr.h, xr_goal)
                v.wait_ge(smm.h, dbase + NB)
                for nbi in range(NB):
                    sl = slice(nbi * 512, (nbi + 1) * 512)
                    v.tensor_tensor(out=tmp2(ic), in0=PS[4 + nbi][:, 0:512], in1=sigb[:, sl], op=AL.mult)
                    v.scalar_tensor_tensor(out=n1(ic)[:, sl], in0=racc(ic)[:, sl], scalar=0.5,
                                           in1=xrslot[:, sl], op0=AL.mult, op1=AL.add)
                    ins = v.tensor_tensor(out=n1(ic)[:, sl], in0=n1(ic)[:, sl], in1=tmp2(ic), op=AL.add)
                    sso.inc(ins)
            dso_at[ic] = sso.n
            out_done = sso.n

            @block.sync
            def _(s):
                s.wait_ge(sso.h, out_done)
                sto.inc(s.dma_start(out=dst_d[ic * 128:(ic + 1) * 128, c0:c0 + HT],
                                    in_=n1(ic)), 16)
        b.fence()

    @block.sync
    def _(s):
        s.wait_ge(sto.h, sto.n)
    b.fence()



def emit_attention(b, t, src_d, dst_d, aw):
    block = b.block
    A = t["arena"]
    PS = t["psum"]
    ident = t["ident"]
    ones1 = t["ones1x128"]
    pairmat2 = t["pairmat2"]
    expmat2 = t["expmat2"]
    selmat = t["selmat"]
    qmask = t["qmask"]
    biases = t["biases_sb"]
    bhelp = t["bhelp_sb"]

    ld_w = b.sem("a_w")
    smm = b.sem("a_mm")
    sev = b.sem("a_ev")
    svv = b.sem("a_vv")
    stp = b.sem("a_tp")
    pk = b.sem("a_pk")
    cc = b.sem("a_cc")
    kld = b.sem("a_kld")
    ssc = b.sem("a_sc")
    sxp = b.sem("a_xp")
    sav = b.sem("a_av")
    soe = b.sem("a_oe")
    sqm = b.sem("a_qm")
    sto = b.sem("a_to")

    agin, agout = aw["agin"], aw["agout"]
    q_d = aw["q_d"]

    M2_N1, M2_XH, M2_SQ = 0, 16384, 24576
    for hh in range(2):
        c0 = hh * 1024
        b.fence()
        emit_ln(b, t,
                lambda jc, c0=c0: src_d[jc * 128:(jc + 1) * 128, c0:c0 + 1024],
                lambda jc: A[:, M2_XH + jc * 1024: M2_XH + (jc + 1) * 1024],
                lambda jc: A[:, M2_SQ + jc * 1024: M2_SQ + (jc + 1) * 1024],
                lambda jc, c0=c0: A[:, M2_N1 + jc * 2048 + c0: M2_N1 + jc * 2048 + c0 + 1024],
                2)
    b.fence()
    n2 = lambda jc: A[:, M2_N1 + jc * 2048: M2_N1 + (jc + 1) * 2048]

    QP, KP, VP = 16384, 18432, 20480
    HELK, HELQ, SQH = 22528, 24576, 26624
    QNP, RDUP = 28672, 30720
    PLN = 32768
    VT = 36864
    ROWS = 37904
    WB0 = 38416
    WHEL = t["fmap"]["WB"]
    qp = A[:, QP: QP + 2048]
    kp = A[:, KP: KP + 2048]
    vp = A[:, VP: VP + 2048]
    helk = A[:, HELK: HELK + 2048]
    helq = A[:, HELQ: HELQ + 2048]
    sqh = A[:, SQH: SQH + 2048]
    qnp = A[:, QNP: QNP + 2048]
    rdup = A[:, RDUP: RDUP + 2048]
    pln = A[:, PLN: PLN + 4096]
    vt = A[:, VT: VT + 1040]
    rows = A[0:1, ROWS: ROWS + 512]
    wbA = lambda p: A[:, WB0 + p * 1024: WB0 + (p + 1) * 1024]
    whel_sb = A[:, WHEL: WHEL + 1024]
    assert WB0 + 2048 <= t["fmap"]["MUBS"], (WB0, t["fmap"]["MUBS"])
    assert WHEL >= t["fmap"]["WB"]

    ldc = b.sem("a_ldc")

    @block.sync
    def _(s):
        ldc.inc(s.dma_start(out=whel_sb.rearrange("p (a c) -> p a c", a=8),
                            in_=aw["whelp"].rearrange("(hp p) c -> p hp c", p=128)), 16)
    whel_ready = ldc.n

    def proj_pair(hp, wt_d, dst, bias_col, wslot):
        @block.sync
        def _(s):
            s.wait_ge(smm.h, smm.n)
            ld_w.inc(s.dma_start(
                out=wbA(wslot).rearrange("p (a c) -> p a c", a=8),
                in_=wt_d.rearrange("(jc p) c -> p jc c", p=128)[:, :, hp * 128:(hp + 1) * 128]), 16)

        evb = sev.n
        @block.tensor
        def _(tn):
            tn.wait_ge(ld_w.h, ld_w.n)
            tn.wait_ge(sev.h, evb)
            for nbi in range(4):
                if nbi >= 2:
                    tn.wait_ge(sev.h, evb + nbi - 1)
                for jc in range(NJ):
                    ins = tn.matmul(PS[nbi % 2][:, 0:512], wbA(wslot)[:, jc * 128:(jc + 1) * 128],
                                    n2(jc)[:, nbi * 512:(nbi + 1) * 512],
                                    start=(jc == 0), stop=(jc == NJ - 1))
                smm.inc(ins)
        pbase = smm.n - 4

        @block.scalar
        def _(sc):
            for nbi in range(4):
                sc.wait_ge(smm.h, pbase + nbi + 1)
                ins = sc.activation(out=dst[:, nbi * 512:(nbi + 1) * 512], in_=PS[nbi % 2][:, 0:512],
                                    func=AF.Identity, bias=biases[:, bias_col:bias_col + 1])
                sev.inc(ins)

    def hel_pair(hp, src, dsttile):
        evb = sev.n
        @block.tensor
        def _(tn):
            tn.wait_ge(ldc.h, whel_ready)
            tn.wait_ge(sev.h, evb)
            for nbi in range(4):
                if nbi >= 2:
                    tn.wait_ge(sev.h, evb + nbi - 1)
                stp.inc(tn.matmul(PS[2 + nbi % 2][:, 0:512], whel_sb[:, hp * 128:(hp + 1) * 128],
                                  src[:, nbi * 512:(nbi + 1) * 512], start=True, stop=True))
        hbase = stp.n - 4

        @block.scalar
        def _(sc):
            for nbi in range(4):
                sc.wait_ge(stp.h, hbase + nbi + 1)
                ins = sc.activation(out=dsttile[:, nbi * 512:(nbi + 1) * 512],
                                    in_=PS[2 + nbi % 2][:, 0:512], func=AF.Identity,
                                    bias=bhelp[:, hp:hp + 1])
                sev.inc(ins)

    def l2norm_pair(srctile):
        @block.scalar
        def _(sc):
            sc.wait_ge(sev.h, sev.n)
            sc.wait_ge(svv.h, svv.n)
            ins = sc.activation(out=sqh, in_=srctile, func=AF.Square)
            svv.inc(ins)
        sq_done = svv.n

        svb = svv.n
        @block.tensor
        def _(tn):
            tn.wait_ge(svv.h, sq_done)
            for nbi in range(4):
                if nbi >= 2:
                    tn.wait_ge(svv.h, svb + nbi - 1)
                stp.inc(tn.matmul(PS[4 + nbi % 2][0:64, 0:512], pairmat2[:, :],
                                  sqh[:, nbi * 512:(nbi + 1) * 512], start=True, stop=True))
        psbase = stp.n - 4

        @block.vector
        def _(v):
            for nbi in range(4):
                v.wait_ge(stp.h, psbase + nbi + 1)
                ins = v.tensor_scalar(out=qnp[0:64, nbi * 512:(nbi + 1) * 512],
                                      in0=PS[4 + nbi % 2][0:64, 0:512],
                                      scalar1=1e-24, scalar2=None, op0=AL.max)
                svv.inc(ins)
            v.wait_ge(svv.h, svv.n)
            ins = v.reciprocal(out=qnp[0:64, :], in_=qnp[0:64, :])
            svv.inc(ins)
        rc_done = svv.n

        @block.scalar
        def _(sc):
            sc.wait_ge(svv.h, rc_done)
            ins = sc.activation(out=qnp[0:64, :], in_=qnp[0:64, :], func=AF.Sqrt)
            svv.inc(ins)
        r_done = svv.n

        svb2 = svv.n
        @block.tensor
        def _(tn):
            tn.wait_ge(svv.h, r_done)
            for nbi in range(4):
                if nbi >= 2:
                    tn.wait_ge(svv.h, svb2 + nbi - 1)
                stp.inc(tn.matmul(PS[6 + nbi % 2][:, 0:512], expmat2[:, :],
                                  qnp[0:64, nbi * 512:(nbi + 1) * 512], start=True, stop=True))
        exbase = stp.n - 4

        @block.vector
        def _(v):
            for nbi in range(4):
                v.wait_ge(stp.h, exbase + nbi + 1)
                ins = v.tensor_copy(out=rdup[:, nbi * 512:(nbi + 1) * 512],
                                    in_=PS[6 + nbi % 2][:, 0:512])
                svv.inc(ins)
            v.wait_ge(svv.h, svv.n)
            ins = v.tensor_tensor(out=srctile, in0=srctile, in1=rdup, op=AL.mult)
            svv.inc(ins)
        return svv.n

    def deinterleave(srctile, done_val):
        @block.vector
        def _(v):
            v.wait_ge(svv.h, done_val)
            v.wait_ge(pk.h, pk.n)
            for band in range(4):
                r0 = band * 32
                for par in range(2):
                    ins = v.tensor_copy(out=pln[r0:r0 + 32, par * 1024:(par + 1) * 1024],
                                        in_=srctile[r0:r0 + 32, par::2])
                    svv.inc(ins)
        return svv.n


    for hp in range(8):
        b.fence()
        proj_pair(hp, aw["wqt"], qp, 0 + hp, 0)
        proj_pair(hp, aw["wkt"], kp, 8 + hp, 1)
        proj_pair(hp, aw["wvt"], vp, 16 + hp, 2)
        qkv_done = sev.n

        @block.sync
        def _(s):
            s.wait_ge(sev.h, qkv_done)
            pk.inc(s.dma_start(out=q_d[hp * 128:(hp + 1) * 128, :], in_=qp), 16)
            if aw.get("adbg") is not None and hp == 0:
                pk.inc(s.dma_start(out=aw["adbg"][0:128, 6656:8704], in_=n2(0)), 16)
                pk.inc(s.dma_start(out=aw["adbg"][0:128, 8704:10752], in_=qp), 16)
                pk.inc(s.dma_start(out=aw["adbg"][0:128, 10752:12288], in_=kp[:, 0:1536]), 16)

        for hh in range(2):
            head = hp * 2 + hh
            g, hl = head // 4, head % 4
            sh = hl * HE
            @block.gpsimd
            def _(gp):
                gp.wait_ge(sev.h, qkv_done)
                pk.inc(gp.dma_start(
                    out=agin[g][sh + OFF_KHT: sh + OFF_KHT + 131072].rearrange("(a c) -> a c", a=64),
                    in_=kp[hh * 64:(hh + 1) * 64, :]), 16)

        for side, tile in ((0, helk), (1, helq)):
            hel_pair(hp, kp if side == 0 else qp, tile)
            dv = l2norm_pair(tile)
            if aw.get("adbg") is not None and hp == 0 and side == 1:
                @block.sync
                def _(s):
                    s.wait_ge(svv.h, dv)
                    sto.inc(s.dma_start(out=aw["adbg"][0:128, 0:2048], in_=helq), 16)
            dv2 = deinterleave(tile, dv)
            if aw.get("adbg") is not None and hp == 0 and side == 1:
                @block.sync
                def _(s):
                    s.wait_ge(svv.h, dv2)
                    sto.inc(s.dma_start(out=aw["adbg"][0:128, 2048:6144], in_=pln), 16)
            for hh in range(2):
                head = hp * 2 + hh
                g, hl = head // 4, head % 4
                sh = hl * HE
                @block.gpsimd
                def _(gp):
                    gp.wait_ge(svv.h, dv2)
                    off = OFF_KN if side == 0 else OFF_QN
                    for q in range(4):
                        comp, par = q // 2, q % 2
                        src = pln[hh * 64 + comp * 32: hh * 64 + comp * 32 + 32,
                                  par * 1024:(par + 1) * 1024]
                        pk.inc(gp.dma_start(
                            out=agin[g][sh + off + q * 32768: sh + off + (q + 1) * 32768].rearrange("(a c) -> a c", a=32),
                            in_=src), 16)

        for hh in range(2):
            head = hp * 2 + hh
            g, hl = head // 4, head % 4
            sh = hl * HE

            @block.vector
            def _(v):
                v.wait_ge(pk.h, pk.n)
                ins = v.memset(vt[:, 64::65], 1.0)
                svv.inc(ins)
            vt_ones = svv.n

            for tb in range(16):
                @block.tensor
                def _(tn):
                    tn.wait_ge(sev.h, qkv_done)
                    tn.wait_ge(stp.h, stp.n)
                    stp.inc(tn.transpose(PS[2][:, 0:128], vp[:, tb * 128:(tb + 1) * 128], ident[:, :]))
                tdone = stp.n

                @block.vector
                def _(v):
                    v.wait_ge(stp.h, tdone)
                    v.wait_ge(svv.h, vt_ones)
                    ins = v.tensor_copy(out=vt[:, tb * 65: tb * 65 + 64],
                                        in_=PS[2][:, hh * 64: hh * 64 + 64])
                    stp.inc(ins)
            vt_done = stp.n

            @block.scalar
            def _(sc):
                sc.wait_ge(sev.h, qkv_done)
                sc.wait_ge(svv.h, svv.n)
                ins = sc.activation(out=sqh, in_=kp, func=AF.Square)
                svv.inc(ins)
            ksq = svv.n

            @block.tensor
            def _(tn):
                tn.wait_ge(svv.h, ksq)
                tn.wait_ge(stp.h, stp.n)
                for nbi in range(4):
                    stp.inc(tn.matmul(PS[4 + nbi][0:1, 0:512], selmat[:, hh:hh + 1],
                                      sqh[:, nbi * 512:(nbi + 1) * 512], start=True, stop=True))
            kbase = stp.n - 4

            @block.vector
            def _(v):
                for nbi in range(4):
                    v.wait_ge(stp.h, kbase + nbi + 1)
                    ins = v.reduce_max(rows[:, 16 + nbi: 17 + nbi], PS[4 + nbi][0:1, 0:512], axis=AX)
                    svv.inc(ins)
                v.wait_ge(svv.h, svv.n)
                ins = v.reduce_max(rows[:, 20:21], rows[:, 16:20], axis=AX)
                svv.inc(ins)
            kmx = svv.n

            @block.scalar
            def _(sc):
                sc.wait_ge(svv.h, kmx)
                ins = sc.activation(out=rows[:, 21:22], in_=rows[:, 20:21], func=AF.Sqrt)
                svv.inc(ins)
            kmx2 = svv.n

            @block.gpsimd
            def _(gp):
                gp.wait_ge(svv.h, kmx2)
                gp.wait_ge(stp.h, vt_done)
                pk.inc(gp.dma_start(
                    out=agin[g][sh + OFF_VB: sh + OFF_VB + 1040 * 128].rearrange("(a c) -> a c", a=128),
                    in_=vt), 16)
                pk.inc(gp.dma_start(
                    out=agin[g][sh + OFF_ST: sh + OFF_ST + 1].rearrange("(a c) -> a c", a=1),
                    in_=rows[:, 21:22]), 16)

        if hp % 2 == 1:
            g = hp // 2
            @block.gpsimd
            def _(gp):
                gp.wait_ge(pk.h, pk.n)
                ins = gp.collective_compute(
                    "AllGather", AL.bypass,
                    replica_groups=[[0, 1, 2, 3], [4, 5, 6, 7]],
                    ins=[agin[g].opt()],
                    outs=[agout[g].opt()],
                )
                cc.inc(ins, 1)

    pack_goals = [(smm.h, smm.n), (stp.h, stp.n), (svv.h, svv.n), (sev.h, sev.n), (pk.h, pk.n)]
    b.fence()

    KT, VTA, QT, PT, OACC = 0, 8192, 12352, 14400, 18496
    APAIR, QW, AHEAD = 34880, 36928, 45120
    ROWS2, RZ, WOB, NEGM = 47168, 47696, 45120 + 0, 47680
    AHEAD = 45120
    ROWS2 = 47168
    NEGM = 47680
    RZ = 47696
    WOB = 49744
    kt = lambda p: A[:, KT + p * 4096: KT + (p + 1) * 4096]
    vta = lambda p: A[:, VTA + p * 2080: VTA + (p + 1) * 2080]
    qt = A[:, QT: QT + 2048]
    pt = lambda p: A[:, PT + p * 2048: PT + (p + 1) * 2048]
    oacc = lambda ic: A[:, OACC + ic * 2048: OACC + (ic + 1) * 2048]
    apair = A[:, APAIR: APAIR + 2048]
    qw = A[:, QW: QW + 8192]
    ahead = A[:, AHEAD: AHEAD + 2048]
    rows2 = A[0:1, ROWS2: ROWS2 + 512]
    negm = A[:, NEGM: NEGM + 1]
    rz = A[:, RZ: RZ + 2048]
    wob = A[:, WOB: WOB + 1024]

    def ag_off(g, r, hl, field):
        return r * SHARD + hl * HE + field

    def assemble_kv(eng, head, th, pbuf):
        g, hl = head // 4, head % 4
        ag = agout[g]
        for i in range(2):
            r = 2 * th + i
            kld.inc(eng.dma_start(
                out=kt(pbuf)[0:64, i * 2048:(i + 1) * 2048],
                in_=ag[ag_off(g, r, hl, OFF_KHT): ag_off(g, r, hl, OFF_KHT) + 131072].rearrange("(a c) -> a c", a=64)), 16)
        for par in range(2):
            for r4 in range(4):
                o = OFF_KN + (2 * th + par) * 32768
                src = ag[ag_off(g, r4, hl, o): ag_off(g, r4, hl, o) + 32768].rearrange("(a c) -> a c", a=32)
                kld.inc(eng.dma_start(
                    out=kt(pbuf)[64 + 32 * par: 96 + 32 * par, r4 * 1024:(r4 + 1) * 1024],
                    in_=src), 16)
        for i in range(2):
            r = 2 * th + i
            kld.inc(eng.dma_start(
                out=vta(pbuf)[:, i * 1040:(i + 1) * 1040],
                in_=ag[ag_off(g, r, hl, OFF_VB): ag_off(g, r, hl, OFF_VB) + 1040 * 128].rearrange("(a c) -> a c", a=128)), 16)

    @block.vector
    def _(v):
        v.wait_ge(pk.h, pk.n)
        for ic in range(NJ):
            ins = v.memset(oacc(ic), 0.0)
        soe.inc(ins)

    for head in range(H):
        g, hl = head // 4, head % 4
        hp, hh = head // 2, head % 2
        sc_head0 = ssc.n
        xp_head0 = sxp.n
        av_head0 = sav.n

        @block.sync
        def _(s):
            s.wait_ge(cc.h, g + 1)
            for hh_, nn_ in pack_goals:
                s.wait_ge(hh_, nn_)
            s.wait_ge(ssc.h, ssc.n)
            s.wait_ge(sav.h, sav.n)
            assemble_kv(s, head, 0, 0)
        kld_h0 = kld.n

        @block.sync
        def _(s):
            s.wait_ge(cc.h, g + 1)
            for hh_, nn_ in pack_goals:
                s.wait_ge(hh_, nn_)
            s.wait_ge(sqm.h, sqm.n)
            s.wait_ge(ssc.h, ssc.n)
            kld.inc(s.dma_start(out=qt[0:64, :], in_=q_d[head * 64:(head + 1) * 64, :]), 16)
            for wsel in range(4):
                p_pl, s2 = wsel // 2, wsel % 2
                for par in range(2):
                    for i in range(2):
                        r = 2 * s2 + i
                        o = OFF_QN + (2 * p_pl + par) * 32768
                        src = agout[g][ag_off(g, r, hl, o): ag_off(g, r, hl, o) + 32768].rearrange("(a c) -> a c", a=32)
                        kld.inc(s.dma_start(
                            out=qw[64 + 32 * par: 96 + 32 * par,
                                   wsel * 2048 + i * 1024: wsel * 2048 + (i + 1) * 1024],
                            in_=src), 16)
            stat_src = agout[g].rearrange("(r x) -> r x", r=4)[:, hl * HE + OFF_ST: hl * HE + OFF_ST + 1]
            kld.inc(s.dma_start(out=rows2[0:1, 0:4], in_=stat_src, allow_slow_non_contiguous=True), 16)
        kld_q = kld.n

        @block.vector
        def _(v):
            v.wait_ge(kld.h, kld_q)
            for wsel in range(4):
                if wsel == 0:
                    ins = v.tensor_scalar(out=qt[64:128, :], in0=qw[64:128, 0:2048],
                                          scalar1=qmask[64:128, 0:1], scalar2=None, op0=AL.mult)
                else:
                    ins = v.scalar_tensor_tensor(out=qt[64:128, :],
                                                 in0=qw[64:128, wsel * 2048:(wsel + 1) * 2048],
                                                 scalar=qmask[64:128, wsel:wsel + 1],
                                                 in1=qt[64:128, :], op0=AL.mult, op1=AL.add)
                sqm.inc(ins)
                v.wait_ge(sqm.h, sqm.n)
        qsel_done = sqm.n

        @block.scalar
        def _(sc):
            sc.wait_ge(sqm.h, qsel_done)
            ins = sc.mul(qt[0:64, :], qt[0:64, :], 0.0625)
            sqm.inc(ins)
            ins = sc.mul(qt[64:128, :], qt[64:128, :], 1.0 / 64.0)
            sqm.inc(ins)
            sc.wait_ge(sqm.h, sqm.n)
            ins = sc.activation(out=qw[0:64, 0:2048], in_=qt[0:64, :], func=AF.Square)
            sqm.inc(ins)
        qsq_done = sqm.n

        @block.tensor
        def _(tn):
            tn.wait_ge(sqm.h, qsq_done)
            tn.wait_ge(soe.h, soe.n)
            tn.wait_ge(stp.h, stp.n)
            for nbi in range(4):
                stp.inc(tn.matmul(PS[4 + nbi][0:1, 0:512], t["ones128"][0:64, 0:1],
                                  qw[0:64, nbi * 512:(nbi + 1) * 512], start=True, stop=True))
        qs_base = stp.n - 4

        @block.vector
        def _(v):
            for nbi in range(4):
                v.wait_ge(stp.h, qs_base + nbi + 1)
                ins = v.reduce_max(rows2[:, 8 + nbi: 9 + nbi], PS[4 + nbi][0:1, 0:512], axis=AX)
                svv.inc(ins)
            v.wait_ge(svv.h, svv.n)
            ins = v.reduce_max(rows2[:, 12:13], rows2[:, 8:12], axis=AX)
            svv.inc(ins)
            v.wait_ge(svv.h, svv.n)
            ins = v.reduce_max(rows2[:, 13:14], rows2[:, 0:4], axis=AX)
            svv.inc(ins)
        qk_red = svv.n

        @block.scalar
        def _(sc):
            sc.wait_ge(svv.h, qk_red)
            ins = sc.activation(out=rows2[:, 14:15], in_=rows2[:, 12:13], func=AF.Sqrt)
            svv.inc(ins)
        qmx = svv.n

        @block.vector
        def _(v):
            v.wait_ge(svv.h, qmx)
            ins = v.tensor_tensor(out=rows2[:, 15:16], in0=rows2[:, 14:15],
                                  in1=rows2[:, 13:14], op=AL.mult)
            svv.inc(ins)
            v.wait_ge(svv.h, svv.n)
            ins = v.tensor_scalar(out=rows2[:, 16:17], in0=rows2[:, 15:16],
                                  scalar1=-1.0, scalar2=-1.5, op0=AL.mult, op1=AL.add)
            svv.inc(ins)
        negm_row = svv.n

        @block.tensor
        def _(tn):
            tn.wait_ge(svv.h, negm_row)
            tn.wait_ge(stp.h, stp.n)
            stp.inc(tn.matmul(PS[4][:, 0:1], ones1[0:1, :], rows2[:, 16:17], start=True, stop=True))
        negm_mm = stp.n

        @block.scalar
        def _(sc):
            sc.wait_ge(stp.h, negm_mm)
            ins = sc.copy(out=negm, in_=PS[4][:, 0:1])
            sqm.inc(ins)
        negm_done = sqm.n

        if aw.get("adbg") is not None and head == 0:
            @block.sync
            def _(s):
                s.wait_ge(sqm.h, negm_done)
                s.wait_ge(kld.h, kld_h0)
                sto.inc(s.dma_start(out=aw["adbg"][0:128, 0:2048], in_=qt), 16)
                sto.inc(s.dma_start(out=aw["adbg"][0:128, 2048:4096], in_=kt(0)[:, 0:2048]), 16)
                sto.inc(s.dma_start(out=aw["adbg"][0:1, 6144:6145], in_=negm[0:1, 0:1]), 16)

        @block.sync
        def _(s):
            s.wait_ge(sav.h, av_head0 + 4 * 16)
            assemble_kv(s, head, 1, 1)
        kld_h1 = kld.n

        sc_head0 = ssc.n

        @block.tensor
        def _(tn):
            for tb in range(64):
                th = tb // 32
                tbl = tb % 32
                if tb == 0:
                    tn.wait_ge(kld.h, kld_q)
                if tb == 32:
                    tn.wait_ge(kld.h, kld_h1)
                if tb == 0:
                    tn.wait_ge(sqm.h, negm_done)
                if tb >= 1:
                    tn.wait_ge(sxp.h, xp_head0 + 4 * tb)
                if tb >= 2:
                    tn.wait_ge(sav.h, av_head0 + 4 * (tb - 1))
                kslice = kt(th)[:, tbl * 128:(tbl + 1) * 128]
                for sb_i in range(4):
                    ssc.inc(tn.matmul(PS[sb_i][:, 0:512], kslice,
                                      qt[:, sb_i * 512:(sb_i + 1) * 512], start=True, stop=True))
                if tb >= 1:
                    pth = (tb - 1) // 32
                    vtile = vta(pth)[:, ((tb - 1) % 32) * 65: ((tb - 1) % 32) * 65 + 65]
                    for sb_i in range(4):
                        sav.inc(tn.matmul(PS[4 + sb_i][0:65, 0:512], vtile,
                                          pt((tb - 1) % 2)[:, sb_i * 512:(sb_i + 1) * 512],
                                          start=(tb - 1 == 0), stop=(tb - 1 == 63)))

        @block.scalar
        def _(sc):
            sc.wait_ge(sqm.h, negm_done)
            for tb in range(64):
                if tb >= 2:
                    sc.wait_ge(sav.h, av_head0 + 4 * (tb - 1))
                sc.wait_ge(ssc.h, sc_head0 + 4 * tb + 4)
                sc.activation(out=pt(tb % 2)[:, 0:1024], in_=t["psq"][:, 0:1024],
                              func=AF.Exp, bias=negm[:, 0:1])
                sxp.inc(sc.activation(out=pt(tb % 2)[:, 1024:2048], in_=t["psq"][:, 1024:2048],
                                      func=AF.Exp, bias=negm[:, 0:1]), 4)

        @block.tensor
        def _(tn):
            tn.wait_ge(sxp.h, sxp.n)
            vtile = vta(1)[:, 31 * 65: 31 * 65 + 65]
            for sb_i in range(4):
                sav.inc(tn.matmul(PS[4 + sb_i][0:65, 0:512], vtile,
                                  pt(63 % 2)[:, sb_i * 512:(sb_i + 1) * 512],
                                  start=False, stop=True))
        av_done = sav.n

        @block.vector
        def _(v):
            v.wait_ge(sav.h, av_done)
            for sb_i in range(4):
                ins = v.tensor_copy(out=rz[64:65, sb_i * 512:(sb_i + 1) * 512],
                                    in_=PS[4 + sb_i][64:65, 0:512])
                svv.inc(ins)
        z_ev = svv.n

        @block.sync
        def _(s):
            s.wait_ge(svv.h, z_ev)
            kld.inc(s.dma_start(out=rz[0:1, :], in_=rz[64:65, :]), 16)
        z_mv = kld.n

        @block.vector
        def _(v):
            v.wait_ge(kld.h, z_mv)
            ins = v.reciprocal(out=rz[0:1, :], in_=rz[0:1, :])
            svv.inc(ins)
        z_rc = svv.n

        @block.tensor
        def _(tn):
            tn.wait_ge(svv.h, z_rc)
            tn.wait_ge(stp.h, stp.n)
            tn.wait_ge(sxp.h, sxp.n)
            for nbi in range(4):
                stp.inc(tn.matmul(PS[nbi][0:64, 0:512], ones1[0:1, 0:64],
                                  rz[0:1, nbi * 512:(nbi + 1) * 512], start=True, stop=True))
        rzb_mm = stp.n - 4

        @block.scalar
        def _(sc):
            for nbi in range(4):
                sc.wait_ge(stp.h, rzb_mm + nbi + 1)
                ins = sc.copy(out=rz[0:64, nbi * 512:(nbi + 1) * 512], in_=PS[nbi][0:64, 0:512])
            svv.inc(ins)
        rzb_done = svv.n

        @block.vector
        def _(v):
            v.wait_ge(svv.h, rzb_done)
            for sb_i in range(4):
                ins = v.tensor_tensor(out=ahead[0:64, sb_i * 512:(sb_i + 1) * 512],
                                      in0=PS[4 + sb_i][0:64, 0:512],
                                      in1=rz[0:64, sb_i * 512:(sb_i + 1) * 512], op=AL.mult)
                soe.inc(ins)
        ah_done = soe.n

        @block.sync
        def _(s):
            s.wait_ge(soe.h, ah_done)
            kld.inc(s.dma_start(out=apair[hh * 64:(hh + 1) * 64, :], in_=ahead[0:64, :]), 16)
        ap_mv = kld.n

        if aw.get("adbg") is not None and head == 1:
            @block.sync
            def _(s):
                s.wait_ge(kld.h, ap_mv)
                sto.inc(s.dma_start(out=aw["adbg"][0:128, 4096:6144], in_=apair), 16)

        if hh == 1:
            @block.sync
            def _(s):
                s.wait_ge(smm.h, smm.n)
                ld_w.inc(s.dma_start(
                    out=wob.rearrange("p (a c) -> p a c", a=8),
                    in_=aw["wot"][hp * 128:(hp + 1) * 128, :].rearrange("p (a c) -> p a c", a=8)), 16)
            wo_ld = ld_w.n

            oeb = soe.n
            @block.tensor
            def _(tn):
                tn.wait_ge(ld_w.h, wo_ld)
                tn.wait_ge(kld.h, ap_mv)
                tn.wait_ge(stp.h, stp.n)
                tn.wait_ge(soe.h, oeb)
                for ic in range(NJ):
                    if ic >= 1:
                        tn.wait_ge(soe.h, oeb + 4 * ic)
                    for nbi in range(4):
                        smm.inc(tn.matmul(PS[nbi][:, 0:512], wob[:, ic * 128:(ic + 1) * 128],
                                          apair[:, nbi * 512:(nbi + 1) * 512],
                                          start=True, stop=True))
            ob = smm.n - 32

            @block.vector
            def _(v):
                cnt = 0
                for ic in range(NJ):
                    for nbi in range(4):
                        cnt += 1
                        v.wait_ge(smm.h, ob + cnt)
                        ins = v.tensor_tensor(out=oacc(ic)[:, nbi * 512:(nbi + 1) * 512],
                                              in0=oacc(ic)[:, nbi * 512:(nbi + 1) * 512],
                                              in1=PS[nbi][:, 0:512], op=AL.add)
                        soe.inc(ins)

    b.fence()

    smm2 = b.sem("a_fm")
    for ic in range(NJ):
        @block.sync
        def _(s):
            s.wait_ge(smm2.h, smm2.n)
            s.wait_ge(sto.h, sto.n)
            kld.inc(s.dma_start(out=qw[:, 0:2048], in_=src_d[ic * 128:(ic + 1) * 128, :]), 16)
        x_ld = kld.n

        @block.scalar
        def _(sc):
            sc.wait_ge(kld.h, x_ld)
            ins = sc.activation(out=ahead[:, 0:2048], in_=oacc(ic), func=AF.Identity,
                                bias=biases[:, 24 + ic: 25 + ic])
            svv.inc(ins)
        o_b = svv.n

        @block.vector
        def _(v):
            v.wait_ge(svv.h, o_b)
            v.wait_ge(kld.h, x_ld)
            ins = v.tensor_tensor(out=qw[:, 0:2048], in0=qw[:, 0:2048], in1=ahead[:, 0:2048], op=AL.add)
            smm2.inc(ins)
        x3_done = smm2.n

        @block.sync
        def _(s):
            s.wait_ge(smm2.h, x3_done)
            sto.inc(s.dma_start(out=dst_d[ic * 128:(ic + 1) * 128, :], in_=qw[:, 0:2048]), 16)

    @block.sync
    def _(s):
        s.wait_ge(sto.h, sto.n)
    b.fence()



def build_program(stage="full"):
    nc = bass.Bass("TRN2", target_bir_lowering=False, debug=False)
    dt = lambda n, shp, k: nc.dram_tensor(n, shp, F32, kind=k).ap()
    x_t = dt("x_t", [E, T], "ExternalInput")
    y_t = dt("y_t", [E, T], "ExternalOutput")
    dbg_t = dt("dbg_t", [128, 8192], "ExternalOutput") if stage == "ffn1dbg" else None
    adbg_t = dt("adbg_t", [128, 12288], "ExternalOutput") if stage == "x3dbg" else None

    w = {}
    w["w1"] = [dt(f"w1_{e}", [E, E], "ExternalInput") for e in range(NEXP)]
    w["w2"] = [dt(f"w2_{e}", [E, E], "ExternalInput") for e in range(NEXP)]
    w["sgu"] = dt("sgu", [E, 2 * E], "ExternalInput")
    w["sd"] = dt("sd", [E, E], "ExternalInput")
    aw = {}
    aw["wqt"] = dt("wqt", [E, E], "ExternalInput")
    aw["wkt"] = dt("wkt", [E, E], "ExternalInput")
    aw["wvt"] = dt("wvt", [E, E], "ExternalInput")
    aw["wot"] = dt("wot", [E, E], "ExternalInput")
    aw["whelp"] = dt("whelp", [8 * 128, 128], "ExternalInput")
    cin = {}
    for name, shp in [("ident", [128, 128]), ("ones128", [128, 1]), ("ones1x128", [1, 128]),
                      ("half1x128", [1, 128]), ("pairmat2", [128, 64]), ("expmat2", [64, 128]),
                      ("selmat", [128, 2]), ("qmask", [128, 4]), ("rw", [128, 64]),
                      ("sgw", [128, 8]), ("biases", [128, 32]), ("bhelp", [128, 8])]:
        cin[name] = dt("c_" + name, shp, "ExternalInput")

    x1_d = dt("x1_d", [E, T], "Internal")
    q_d = dt("q_d", [E, T], "Internal")
    x3_d = dt("x3_d", [E, T], "Internal")
    aw["q_d"] = q_d
    aw["agin"] = [dt(f"agin{g}", [SHARD], "Internal") for g in range(4)]
    aw["agout"] = [dt(f"agout{g}", [4 * SHARD], "Internal") for g in range(4)]

    es = contextlib.ExitStack()
    with es:
        ARENA_W = 52272
        arena = es.enter_context(nc.sbuf_tensor("arena", [128, ARENA_W], F32))
        sb = {}
        for name, shp in [("ident", [128, 128]), ("ones128", [128, 1]), ("ones1x128", [1, 128]),
                          ("half1x128", [1, 128]), ("pairmat2", [128, 64]), ("expmat2", [64, 128]),
                          ("selmat", [128, 2]), ("qmask", [128, 4]), ("rw_sb", [128, 64]),
                          ("sgw_sb", [128, 8]), ("biases_sb", [128, 32]), ("bhelp_sb", [128, 8])]:
            sb[name] = es.enter_context(nc.sbuf_tensor("s_" + name, shp, F32))
        psq = es.enter_context(nc.psum_tensor("psq", [128, 2048], F32))
        psa = es.enter_context(nc.psum_tensor("psa", [128, 2048], F32))
        psum = [(psq if i < 4 else psa)[:, (i % 4) * 512: (i % 4 + 1) * 512] for i in range(8)]
        block = es.enter_context(nc.Block())
        b = B(nc, block, es)

        t = dict(arena=arena, psum=psum, psq=psq, psa=psa)
        for k in sb:
            t[k] = sb[k]

        fmap = {}
        off = 0
        for name, width in [("XH", 8192), ("N1", 8192), ("HB", 8192), ("RACC", 8192),
                            ("CBT", 1024), ("SIGB", 1024), ("GS", 1024), ("GS2", 1024), ("TMP", 1024),
                            ("LG", 1024), ("LGT", 8), ("CBM", 1024), ("CBM2", 1024), ("SIGROW", 1024),
                            ("SML", 40), ("MUBS", 1024), ("RBS", 1024), ("ROWS", 3072),
                            ("WB", 5120)]:
            fmap[name] = off
            off += width
        assert off <= ARENA_W, off
        t["fmap"] = fmap
        t["rowoff"] = fmap["ROWS"]
        t["mub"] = arena[:, fmap["MUBS"]: fmap["MUBS"] + 1024]
        t["rb"] = arena[:, fmap["RBS"]: fmap["RBS"] + 1024]

        ldc = b.sem("ldc0")

        @block.sync
        def _(s):
            for name, key, rowlim in [("ident", "ident", 128), ("ones128", "ones128", 128),
                                      ("ones1x128", "ones1x128", 1), ("half1x128", "half1x128", 1),
                                      ("pairmat2", "pairmat2", 128), ("expmat2", "expmat2", 64),
                                      ("selmat", "selmat", 128), ("qmask", "qmask", 128),
                                      ("biases", "biases_sb", 128), ("bhelp", "bhelp_sb", 128)]:
                ldc.inc(s.dma_start(out=sb[key][0:rowlim, :], in_=cin[name]), 16)
            ldc.inc(s.dma_start(out=sb["rw_sb"][:, :], in_=cin["rw"]), 16)
            ldc.inc(s.dma_start(out=sb["sgw_sb"][:, :], in_=cin["sgw"]), 16)

        for name in ("tensor", "vector", "scalar", "gpsimd"):
            def f(e):
                e.wait_ge(ldc.h, ldc.n)
            getattr(block, name)(f)

        if stage == "ffn1dbg":
            emit_ffn(b, t, x_t, y_t, w, dbg_d=dbg_t)
        else:
            emit_ffn(b, t, x_t, x1_d if stage != "ffn1" else y_t, w)
        if stage not in ("ffn1", "ffn1dbg"):
            aw["adbg"] = adbg_t if stage == "x3dbg" else None
            emit_attention(b, t, x1_d, x3_d if stage not in ("x3", "x3dbg") else y_t, aw)
            if stage != "x3":
                emit_ffn(b, t, x3_d, y_t, w)

        snap = [(s_.h, s_.n) for s_ in b._sems.values() if s_.n > 0]

        @block.sync
        def _(s):
            for h, n in snap:
                s.wait_ge(h, n)
    return nc



def _consts(quar):
    c = {}
    c["ident"] = np.eye(128, dtype=np.float32)
    c["ones128"] = np.ones((128, 1), dtype=np.float32)
    c["ones1x128"] = np.ones((1, 128), dtype=np.float32)
    c["half1x128"] = np.full((1, 128), 0.5, dtype=np.float32)
    pm = np.zeros((128, 64), dtype=np.float32)
    for u in range(128):
        if u < 64:
            pm[u, u % 32] = 1.0
        else:
            pm[u, 32 + (u - 64) % 32] = 1.0
    c["pairmat2"] = pm
    em = np.zeros((64, 128), dtype=np.float32)
    for u in range(128):
        if u < 64:
            em[u % 32, u] = 1.0
        else:
            em[32 + (u - 64) % 32, u] = 1.0
    c["expmat2"] = em
    sel = np.zeros((128, 2), dtype=np.float32)
    sel[0:64, 0] = 1.0
    sel[64:128, 1] = 1.0
    c["selmat"] = sel
    qm = np.zeros((128, 4), dtype=np.float32)
    qm[:, quar] = 1.0
    c["qmask"] = qm
    return c


def kernel(L0_attn_x, L0_attn_Wq, L0_attn_bq, L0_attn_Wk, L0_attn_bk, L0_attn_Wv,
           L0_attn_bv, L0_attn_Wo, L0_attn_bo, L0_attn_Whel, L0_attn_bhel,
           L0_ffn_router_w, L0_ffn_w1, L0_ffn_w2, L0_ffn_shared_gate_up,
           L0_ffn_shared_down, L0_ffn_shared_gate_w, _stage="full"):
    x = np.ascontiguousarray(np.asarray(L0_attn_x, dtype=np.float32)).reshape(2 * S, E)

    perm = np.array([2 * (ep % 32) + (ep // 32) for ep in range(64)], dtype=np.int64)
    whel = np.asarray(L0_attn_Whel, dtype=np.float32)
    whelT = whel.transpose(0, 2, 1)[:, :, perm]
    whelp = np.zeros((8 * 128, 128), dtype=np.float32)
    for hp in range(8):
        whelp[hp * 128: hp * 128 + 64, 0:64] = whelT[2 * hp]
        whelp[hp * 128 + 64: hp * 128 + 128, 64:128] = whelT[2 * hp + 1]
    bhel = np.asarray(L0_attn_bhel, dtype=np.float32)[:, perm]
    bhelp = np.zeros((128, 8), dtype=np.float32)
    for hp in range(8):
        bhelp[0:64, hp] = bhel[2 * hp]
        bhelp[64:128, hp] = bhel[2 * hp + 1]

    biases = np.zeros((128, 32), dtype=np.float32)
    for hp in range(8):
        biases[:, hp] = np.asarray(L0_attn_bq, dtype=np.float32)[hp * 128:(hp + 1) * 128]
        biases[:, 8 + hp] = np.asarray(L0_attn_bk, dtype=np.float32)[hp * 128:(hp + 1) * 128]
        biases[:, 16 + hp] = np.asarray(L0_attn_bv, dtype=np.float32)[hp * 128:(hp + 1) * 128]
        biases[:, 24 + hp] = np.asarray(L0_attn_bo, dtype=np.float32)[hp * 128:(hp + 1) * 128]

    shared = {
        "sgu": np.ascontiguousarray(np.asarray(L0_ffn_shared_gate_up, dtype=np.float32)),
        "sd": np.ascontiguousarray(np.asarray(L0_ffn_shared_down, dtype=np.float32)),
        "wqt": np.ascontiguousarray(np.asarray(L0_attn_Wq, dtype=np.float32).T),
        "wkt": np.ascontiguousarray(np.asarray(L0_attn_Wk, dtype=np.float32).T),
        "wvt": np.ascontiguousarray(np.asarray(L0_attn_Wv, dtype=np.float32).T),
        "wot": np.ascontiguousarray(np.asarray(L0_attn_Wo, dtype=np.float32).T),
        "whelp": whelp,
        "c_rw": np.ascontiguousarray(np.asarray(L0_ffn_router_w, dtype=np.float32).reshape(8, 128, 8).transpose(1, 0, 2).reshape(128, 64)),
        "c_sgw": np.ascontiguousarray(np.asarray(L0_ffn_shared_gate_w, dtype=np.float32).reshape(8, 128).T),
        "c_biases": biases,
        "c_bhelp": bhelp,
    }
    w1 = np.asarray(L0_ffn_w1, dtype=np.float32)
    w2 = np.asarray(L0_ffn_w2, dtype=np.float32)
    for e in range(NEXP):
        shared[f"w1_{e}"] = np.ascontiguousarray(w1[e])
        shared[f"w2_{e}"] = np.ascontiguousarray(w2[e])

    nc = build_program(_stage)
    in_maps = []
    for c in range(8):
        quar = c % 4
        m = dict(shared)
        cc = _consts(quar)
        m["c_ident"] = cc["ident"]
        m["c_ones128"] = cc["ones128"]
        m["c_ones1x128"] = cc["ones1x128"]
        m["c_half1x128"] = cc["half1x128"]
        m["c_pairmat2"] = cc["pairmat2"]
        m["c_expmat2"] = cc["expmat2"]
        m["c_selmat"] = cc["selmat"]
        m["c_qmask"] = cc["qmask"]
        m["x_t"] = np.ascontiguousarray(x[c * T:(c + 1) * T, :].T)
        in_maps.append(m)

    res = run_bass_kernel_spmd(nc, in_maps, core_ids=list(range(8)))
    out = np.empty((2 * S, E), dtype=np.float32)
    for c in range(8):
        out[c * T:(c + 1) * T, :] = res.results[c]["y_t"].T
    if _stage == "ffn1dbg":
        return out.reshape(2, S, E), [r["dbg_t"] for r in res.results]
    if _stage == "x3dbg":
        return out.reshape(2, S, E), [r["adbg_t"] for r in res.results]
    return out.reshape(2, S, E)


# version 18
# speedup vs baseline: 1.0327x; 1.0327x over previous
import contextlib
import numpy as np
import concourse.bass as bass
import concourse.mybir as mybir
from concourse.bass_utils import run_bass_kernel_spmd

F32 = mybir.dt.float32
BF16 = mybir.dt.bfloat16
AF = mybir.ActivationFunctionType
AL = mybir.AluOpType
AX = mybir.AxisListType.X

E, T, S, H, HD = 1024, 2048, 8192, 16, 64
NJ = 8
NEXP = 8

OFF_KHT = 0
OFF_KN = 131072
OFF_QN = 262144
OFF_VB = 393216
OFF_ST = 526336
HE = 526400
SHARD = 4 * HE


class Sem:
    def __init__(self, nc, es, name):
        self.h = es.enter_context(nc.semaphore(name))
        self.n = 0

    def inc(self, instr, k=1):
        instr.then_inc(self.h, k)
        self.n += k
        return instr


class B:
    def __init__(self, nc, block, es):
        self.nc, self.block, self.es = nc, block, es
        self._sems = {}

    def sem(self, name):
        if name not in self._sems:
            self._sems[name] = Sem(self.nc, self.es, name)
        return self._sems[name]

    def fence(self, exclude=("a_cc",)):
        snap = [(s.h, s.n) for k, s in self._sems.items() if s.n > 0 and k not in exclude]
        for name in ("tensor", "vector", "scalar"):
            def f(e, snap=snap):
                for h, n in snap:
                    e.wait_ge(h, n)
            getattr(self.block, name)(f)



def emit_ln(b, t, src_jc, xh, sq, n1, nb):
    block = b.block
    PS = t["psum"]
    W = nb * 512
    ld = b.sem("ln_ld")
    ssq = b.sem("ln_sq")
    sst = b.sem("ln_st")
    srow = b.sem("ln_row")
    sbc = b.sem("ln_bc")
    snm = b.sem("ln_nm")
    ones128 = t["ones128"]
    ones1 = t["ones1x128"]
    A = t["arena"]
    R = t["rowoff"]
    murow = A[0:1, R: R + W]
    varow = A[0:1, R + 1024: R + 1024 + W]
    mu2 = A[0:1, R + 2048: R + 2048 + W]
    mub = t["mub"][:, 0:W]
    rb = t["rb"][:, 0:W]

    @block.sync
    def _(s):
        for nm in ("f_to", "a_to"):
            ss = b.sem(nm)
            s.wait_ge(ss.h, ss.n)
        s.wait_ge(snm.h, snm.n)
        s.wait_ge(sst.h, sst.n)
        for jc in range(NJ):
            ld.inc(s.dma_start(out=xh(jc), in_=src_jc(jc)), 16)
    ld_goal = ld.n

    @block.scalar
    def _(sc):
        sc.wait_ge(ld.h, ld_goal)
        for jc in range(NJ):
            ins = sc.activation(out=sq(jc), in_=xh(jc), func=AF.Square)
        ssq.inc(ins)
    ssq_goal = ssq.n

    @block.tensor
    def _(tn):
        tn.wait_ge(ld.h, ld_goal)
        for nbi in range(nb):
            for jc in range(NJ):
                ins = tn.matmul(PS[nbi % 2][0:1, 0:512], ones128[:, 0:1],
                                xh(jc)[:, nbi * 512:(nbi + 1) * 512],
                                start=(jc == 0), stop=(jc == NJ - 1))
            sst.inc(ins)
        tn.wait_ge(ssq.h, ssq_goal)
        for nbi in range(nb):
            for jc in range(NJ):
                ins = tn.matmul(PS[2 + nbi % 2][0:1, 0:512], ones128[:, 0:1],
                                sq(jc)[:, nbi * 512:(nbi + 1) * 512],
                                start=(jc == 0), stop=(jc == NJ - 1))
            sst.inc(ins)
    base = sst.n - 2 * nb

    @block.vector
    def _(v):
        for nbi in range(nb):
            v.wait_ge(sst.h, base + nbi + 1)
            v.tensor_scalar(out=murow[:, nbi * 512:(nbi + 1) * 512],
                            in0=PS[nbi % 2][0:1, 0:512], scalar1=1.0 / E,
                            scalar2=None, op0=AL.mult)
        for nbi in range(nb):
            v.wait_ge(sst.h, base + nb + nbi + 1)
            ins = v.tensor_scalar(out=varow[:, nbi * 512:(nbi + 1) * 512],
                                  in0=PS[2 + nbi % 2][0:1, 0:512], scalar1=1.0 / E,
                                  scalar2=None, op0=AL.mult)
        srow.inc(ins)
        v.wait_ge(srow.h, srow.n)
        v.tensor_tensor(out=mu2, in0=murow, in1=murow, op=AL.mult)
        ins = v.tensor_tensor(out=varow, in0=varow, in1=mu2, op=AL.subtract)
        srow.inc(ins)
        v.wait_ge(srow.h, srow.n)
        ins = v.tensor_scalar(out=varow, in0=varow, scalar1=1e-5, scalar2=None, op0=AL.add)
        srow.inc(ins)
        v.wait_ge(srow.h, srow.n)
        ins = v.reciprocal(out=varow, in_=varow)
        srow.inc(ins)
    srow_goal = srow.n

    @block.scalar
    def _(sc):
        sc.wait_ge(srow.h, srow_goal)
        ins = sc.activation(out=varow, in_=varow, func=AF.Sqrt)
        srow.inc(ins)
    srow_goal2 = srow.n

    @block.tensor
    def _(tn):
        tn.wait_ge(srow.h, srow_goal2)
        for nbi in range(nb):
            sbc.inc(tn.matmul(PS[4 + nbi % 2][:, 0:512], ones1[0:1, :],
                              murow[:, nbi * 512:(nbi + 1) * 512], start=True, stop=True))
            sbc.inc(tn.matmul(PS[6 + nbi % 2][:, 0:512], ones1[0:1, :],
                              varow[:, nbi * 512:(nbi + 1) * 512], start=True, stop=True))
    bbase = sbc.n - 2 * nb

    @block.scalar
    def _(sc):
        for nbi in range(nb):
            sc.wait_ge(sbc.h, bbase + 2 * nbi + 2)
            sc.copy(out=mub[:, nbi * 512:(nbi + 1) * 512], in_=PS[4 + nbi % 2][:, 0:512])
            ins = sc.copy(out=rb[:, nbi * 512:(nbi + 1) * 512], in_=PS[6 + nbi % 2][:, 0:512])
        sbc.inc(ins)
    sbc_goal = sbc.n

    @block.vector
    def _(v):
        v.wait_ge(sbc.h, sbc_goal)
        for jc in range(NJ):
            v.tensor_tensor(out=n1(jc), in0=xh(jc), in1=mub, op=AL.subtract)
        for jc in range(NJ):
            ins = v.tensor_tensor(out=n1(jc), in0=n1(jc), in1=rb, op=AL.mult)
        snm.inc(ins)



def emit_ffn(b, t, src_d, dst_d, w, dbg_d=None):
    block = b.block
    A = t["arena"]
    PS = t["psum"]
    M = t["fmap"]
    HT = 1024
    NB = 2
    ld_w = b.sem("f_w")
    ldh = b.sem("f_wh")
    swl = b.sem("f_wl")
    smm = b.sem("f_mm")
    sev = b.sem("f_ev")
    scb = b.sem("f_cb")
    sra = b.sem("f_ra")
    sgu = b.sem("f_gu")
    stp = b.sem("f_tp")
    svv = b.sem("f_vv")
    sso = b.sem("f_so")
    sto = b.sem("f_to")
    ones1 = t["ones1x128"]
    half1 = t["half1x128"]
    ident = t["ident"]

    xh = lambda jc: A[:, M["XH"] + jc * HT: M["XH"] + (jc + 1) * HT]
    n1 = lambda jc: A[:, M["N1"] + jc * HT: M["N1"] + (jc + 1) * HT]
    hb = lambda fc: A[:, M["HB"] + fc * HT: M["HB"] + (fc + 1) * HT]
    racc = lambda ic: A[:, M["RACC"] + ic * HT: M["RACC"] + (ic + 1) * HT]
    cbt = A[:, M["CBT"]: M["CBT"] + HT]
    sigb = A[:, M["SIGB"]: M["SIGB"] + HT]
    gs = A[:, M["GS"]: M["GS"] + HT]
    tmp = A[:, M["TMP"]: M["TMP"] + 512]
    lg = A[0:8, M["LG"]: M["LG"] + HT]
    lgt = A[:, M["LGT"]: M["LGT"] + 8]
    cbm = A[0:8, M["CBM"]: M["CBM"] + HT]
    cbm2 = A[0:1, M["CBM2"]: M["CBM2"] + HT]
    sigrow = A[0:1, M["SIGROW"]: M["SIGROW"] + HT]
    m1c = A[:, M["SML"]: M["SML"] + 1]
    m2c = A[:, M["SML"] + 1: M["SML"] + 2]
    eqt = A[:, M["SML"] + 8: M["SML"] + 16]
    mskt = A[:, M["SML"] + 16: M["SML"] + 24]
    smet = A[:, M["SML"] + 24: M["SML"] + 32]
    smes = A[:, M["SML"] + 32: M["SML"] + 33]
    wb = lambda p: A[:, M["WB"] + p * 1024: M["WB"] + (p + 1) * 1024]

    for hh in range(2):
        c0 = hh * HT
        b.fence()
        emit_ln(b, t, lambda jc, c0=c0: src_d[jc * 128:(jc + 1) * 128, c0:c0 + HT],
                xh, hb, n1, NB)
        b.fence()

        @block.tensor
        def _(tn):
            for nbi in range(NB):
                for jc in range(NJ):
                    ins = tn.matmul(PS[nbi][0:8, 0:512], t["rw_sb"][:, jc * 8:(jc + 1) * 8],
                                    n1(jc)[:, nbi * 512:(nbi + 1) * 512],
                                    start=(jc == 0), stop=(jc == NJ - 1))
                smm.inc(ins)
        mbase = smm.n - NB

        @block.scalar
        def _(sc):
            for nbi in range(NB):
                sc.wait_ge(smm.h, mbase + nbi + 1)
                ins = sc.copy(out=lg[:, nbi * 512:(nbi + 1) * 512], in_=PS[nbi][0:8, 0:512])
            sev.inc(ins)
        lg_done = sev.n

        for tt in range(8):
            @block.tensor
            def _(tn):
                tn.wait_ge(sev.h, lg_done)
                tn.wait_ge(stp.h, stp.n)
                tn.wait_ge(scb.h, scb.n)
                stp.inc(tn.transpose(PS[4][0:128, 0:8], lg[:, tt * 128:(tt + 1) * 128],
                                     ident[0:8, 0:8]))
            tp1 = stp.n

            @block.vector
            def _(v):
                v.wait_ge(stp.h, tp1)
                ins = v.tensor_copy(out=lgt, in_=PS[4][0:128, 0:8])
                stp.inc(ins)
                v.wait_ge(stp.h, stp.n)
                ins = v.reduce_max(m1c, lgt, axis=AX)
                svv.inc(ins)
            g1 = svv.n

            @block.scalar
            def _(sc):
                sc.wait_ge(svv.h, g1)
                ins = sc.mul(m2c, m1c, -1.0)
                svv.inc(ins)
                sc.wait_ge(svv.h, svv.n)
                ins = sc.activation(out=smet, in_=lgt, func=AF.Exp, bias=m2c[:, 0:1])
                svv.inc(ins)
            g2 = svv.n

            @block.vector
            def _(v):
                v.wait_ge(svv.h, g2)
                ins = v.reduce_sum(smes, smet, axis=AX)
                svv.inc(ins)
                v.wait_ge(svv.h, svv.n)
                ins = v.reciprocal(out=smes, in_=smes)
                svv.inc(ins)
                v.wait_ge(svv.h, svv.n)
                ins = v.tensor_scalar(out=smet, in0=smet, scalar1=smes[:, 0:1], scalar2=None, op0=AL.mult)
                svv.inc(ins)
                v.wait_ge(svv.h, svv.n)
                ins = v.reduce_max(m1c, smet, axis=AX)
                svv.inc(ins)
                v.wait_ge(svv.h, svv.n)
                ins = v.tensor_scalar(out=eqt, in0=smet, scalar1=m1c[:, 0:1], scalar2=None, op0=AL.is_equal)
                svv.inc(ins)
                v.wait_ge(svv.h, svv.n)
                ins = v.scalar_tensor_tensor(out=mskt, in0=eqt, scalar=-1e9, in1=smet, op0=AL.mult, op1=AL.add)
                svv.inc(ins)
                v.wait_ge(svv.h, svv.n)
                ins = v.reduce_max(m2c, mskt, axis=AX)
                svv.inc(ins)
                v.wait_ge(svv.h, svv.n)
                ins = v.tensor_scalar(out=eqt, in0=smet, scalar1=m2c[:, 0:1], scalar2=None, op0=AL.is_ge)
                svv.inc(ins)
                v.wait_ge(svv.h, svv.n)
                ins = v.tensor_tensor(out=lgt, in0=smet, in1=eqt, op=AL.mult)
                svv.inc(ins)
            g3 = svv.n

            @block.tensor
            def _(tn):
                tn.wait_ge(svv.h, g3)
                tn.wait_ge(stp.h, stp.n)
                stp.inc(tn.transpose(PS[4][0:8, 0:128], lgt, ident[0:128, 0:128]))
            tp2 = stp.n

            @block.scalar
            def _(sc):
                sc.wait_ge(stp.h, tp2)
                ins = sc.copy(out=cbm[:, tt * 128:(tt + 1) * 128], in_=PS[4][0:8, 0:128])
                scb.inc(ins)
        cbm_ready = scb.n
        if dbg_d is not None and hh == 0:
            snm_now = b.sem("ln_nm").n
            @block.sync
            def _(s):
                s.wait_ge(b.sem("ln_nm").h, snm_now)
                s.wait_ge(scb.h, cbm_ready)
                sto.inc(s.dma_start(out=dbg_d[0:128, 0:1024], in_=n1(0)), 16)
                sto.inc(s.dma_start(out=dbg_d[0:8, 1024:2048], in_=cbm), 16)
        b.fence()

        AB = A[:, :].bitcast(BF16)
        b16 = lambda c0, w: AB[:, 2 * c0: 2 * c0 + 2 * w]
        n1hi = lambda jc: b16(M["XH"] + jc * 512, 512)
        n1lo = lambda jc: b16(M["XH"] + 4096 + jc * 512, 512)
        hshi = lambda fc: b16(M["HB"] + fc * 512, 512)
        hslo = lambda fc: b16(M["HB"] + 4096 + fc * 512, 512)
        wstage = lambda k: A[:, M["WB"] + k * 1024: M["WB"] + (k + 1) * 1024]
        whi1 = lambda p: b16(M["WB"] + 1024 + p * 512, 512)
        wlo1 = lambda p: b16(M["WB"] + 3072 + p * 512, 512)
        stemp = lambda fc: A[:, (M["TMP"] if fc % 2 == 0 else M["GS"]): (M["TMP"] if fc % 2 == 0 else M["GS"]) + 1024]
        ptemp = A[:, M["GS2"]: M["GS2"] + 1024]
        shs = b.sem("f_hs")

        snm2 = b.sem("ln_nm")
        n1r = snm2.n
        @block.vector
        def _(v):
            v.wait_ge(snm2.h, n1r)
            for jc in range(NJ):
                v.tensor_copy(out=n1hi(jc), in_=n1(jc))
            for jc in range(NJ):
                ins = v.tensor_tensor(out=n1lo(jc), in0=n1(jc), in1=n1hi(jc), op=AL.subtract)
            shs.inc(ins)
        n1split = shs.n

        scm = b.sem("f_cm")
        for e in range(NEXP):
            @block.sync
            def _(s):
                s.wait_ge(scb.h, cbm_ready)
                s.wait_ge(stp.h, stp.n)
                scm.inc(s.dma_start(out=cbm2, in_=cbm[e:e + 1, :]), 16)
            cb_ld = scm.n

            @block.tensor
            def _(tn):
                tn.wait_ge(scm.h, cb_ld)
                tn.wait_ge(scb.h, scb.n)
                tn.wait_ge(sev.h, sev.n)
                tn.wait_ge(sgu.h, sgu.n)
                for nbi in range(NB):
                    stp.inc(tn.matmul(PS[2 + nbi][:, 0:512], ones1[0:1, :],
                                      cbm2[0:1, nbi * 512:(nbi + 1) * 512],
                                      start=True, stop=True))
            cb_mm = stp.n

            @block.scalar
            def _(sc):
                sc.wait_ge(stp.h, cb_mm)
                sc.wait_ge(smm.h, smm.n)
                for nbi in range(NB):
                    ins = sc.copy(out=cbt[:, nbi * 512:(nbi + 1) * 512], in_=PS[2 + nbi][:, 0:512])
                scb.inc(ins)
            cbt_ready = scb.n

            sev_at = {}
            sgu_at = {}
            sev_base = sev.n
            sgu_base = sgu.n
            for k in range(4):
                sev_at[2 * k + 1] = sev_base + k + 1
            hbase_at = {}
            for fc in range(NJ):
                wpar = (e * NJ + fc) % 2
                pbank = (fc % 2) * 2
                @block.sync
                def _(s):
                    s.wait_ge(swl.h, swl.n)
                    ld_w.inc(s.dma_start(
                        out=wstage(0).rearrange("p (a c) -> p a c", a=8),
                        in_=w["w1"][e].rearrange("(jc p) c -> p jc c", p=128)[:, :, fc * 128:(fc + 1) * 128]), 16)
                w1_ld = ld_w.n

                @block.gpsimd
                def _(g):
                    if e * NJ + fc >= 2:
                        g.wait_ge(smm.h, smm.n - NB)
                    else:
                        for nm in ("a_to", "a_vv", "a_oe"):
                            ss = b.sem(nm)
                            g.wait_ge(ss.h, ss.n)
                        g.wait_ge(smm.h, smm.n)
                    ldh.inc(g.dma_start(
                        out=whi1(wpar).rearrange("p (a c) -> p a c", a=8),
                        in_=w["w1"][e].rearrange("(jc p) c -> p jc c", p=128)[:, :, fc * 128:(fc + 1) * 128]), 16)
                w1_hi = ldh.n

                @block.vector
                def _(v):
                    v.wait_ge(ld_w.h, w1_ld)
                    v.wait_ge(ldh.h, w1_hi)
                    if e * NJ + fc >= 2:
                        v.wait_ge(smm.h, smm.n - NB)
                    ins = v.tensor_tensor(out=wlo1(wpar), in0=wstage(0), in1=whi1(wpar), op=AL.subtract)
                    swl.inc(ins)
                w1_split = swl.n

                @block.tensor
                def _(tn):
                    tn.wait_ge(swl.h, w1_split)
                    if fc >= 2:
                        tn.wait_ge(sev.h, sev_at[fc - 1] if fc % 2 == 0 else sev_at[fc - 2])
                    else:
                        tn.wait_ge(sev.h, sev.n)
                        tn.wait_ge(scb.h, cbt_ready)
                        tn.wait_ge(sra.h, sra.n)
                    for nbi in range(NB):
                        for jc in range(NJ):
                            ins = tn.matmul(PS[pbank + nbi][:, 0:512], whi1(wpar)[:, jc * 128:(jc + 1) * 128],
                                            n1hi(jc)[:, nbi * 512:(nbi + 1) * 512],
                                            start=(jc == 0), stop=False)
                            ins = tn.matmul(PS[pbank + nbi][:, 0:512], whi1(wpar)[:, jc * 128:(jc + 1) * 128],
                                            n1lo(jc)[:, nbi * 512:(nbi + 1) * 512],
                                            start=False, stop=False)
                            ins = tn.matmul(PS[pbank + nbi][:, 0:512], wlo1(wpar)[:, jc * 128:(jc + 1) * 128],
                                            n1hi(jc)[:, nbi * 512:(nbi + 1) * 512],
                                            start=False, stop=(jc == NJ - 1))
                        smm.inc(ins)
                hbase = smm.n - NB

                if fc % 2 == 1:
                    hbase_at[fc] = hbase

                    @block.vector
                    def _(v, fc=fc):
                        v.wait_ge(sev.h, sev_at[fc])
                        v.wait_ge(scb.h, cbt_ready)
                        for f2 in (fc - 1, fc):
                            v.tensor_tensor(out=ptemp, in0=stemp(f2), in1=cbt, op=AL.mult)
                            sgu.inc(v.tensor_copy(out=hshi(f2), in_=ptemp))
                            ins = v.tensor_tensor(out=hslo(f2), in0=ptemp, in1=hshi(f2), op=AL.subtract)
                            shs.inc(ins)
                    sgu_at[fc] = sgu.n

            @block.scalar
            def _(sc):
                for k in range(4):
                    fc = 2 * k + 1
                    sc.wait_ge(sgu.h, sgu_base + 2 * (k - 1) + 2 if k >= 1 else sgu_base)
                    sc.wait_ge(smm.h, hbase_at[fc] + NB)
                    for f2 in (fc - 1, fc):
                        pb2 = (f2 % 2) * 2
                        for nbi in range(NB):
                            ins = sc.activation(out=stemp(f2)[:, nbi * 512:(nbi + 1) * 512],
                                                in_=PS[pb2 + nbi][:, 0:512], func=AF.Silu)
                    sev.inc(ins)
            assert sev.n == sev_base + 4
            h_scaled = shs.n

            sra_at = {}
            for ic in range(NJ):
                wpar = (e * NJ + ic) % 2
                pbank = 4
                @block.sync
                def _(s):
                    s.wait_ge(swl.h, swl.n)
                    ld_w.inc(s.dma_start(
                        out=wstage(0).rearrange("p (a c) -> p a c", a=8),
                        in_=w["w2"][e].rearrange("(fc p) c -> p fc c", p=128)[:, :, ic * 128:(ic + 1) * 128]), 16)
                w2_ld = ld_w.n

                @block.gpsimd
                def _(g):
                    if e * NJ + ic >= 2:
                        g.wait_ge(smm.h, smm.n - NB)
                    ldh.inc(g.dma_start(
                        out=whi1(2 + wpar).rearrange("p (a c) -> p a c", a=8),
                        in_=w["w2"][e].rearrange("(fc p) c -> p fc c", p=128)[:, :, ic * 128:(ic + 1) * 128]), 16)
                w2_hi = ldh.n

                @block.vector
                def _(v):
                    v.wait_ge(ld_w.h, w2_ld)
                    v.wait_ge(ldh.h, w2_hi)
                    if e * NJ + ic >= 2:
                        v.wait_ge(smm.h, smm.n - NB)
                    ins = v.tensor_tensor(out=wlo1(2 + wpar), in0=wstage(0), in1=whi1(2 + wpar), op=AL.subtract)
                    swl.inc(ins)
                w2_split = swl.n

                @block.tensor
                def _(tn):
                    tn.wait_ge(swl.h, w2_split)
                    tn.wait_ge(shs.h, h_scaled)
                    tn.wait_ge(sev.h, sev.n)
                    if ic >= 1:
                        tn.wait_ge(sra.h, sra_at[ic - 1])
                    else:
                        tn.wait_ge(sra.h, sra.n)
                    for nbi in range(NB):
                        for fc in range(NJ):
                            ins = tn.matmul(PS[pbank + nbi][:, 0:512], whi1(2 + wpar)[:, fc * 128:(fc + 1) * 128],
                                            hshi(fc)[:, nbi * 512:(nbi + 1) * 512],
                                            start=(fc == 0), stop=False)
                            ins = tn.matmul(PS[pbank + nbi][:, 0:512], whi1(2 + wpar)[:, fc * 128:(fc + 1) * 128],
                                            hslo(fc)[:, nbi * 512:(nbi + 1) * 512],
                                            start=False, stop=False)
                            ins = tn.matmul(PS[pbank + nbi][:, 0:512], wlo1(2 + wpar)[:, fc * 128:(fc + 1) * 128],
                                            hshi(fc)[:, nbi * 512:(nbi + 1) * 512],
                                            start=False, stop=(fc == NJ - 1))
                        smm.inc(ins)
                rbase = smm.n - NB

                @block.vector
                def _(v):
                    for nbi in range(NB):
                        v.wait_ge(smm.h, rbase + nbi + 1)
                        sl = slice(nbi * 512, (nbi + 1) * 512)
                        if e == 0:
                            ins = v.tensor_copy(out=racc(ic)[:, sl], in_=PS[pbank + nbi][:, 0:512])
                        else:
                            ins = v.tensor_tensor(out=racc(ic)[:, sl], in0=racc(ic)[:, sl],
                                                  in1=PS[pbank + nbi][:, 0:512], op=AL.add)
                    sra.inc(ins)
                sra_at[ic] = sra.n
        @block.tensor
        def _(tn):
            for nbi in range(NB):
                for jc in range(NJ):
                    ins = tn.matmul(PS[6 + nbi][0:1, 0:512], t["sgw_sb"][:, jc:jc + 1],
                                    n1(jc)[:, nbi * 512:(nbi + 1) * 512],
                                    start=(jc == 0), stop=(jc == NJ - 1))
                smm.inc(ins)
        sgbase = smm.n - NB

        @block.scalar
        def _(sc):
            for nbi in range(NB):
                sc.wait_ge(smm.h, sgbase + nbi + 1)
                ins = sc.activation(out=sigrow[:, nbi * 512:(nbi + 1) * 512],
                                    in_=PS[6 + nbi][0:1, 0:512], func=AF.Sigmoid)
            sev.inc(ins)
        sig_done = sev.n

        @block.tensor
        def _(tn):
            tn.wait_ge(sev.h, sig_done)
            for nbi in range(NB):
                stp.inc(tn.matmul(PS[2 + nbi][:, 0:512], half1[0:1, :],
                                  sigrow[:, nbi * 512:(nbi + 1) * 512], start=True, stop=True))
        sgb2 = stp.n

        @block.scalar
        def _(sc):
            sc.wait_ge(stp.h, sgb2)
            for nbi in range(NB):
                ins = sc.copy(out=sigb[:, nbi * 512:(nbi + 1) * 512], in_=PS[2 + nbi][:, 0:512])
            sev.inc(ins)
        b.fence()

        guphi = lambda gp: b16(M["HB"] + gp * 512, 512)
        guplo = lambda gp: b16(M["HB"] + 4096 + gp * 512, 512)
        gutmp = A[:, M["TMP"]: M["TMP"] + 1024]
        gu_done_at = {}
        for gp in range(8):
            for part in range(2):
                gc = gp + 8 * part
                @block.sync
                def _(s):
                    s.wait_ge(swl.h, swl.n)
                    ld_w.inc(s.dma_start(
                        out=wstage(0).rearrange("p (a c) -> p a c", a=8),
                        in_=w["sgu"].rearrange("(jc p) c -> p jc c", p=128)[:, :, gc * 128:(gc + 1) * 128]), 16)
                g_ld = ld_w.n

                @block.gpsimd
                def _(g):
                    g.wait_ge(smm.h, smm.n)
                    ldh.inc(g.dma_start(
                        out=whi1(part).rearrange("p (a c) -> p a c", a=8),
                        in_=w["sgu"].rearrange("(jc p) c -> p jc c", p=128)[:, :, gc * 128:(gc + 1) * 128]), 16)
                g_hi = ldh.n

                @block.vector
                def _(v):
                    v.wait_ge(ld_w.h, g_ld)
                    v.wait_ge(ldh.h, g_hi)
                    v.wait_ge(smm.h, smm.n)
                    ins = v.tensor_tensor(out=wlo1(part), in0=wstage(0), in1=whi1(part), op=AL.subtract)
                    swl.inc(ins)
                g_split = swl.n

                @block.tensor
                def _(tn):
                    tn.wait_ge(swl.h, g_split)
                    if gp == 0 and part == 0:
                        tn.wait_ge(sra.h, sra.n)
                        tn.wait_ge(sev.h, sev.n)
                    elif part == 0:
                        tn.wait_ge(sgu.h, gu_done_at[gp - 1])
                    for nbi in range(NB):
                        pb = PS[0 + nbi] if part == 0 else PS[2 + nbi]
                        for jc in range(NJ):
                            tn.matmul(pb[:, 0:512], whi1(part)[:, jc * 128:(jc + 1) * 128],
                                      n1hi(jc)[:, nbi * 512:(nbi + 1) * 512],
                                      start=(jc == 0), stop=False)
                            tn.matmul(pb[:, 0:512], whi1(part)[:, jc * 128:(jc + 1) * 128],
                                      n1lo(jc)[:, nbi * 512:(nbi + 1) * 512],
                                      start=False, stop=False)
                            ins = tn.matmul(pb[:, 0:512], wlo1(part)[:, jc * 128:(jc + 1) * 128],
                                            n1hi(jc)[:, nbi * 512:(nbi + 1) * 512],
                                            start=False, stop=(jc == NJ - 1))
                        smm.inc(ins)
            gu_mm = smm.n

            @block.scalar
            def _(sc):
                sc.wait_ge(smm.h, gu_mm)
                for nbi in range(NB):
                    ins = sc.activation(out=gs[:, nbi * 512:(nbi + 1) * 512],
                                        in_=PS[nbi][:, 0:512], func=AF.Silu)
                sev.inc(ins)
            gsil = sev.n

            @block.vector
            def _(v):
                v.wait_ge(sev.h, gsil)
                for nbi in range(NB):
                    ins = v.tensor_tensor(out=gutmp[:, nbi * 512:(nbi + 1) * 512],
                                          in0=gs[:, nbi * 512:(nbi + 1) * 512],
                                          in1=PS[2 + nbi][:, 0:512], op=AL.mult)
                sgu.inc(ins)
                v.wait_ge(sgu.h, sgu.n)
                v.tensor_copy(out=guphi(gp), in_=gutmp)
                ins = v.tensor_tensor(out=guplo(gp), in0=gutmp, in1=guphi(gp), op=AL.subtract)
                shs.inc(ins)
            gu_done_at[gp] = sgu.n
        gup_done = shs.n
        b.fence()

        dso_at = {}
        sxr = b.sem("f_xr")
        xrslot = A[:, M["GS"]: M["GS"] + 1024]
        tmp2 = lambda i_: A[:, M["TMP"] + (i_ % 2) * 512: M["TMP"] + (i_ % 2) * 512 + 512]
        for ic in range(NJ):
            @block.sync
            def _(s):
                if ic > 0:
                    s.wait_ge(sso.h, dso_at[ic - 1])
                else:
                    s.wait_ge(sgu.h, sgu.n)
                    s.wait_ge(sev.h, sev.n)
                sxr.inc(s.dma_start(out=xrslot, in_=src_d[ic * 128:(ic + 1) * 128, c0:c0 + HT]), 16)

            @block.sync
            def _(s):
                s.wait_ge(swl.h, swl.n)
                ld_w.inc(s.dma_start(
                    out=wstage(0).rearrange("p (a c) -> p a c", a=8),
                    in_=w["sd"].rearrange("(gc p) c -> p gc c", p=128)[:, :, ic * 128:(ic + 1) * 128]), 16)
            d_ld = ld_w.n

            @block.gpsimd
            def _(g):
                g.wait_ge(smm.h, smm.n)
                ldh.inc(g.dma_start(
                    out=whi1(0).rearrange("p (a c) -> p a c", a=8),
                    in_=w["sd"].rearrange("(gc p) c -> p gc c", p=128)[:, :, ic * 128:(ic + 1) * 128]), 16)
            d_hi = ldh.n

            @block.vector
            def _(v):
                v.wait_ge(ld_w.h, d_ld)
                v.wait_ge(ldh.h, d_hi)
                v.wait_ge(smm.h, smm.n)
                ins = v.tensor_tensor(out=wlo1(0), in0=wstage(0), in1=whi1(0), op=AL.subtract)
                swl.inc(ins)
            d_split = swl.n

            @block.tensor
            def _(tn):
                tn.wait_ge(swl.h, d_split)
                tn.wait_ge(shs.h, gup_done)
                if ic >= 1:
                    tn.wait_ge(sso.h, dso_at[ic - 1])
                else:
                    tn.wait_ge(sso.h, sso.n)
                    tn.wait_ge(sgu.h, sgu.n)
                for nbi in range(NB):
                    for gc in range(NJ):
                        tn.matmul(PS[4 + nbi][:, 0:512], whi1(0)[:, gc * 128:(gc + 1) * 128],
                                  guphi(gc)[:, nbi * 512:(nbi + 1) * 512],
                                  start=(gc == 0), stop=False)
                        tn.matmul(PS[4 + nbi][:, 0:512], whi1(0)[:, gc * 128:(gc + 1) * 128],
                                  guplo(gc)[:, nbi * 512:(nbi + 1) * 512],
                                  start=False, stop=False)
                        ins = tn.matmul(PS[4 + nbi][:, 0:512], wlo1(0)[:, gc * 128:(gc + 1) * 128],
                                        guphi(gc)[:, nbi * 512:(nbi + 1) * 512],
                                        start=False, stop=(gc == NJ - 1))
                    smm.inc(ins)
            dbase = smm.n - NB

            xr_goal = sxr.n
            @block.vector
            def _(v):
                v.wait_ge(sxr.h, xr_goal)
                v.wait_ge(smm.h, dbase + NB)
                for nbi in range(NB):
                    sl = slice(nbi * 512, (nbi + 1) * 512)
                    v.tensor_tensor(out=tmp2(ic), in0=PS[4 + nbi][:, 0:512], in1=sigb[:, sl], op=AL.mult)
                    v.scalar_tensor_tensor(out=n1(ic)[:, sl], in0=racc(ic)[:, sl], scalar=0.5,
                                           in1=xrslot[:, sl], op0=AL.mult, op1=AL.add)
                    ins = v.tensor_tensor(out=n1(ic)[:, sl], in0=n1(ic)[:, sl], in1=tmp2(ic), op=AL.add)
                    sso.inc(ins)
            dso_at[ic] = sso.n
            out_done = sso.n

            @block.sync
            def _(s):
                s.wait_ge(sso.h, out_done)
                sto.inc(s.dma_start(out=dst_d[ic * 128:(ic + 1) * 128, c0:c0 + HT],
                                    in_=n1(ic)), 16)
        b.fence()

    @block.sync
    def _(s):
        s.wait_ge(sto.h, sto.n)
    b.fence()



def emit_attention(b, t, src_d, dst_d, aw):
    block = b.block
    A = t["arena"]
    PS = t["psum"]
    ident = t["ident"]
    ones1 = t["ones1x128"]
    pairmat2 = t["pairmat2"]
    expmat2 = t["expmat2"]
    selmat = t["selmat"]
    qmask = t["qmask"]
    biases = t["biases_sb"]
    bhelp = t["bhelp_sb"]

    ld_w = b.sem("a_w")
    smm = b.sem("a_mm")
    sev = b.sem("a_ev")
    svv = b.sem("a_vv")
    stp = b.sem("a_tp")
    pk = b.sem("a_pk")
    cc = b.sem("a_cc")
    kld = b.sem("a_kld")
    ssc = b.sem("a_sc")
    sxp = b.sem("a_xp")
    sav = b.sem("a_av")
    soe = b.sem("a_oe")
    sqm = b.sem("a_qm")
    sto = b.sem("a_to")

    agin, agout = aw["agin"], aw["agout"]
    q_d = aw["q_d"]

    M2_N1, M2_XH, M2_SQ = 0, 16384, 24576
    for hh in range(2):
        c0 = hh * 1024
        b.fence()
        emit_ln(b, t,
                lambda jc, c0=c0: src_d[jc * 128:(jc + 1) * 128, c0:c0 + 1024],
                lambda jc: A[:, M2_XH + jc * 1024: M2_XH + (jc + 1) * 1024],
                lambda jc: A[:, M2_SQ + jc * 1024: M2_SQ + (jc + 1) * 1024],
                lambda jc, c0=c0: A[:, M2_N1 + jc * 2048 + c0: M2_N1 + jc * 2048 + c0 + 1024],
                2)
    b.fence()
    n2 = lambda jc: A[:, M2_N1 + jc * 2048: M2_N1 + (jc + 1) * 2048]

    QP, KP, VP = 16384, 18432, 20480
    HELK, HELQ, SQH = 22528, 24576, 26624
    QNP, RDUP = 28672, 30720
    PLN = 32768
    VT = 36864
    ROWS = 37904
    WB0 = 38416
    WHEL = t["fmap"]["WB"]
    qp = A[:, QP: QP + 2048]
    kp = A[:, KP: KP + 2048]
    vp = A[:, VP: VP + 2048]
    helk = A[:, HELK: HELK + 2048]
    helq = A[:, HELQ: HELQ + 2048]
    sqh = A[:, SQH: SQH + 2048]
    qnp = A[:, QNP: QNP + 2048]
    rdup = A[:, RDUP: RDUP + 2048]
    pln = A[:, PLN: PLN + 4096]
    vt = A[:, VT: VT + 1040]
    rows = A[0:1, ROWS: ROWS + 512]
    wbA = lambda p: A[:, WB0 + p * 1024: WB0 + (p + 1) * 1024]
    whel_sb = A[:, WHEL: WHEL + 1024]
    assert WB0 + 2048 <= t["fmap"]["MUBS"], (WB0, t["fmap"]["MUBS"])
    assert WHEL >= t["fmap"]["WB"]

    ldc = b.sem("a_ldc")

    @block.sync
    def _(s):
        ldc.inc(s.dma_start(out=whel_sb.rearrange("p (a c) -> p a c", a=8),
                            in_=aw["whelp"].rearrange("(hp p) c -> p hp c", p=128)), 16)
    whel_ready = ldc.n

    def proj_pair(hp, wt_d, dst, bias_col, wslot):
        @block.sync
        def _(s):
            s.wait_ge(smm.h, smm.n)
            ld_w.inc(s.dma_start(
                out=wbA(wslot).rearrange("p (a c) -> p a c", a=8),
                in_=wt_d.rearrange("(jc p) c -> p jc c", p=128)[:, :, hp * 128:(hp + 1) * 128]), 16)

        evb = sev.n
        @block.tensor
        def _(tn):
            tn.wait_ge(ld_w.h, ld_w.n)
            tn.wait_ge(sev.h, evb)
            for nbi in range(4):
                if nbi >= 2:
                    tn.wait_ge(sev.h, evb + nbi - 1)
                for jc in range(NJ):
                    ins = tn.matmul(PS[nbi % 2][:, 0:512], wbA(wslot)[:, jc * 128:(jc + 1) * 128],
                                    n2(jc)[:, nbi * 512:(nbi + 1) * 512],
                                    start=(jc == 0), stop=(jc == NJ - 1))
                smm.inc(ins)
        pbase = smm.n - 4

        @block.scalar
        def _(sc):
            for nbi in range(4):
                sc.wait_ge(smm.h, pbase + nbi + 1)
                ins = sc.activation(out=dst[:, nbi * 512:(nbi + 1) * 512], in_=PS[nbi % 2][:, 0:512],
                                    func=AF.Identity, bias=biases[:, bias_col:bias_col + 1])
                sev.inc(ins)

    def hel_pair(hp, src, dsttile):
        evb = sev.n
        @block.tensor
        def _(tn):
            tn.wait_ge(ldc.h, whel_ready)
            tn.wait_ge(sev.h, evb)
            for nbi in range(4):
                if nbi >= 2:
                    tn.wait_ge(sev.h, evb + nbi - 1)
                stp.inc(tn.matmul(PS[2 + nbi % 2][:, 0:512], whel_sb[:, hp * 128:(hp + 1) * 128],
                                  src[:, nbi * 512:(nbi + 1) * 512], start=True, stop=True))
        hbase = stp.n - 4

        @block.scalar
        def _(sc):
            for nbi in range(4):
                sc.wait_ge(stp.h, hbase + nbi + 1)
                ins = sc.activation(out=dsttile[:, nbi * 512:(nbi + 1) * 512],
                                    in_=PS[2 + nbi % 2][:, 0:512], func=AF.Identity,
                                    bias=bhelp[:, hp:hp + 1])
                sev.inc(ins)

    def l2norm_pair(srctile):
        @block.scalar
        def _(sc):
            sc.wait_ge(sev.h, sev.n)
            sc.wait_ge(svv.h, svv.n)
            ins = sc.activation(out=sqh, in_=srctile, func=AF.Square)
            svv.inc(ins)
        sq_done = svv.n

        svb = svv.n
        @block.tensor
        def _(tn):
            tn.wait_ge(svv.h, sq_done)
            for nbi in range(4):
                if nbi >= 2:
                    tn.wait_ge(svv.h, svb + nbi - 1)
                stp.inc(tn.matmul(PS[4 + nbi % 2][0:64, 0:512], pairmat2[:, :],
                                  sqh[:, nbi * 512:(nbi + 1) * 512], start=True, stop=True))
        psbase = stp.n - 4

        @block.vector
        def _(v):
            for nbi in range(4):
                v.wait_ge(stp.h, psbase + nbi + 1)
                ins = v.tensor_scalar(out=qnp[0:64, nbi * 512:(nbi + 1) * 512],
                                      in0=PS[4 + nbi % 2][0:64, 0:512],
                                      scalar1=1e-24, scalar2=None, op0=AL.max)
                svv.inc(ins)
            v.wait_ge(svv.h, svv.n)
            ins = v.reciprocal(out=qnp[0:64, :], in_=qnp[0:64, :])
            svv.inc(ins)
        rc_done = svv.n

        @block.scalar
        def _(sc):
            sc.wait_ge(svv.h, rc_done)
            ins = sc.activation(out=qnp[0:64, :], in_=qnp[0:64, :], func=AF.Sqrt)
            svv.inc(ins)
        r_done = svv.n

        svb2 = svv.n
        @block.tensor
        def _(tn):
            tn.wait_ge(svv.h, r_done)
            for nbi in range(4):
                if nbi >= 2:
                    tn.wait_ge(svv.h, svb2 + nbi - 1)
                stp.inc(tn.matmul(PS[6 + nbi % 2][:, 0:512], expmat2[:, :],
                                  qnp[0:64, nbi * 512:(nbi + 1) * 512], start=True, stop=True))
        exbase = stp.n - 4

        @block.vector
        def _(v):
            for nbi in range(4):
                v.wait_ge(stp.h, exbase + nbi + 1)
                ins = v.tensor_copy(out=rdup[:, nbi * 512:(nbi + 1) * 512],
                                    in_=PS[6 + nbi % 2][:, 0:512])
                svv.inc(ins)
            v.wait_ge(svv.h, svv.n)
            ins = v.tensor_tensor(out=srctile, in0=srctile, in1=rdup, op=AL.mult)
            svv.inc(ins)
        return svv.n

    def deinterleave(srctile, done_val):
        @block.vector
        def _(v):
            v.wait_ge(svv.h, done_val)
            v.wait_ge(pk.h, pk.n)
            for band in range(4):
                r0 = band * 32
                for par in range(2):
                    ins = v.tensor_copy(out=pln[r0:r0 + 32, par * 1024:(par + 1) * 1024],
                                        in_=srctile[r0:r0 + 32, par::2])
                    svv.inc(ins)
        return svv.n


    for hp in range(8):
        b.fence()
        proj_pair(hp, aw["wqt"], qp, 0 + hp, 0)
        proj_pair(hp, aw["wkt"], kp, 8 + hp, 1)
        proj_pair(hp, aw["wvt"], vp, 16 + hp, 2)
        qkv_done = sev.n

        @block.sync
        def _(s):
            s.wait_ge(sev.h, qkv_done)
            pk.inc(s.dma_start(out=q_d[hp * 128:(hp + 1) * 128, :], in_=qp), 16)
            if aw.get("adbg") is not None and hp == 0:
                pk.inc(s.dma_start(out=aw["adbg"][0:128, 6656:8704], in_=n2(0)), 16)
                pk.inc(s.dma_start(out=aw["adbg"][0:128, 8704:10752], in_=qp), 16)
                pk.inc(s.dma_start(out=aw["adbg"][0:128, 10752:12288], in_=kp[:, 0:1536]), 16)

        for hh in range(2):
            head = hp * 2 + hh
            g, hl = head // 4, head % 4
            sh = hl * HE
            @block.gpsimd
            def _(gp):
                gp.wait_ge(sev.h, qkv_done)
                pk.inc(gp.dma_start(
                    out=agin[g][sh + OFF_KHT: sh + OFF_KHT + 131072].rearrange("(a c) -> a c", a=64),
                    in_=kp[hh * 64:(hh + 1) * 64, :]), 16)

        for side, tile in ((0, helk), (1, helq)):
            hel_pair(hp, kp if side == 0 else qp, tile)
            dv = l2norm_pair(tile)
            if aw.get("adbg") is not None and hp == 0 and side == 1:
                @block.sync
                def _(s):
                    s.wait_ge(svv.h, dv)
                    sto.inc(s.dma_start(out=aw["adbg"][0:128, 0:2048], in_=helq), 16)
            dv2 = deinterleave(tile, dv)
            if aw.get("adbg") is not None and hp == 0 and side == 1:
                @block.sync
                def _(s):
                    s.wait_ge(svv.h, dv2)
                    sto.inc(s.dma_start(out=aw["adbg"][0:128, 2048:6144], in_=pln), 16)
            for hh in range(2):
                head = hp * 2 + hh
                g, hl = head // 4, head % 4
                sh = hl * HE
                @block.gpsimd
                def _(gp):
                    gp.wait_ge(svv.h, dv2)
                    off = OFF_KN if side == 0 else OFF_QN
                    for q in range(4):
                        comp, par = q // 2, q % 2
                        src = pln[hh * 64 + comp * 32: hh * 64 + comp * 32 + 32,
                                  par * 1024:(par + 1) * 1024]
                        pk.inc(gp.dma_start(
                            out=agin[g][sh + off + q * 32768: sh + off + (q + 1) * 32768].rearrange("(a c) -> a c", a=32),
                            in_=src), 16)

        for hh in range(2):
            head = hp * 2 + hh
            g, hl = head // 4, head % 4
            sh = hl * HE

            @block.vector
            def _(v):
                v.wait_ge(pk.h, pk.n)
                ins = v.memset(vt[:, 64::65], 1.0)
                svv.inc(ins)
            vt_ones = svv.n

            for tb in range(16):
                @block.tensor
                def _(tn):
                    tn.wait_ge(sev.h, qkv_done)
                    tn.wait_ge(stp.h, stp.n)
                    stp.inc(tn.transpose(PS[2][:, 0:128], vp[:, tb * 128:(tb + 1) * 128], ident[:, :]))
                tdone = stp.n

                @block.vector
                def _(v):
                    v.wait_ge(stp.h, tdone)
                    v.wait_ge(svv.h, vt_ones)
                    ins = v.tensor_copy(out=vt[:, tb * 65: tb * 65 + 64],
                                        in_=PS[2][:, hh * 64: hh * 64 + 64])
                    stp.inc(ins)
            vt_done = stp.n

            @block.scalar
            def _(sc):
                sc.wait_ge(sev.h, qkv_done)
                sc.wait_ge(svv.h, svv.n)
                ins = sc.activation(out=sqh, in_=kp, func=AF.Square)
                svv.inc(ins)
            ksq = svv.n

            @block.tensor
            def _(tn):
                tn.wait_ge(svv.h, ksq)
                tn.wait_ge(stp.h, stp.n)
                for nbi in range(4):
                    stp.inc(tn.matmul(PS[4 + nbi][0:1, 0:512], selmat[:, hh:hh + 1],
                                      sqh[:, nbi * 512:(nbi + 1) * 512], start=True, stop=True))
            kbase = stp.n - 4

            @block.vector
            def _(v):
                for nbi in range(4):
                    v.wait_ge(stp.h, kbase + nbi + 1)
                    ins = v.reduce_max(rows[:, 16 + nbi: 17 + nbi], PS[4 + nbi][0:1, 0:512], axis=AX)
                    svv.inc(ins)
                v.wait_ge(svv.h, svv.n)
                ins = v.reduce_max(rows[:, 20:21], rows[:, 16:20], axis=AX)
                svv.inc(ins)
            kmx = svv.n

            @block.scalar
            def _(sc):
                sc.wait_ge(svv.h, kmx)
                ins = sc.activation(out=rows[:, 21:22], in_=rows[:, 20:21], func=AF.Sqrt)
                svv.inc(ins)
            kmx2 = svv.n

            @block.gpsimd
            def _(gp):
                gp.wait_ge(svv.h, kmx2)
                gp.wait_ge(stp.h, vt_done)
                pk.inc(gp.dma_start(
                    out=agin[g][sh + OFF_VB: sh + OFF_VB + 1040 * 128].rearrange("(a c) -> a c", a=128),
                    in_=vt), 16)
                pk.inc(gp.dma_start(
                    out=agin[g][sh + OFF_ST: sh + OFF_ST + 1].rearrange("(a c) -> a c", a=1),
                    in_=rows[:, 21:22]), 16)

        if hp % 2 == 1:
            g = hp // 2
            @block.gpsimd
            def _(gp):
                gp.wait_ge(pk.h, pk.n)
                ins = gp.collective_compute(
                    "AllGather", AL.bypass,
                    replica_groups=[[0, 1, 2, 3], [4, 5, 6, 7]],
                    ins=[agin[g].opt()],
                    outs=[agout[g].opt()],
                )
                cc.inc(ins, 1)

    pack_goals = [(smm.h, smm.n), (stp.h, stp.n), (svv.h, svv.n), (sev.h, sev.n), (pk.h, pk.n)]
    b.fence()

    KT, VTA, QT, PT, OACC = 0, 8192, 12352, 14400, 18496
    APAIR, QW, AHEAD = 34880, 36928, 45120
    ROWS2, RZ, WOB, NEGM = 47168, 47696, 45120 + 0, 47680
    AHEAD = 45120
    ROWS2 = 47168
    NEGM = 47680
    RZ = 47696
    WOB = 49744
    kt = lambda p: A[:, KT + p * 4096: KT + (p + 1) * 4096]
    vta = lambda p: A[:, VTA + p * 2080: VTA + (p + 1) * 2080]
    qt = A[:, QT: QT + 2048]
    pt = lambda p: A[:, PT + p * 2048: PT + (p + 1) * 2048]
    oacc = lambda ic: A[:, OACC + ic * 2048: OACC + (ic + 1) * 2048]
    apair = A[:, APAIR: APAIR + 2048]
    qw = A[:, QW: QW + 8192]
    ahead = A[:, AHEAD: AHEAD + 2048]
    rows2 = A[0:1, ROWS2: ROWS2 + 512]
    negm = A[:, NEGM: NEGM + 1]
    rz = A[:, RZ: RZ + 2048]
    wob = A[:, WOB: WOB + 1024]

    def ag_off(g, r, hl, field):
        return r * SHARD + hl * HE + field

    def assemble_kv(eng, head, th, pbuf):
        g, hl = head // 4, head % 4
        ag = agout[g]
        for i in range(2):
            r = 2 * th + i
            kld.inc(eng.dma_start(
                out=kt(pbuf)[0:64, i * 2048:(i + 1) * 2048],
                in_=ag[ag_off(g, r, hl, OFF_KHT): ag_off(g, r, hl, OFF_KHT) + 131072].rearrange("(a c) -> a c", a=64)), 16)
        for par in range(2):
            for r4 in range(4):
                o = OFF_KN + (2 * th + par) * 32768
                src = ag[ag_off(g, r4, hl, o): ag_off(g, r4, hl, o) + 32768].rearrange("(a c) -> a c", a=32)
                kld.inc(eng.dma_start(
                    out=kt(pbuf)[64 + 32 * par: 96 + 32 * par, r4 * 1024:(r4 + 1) * 1024],
                    in_=src), 16)
        for i in range(2):
            r = 2 * th + i
            kld.inc(eng.dma_start(
                out=vta(pbuf)[:, i * 1040:(i + 1) * 1040],
                in_=ag[ag_off(g, r, hl, OFF_VB): ag_off(g, r, hl, OFF_VB) + 1040 * 128].rearrange("(a c) -> a c", a=128)), 16)

    @block.vector
    def _(v):
        v.wait_ge(pk.h, pk.n)
        for ic in range(NJ):
            ins = v.memset(oacc(ic), 0.0)
        soe.inc(ins)

    for head in range(H):
        g, hl = head // 4, head % 4
        hp, hh = head // 2, head % 2
        sc_head0 = ssc.n
        xp_head0 = sxp.n
        av_head0 = sav.n

        @block.sync
        def _(s):
            s.wait_ge(cc.h, g + 1)
            for hh_, nn_ in pack_goals:
                s.wait_ge(hh_, nn_)
            s.wait_ge(ssc.h, ssc.n)
            s.wait_ge(sav.h, sav.n)
            assemble_kv(s, head, 0, 0)
        kld_h0 = kld.n

        @block.sync
        def _(s):
            s.wait_ge(cc.h, g + 1)
            for hh_, nn_ in pack_goals:
                s.wait_ge(hh_, nn_)
            s.wait_ge(sqm.h, sqm.n)
            s.wait_ge(ssc.h, ssc.n)
            kld.inc(s.dma_start(out=qt[0:64, :], in_=q_d[head * 64:(head + 1) * 64, :]), 16)
            for wsel in range(4):
                p_pl, s2 = wsel // 2, wsel % 2
                for par in range(2):
                    for i in range(2):
                        r = 2 * s2 + i
                        o = OFF_QN + (2 * p_pl + par) * 32768
                        src = agout[g][ag_off(g, r, hl, o): ag_off(g, r, hl, o) + 32768].rearrange("(a c) -> a c", a=32)
                        kld.inc(s.dma_start(
                            out=qw[64 + 32 * par: 96 + 32 * par,
                                   wsel * 2048 + i * 1024: wsel * 2048 + (i + 1) * 1024],
                            in_=src), 16)
            stat_src = agout[g].rearrange("(r x) -> r x", r=4)[:, hl * HE + OFF_ST: hl * HE + OFF_ST + 1]
            kld.inc(s.dma_start(out=rows2[0:1, 0:4], in_=stat_src, allow_slow_non_contiguous=True), 16)
        kld_q = kld.n

        @block.vector
        def _(v):
            v.wait_ge(kld.h, kld_q)
            for wsel in range(4):
                if wsel == 0:
                    ins = v.tensor_scalar(out=qt[64:128, :], in0=qw[64:128, 0:2048],
                                          scalar1=qmask[64:128, 0:1], scalar2=None, op0=AL.mult)
                else:
                    ins = v.scalar_tensor_tensor(out=qt[64:128, :],
                                                 in0=qw[64:128, wsel * 2048:(wsel + 1) * 2048],
                                                 scalar=qmask[64:128, wsel:wsel + 1],
                                                 in1=qt[64:128, :], op0=AL.mult, op1=AL.add)
                sqm.inc(ins)
                v.wait_ge(sqm.h, sqm.n)
        qsel_done = sqm.n

        @block.scalar
        def _(sc):
            sc.wait_ge(sqm.h, qsel_done)
            ins = sc.mul(qt[0:64, :], qt[0:64, :], 0.0625)
            sqm.inc(ins)
            ins = sc.mul(qt[64:128, :], qt[64:128, :], 1.0 / 64.0)
            sqm.inc(ins)
            sc.wait_ge(sqm.h, sqm.n)
            ins = sc.activation(out=qw[0:64, 0:2048], in_=qt[0:64, :], func=AF.Square)
            sqm.inc(ins)
        qsq_done = sqm.n

        @block.tensor
        def _(tn):
            tn.wait_ge(sqm.h, qsq_done)
            tn.wait_ge(soe.h, soe.n)
            tn.wait_ge(stp.h, stp.n)
            for nbi in range(4):
                stp.inc(tn.matmul(PS[4 + nbi][0:1, 0:512], t["ones128"][0:64, 0:1],
                                  qw[0:64, nbi * 512:(nbi + 1) * 512], start=True, stop=True))
        qs_base = stp.n - 4

        @block.vector
        def _(v):
            for nbi in range(4):
                v.wait_ge(stp.h, qs_base + nbi + 1)
                ins = v.reduce_max(rows2[:, 8 + nbi: 9 + nbi], PS[4 + nbi][0:1, 0:512], axis=AX)
                svv.inc(ins)
            v.wait_ge(svv.h, svv.n)
            ins = v.reduce_max(rows2[:, 12:13], rows2[:, 8:12], axis=AX)
            svv.inc(ins)
            v.wait_ge(svv.h, svv.n)
            ins = v.reduce_max(rows2[:, 13:14], rows2[:, 0:4], axis=AX)
            svv.inc(ins)
        qk_red = svv.n

        @block.scalar
        def _(sc):
            sc.wait_ge(svv.h, qk_red)
            ins = sc.activation(out=rows2[:, 14:15], in_=rows2[:, 12:13], func=AF.Sqrt)
            svv.inc(ins)
        qmx = svv.n

        @block.vector
        def _(v):
            v.wait_ge(svv.h, qmx)
            ins = v.tensor_tensor(out=rows2[:, 15:16], in0=rows2[:, 14:15],
                                  in1=rows2[:, 13:14], op=AL.mult)
            svv.inc(ins)
            v.wait_ge(svv.h, svv.n)
            ins = v.tensor_scalar(out=rows2[:, 16:17], in0=rows2[:, 15:16],
                                  scalar1=-1.0, scalar2=-1.5, op0=AL.mult, op1=AL.add)
            svv.inc(ins)
        negm_row = svv.n

        @block.tensor
        def _(tn):
            tn.wait_ge(svv.h, negm_row)
            tn.wait_ge(stp.h, stp.n)
            stp.inc(tn.matmul(PS[4][:, 0:1], ones1[0:1, :], rows2[:, 16:17], start=True, stop=True))
        negm_mm = stp.n

        @block.scalar
        def _(sc):
            sc.wait_ge(stp.h, negm_mm)
            ins = sc.copy(out=negm, in_=PS[4][:, 0:1])
            sqm.inc(ins)
        negm_done = sqm.n

        if aw.get("adbg") is not None and head == 0:
            @block.sync
            def _(s):
                s.wait_ge(sqm.h, negm_done)
                s.wait_ge(kld.h, kld_h0)
                sto.inc(s.dma_start(out=aw["adbg"][0:128, 0:2048], in_=qt), 16)
                sto.inc(s.dma_start(out=aw["adbg"][0:128, 2048:4096], in_=kt(0)[:, 0:2048]), 16)
                sto.inc(s.dma_start(out=aw["adbg"][0:1, 6144:6145], in_=negm[0:1, 0:1]), 16)

        @block.sync
        def _(s):
            s.wait_ge(sav.h, av_head0 + 4 * 16)
            assemble_kv(s, head, 1, 1)
        kld_h1 = kld.n

        sc_head0 = ssc.n

        @block.tensor
        def _(tn):
            for tb in range(64):
                th = tb // 32
                tbl = tb % 32
                if tb == 0:
                    tn.wait_ge(kld.h, kld_q)
                if tb == 32:
                    tn.wait_ge(kld.h, kld_h1)
                if tb == 0:
                    tn.wait_ge(sqm.h, negm_done)
                if tb >= 1:
                    tn.wait_ge(sxp.h, xp_head0 + 4 * tb)
                if tb >= 2:
                    tn.wait_ge(sav.h, av_head0 + 4 * (tb - 1))
                kslice = kt(th)[:, tbl * 128:(tbl + 1) * 128]
                for sb_i in range(4):
                    ssc.inc(tn.matmul(PS[sb_i][:, 0:512], kslice,
                                      qt[:, sb_i * 512:(sb_i + 1) * 512], start=True, stop=True))
                if tb >= 1:
                    pth = (tb - 1) // 32
                    vtile = vta(pth)[:, ((tb - 1) % 32) * 65: ((tb - 1) % 32) * 65 + 65]
                    for sb_i in range(4):
                        sav.inc(tn.matmul(PS[4 + sb_i][0:65, 0:512], vtile,
                                          pt((tb - 1) % 2)[:, sb_i * 512:(sb_i + 1) * 512],
                                          start=(tb - 1 == 0), stop=(tb - 1 == 63)))

        @block.scalar
        def _(sc):
            sc.wait_ge(sqm.h, negm_done)
            for tb in range(64):
                if tb >= 2:
                    sc.wait_ge(sav.h, av_head0 + 4 * (tb - 1))
                sc.wait_ge(ssc.h, sc_head0 + 4 * tb + 4)
                sc.activation(out=pt(tb % 2)[:, 0:1024], in_=t["psq"][:, 0:1024],
                              func=AF.Exp, bias=negm[:, 0:1])
                sxp.inc(sc.activation(out=pt(tb % 2)[:, 1024:2048], in_=t["psq"][:, 1024:2048],
                                      func=AF.Exp, bias=negm[:, 0:1]), 4)

        @block.tensor
        def _(tn):
            tn.wait_ge(sxp.h, sxp.n)
            vtile = vta(1)[:, 31 * 65: 31 * 65 + 65]
            for sb_i in range(4):
                sav.inc(tn.matmul(PS[4 + sb_i][0:65, 0:512], vtile,
                                  pt(63 % 2)[:, sb_i * 512:(sb_i + 1) * 512],
                                  start=False, stop=True))
        av_done = sav.n

        @block.vector
        def _(v):
            v.wait_ge(sav.h, av_done)
            for sb_i in range(4):
                ins = v.tensor_copy(out=rz[64:65, sb_i * 512:(sb_i + 1) * 512],
                                    in_=PS[4 + sb_i][64:65, 0:512])
                svv.inc(ins)
        z_ev = svv.n

        @block.sync
        def _(s):
            s.wait_ge(svv.h, z_ev)
            kld.inc(s.dma_start(out=rz[0:1, :], in_=rz[64:65, :]), 16)
        z_mv = kld.n

        @block.vector
        def _(v):
            v.wait_ge(kld.h, z_mv)
            ins = v.reciprocal(out=rz[0:1, :], in_=rz[0:1, :])
            svv.inc(ins)
        z_rc = svv.n

        @block.tensor
        def _(tn):
            tn.wait_ge(svv.h, z_rc)
            tn.wait_ge(stp.h, stp.n)
            tn.wait_ge(sxp.h, sxp.n)
            for nbi in range(4):
                stp.inc(tn.matmul(PS[nbi][0:64, 0:512], ones1[0:1, 0:64],
                                  rz[0:1, nbi * 512:(nbi + 1) * 512], start=True, stop=True))
        rzb_mm = stp.n - 4

        @block.scalar
        def _(sc):
            for nbi in range(4):
                sc.wait_ge(stp.h, rzb_mm + nbi + 1)
                ins = sc.copy(out=rz[0:64, nbi * 512:(nbi + 1) * 512], in_=PS[nbi][0:64, 0:512])
            svv.inc(ins)
        rzb_done = svv.n

        @block.vector
        def _(v):
            v.wait_ge(svv.h, rzb_done)
            for sb_i in range(4):
                ins = v.tensor_tensor(out=ahead[0:64, sb_i * 512:(sb_i + 1) * 512],
                                      in0=PS[4 + sb_i][0:64, 0:512],
                                      in1=rz[0:64, sb_i * 512:(sb_i + 1) * 512], op=AL.mult)
                soe.inc(ins)
        ah_done = soe.n

        @block.sync
        def _(s):
            s.wait_ge(soe.h, ah_done)
            kld.inc(s.dma_start(out=apair[hh * 64:(hh + 1) * 64, :], in_=ahead[0:64, :]), 16)
        ap_mv = kld.n

        if aw.get("adbg") is not None and head == 1:
            @block.sync
            def _(s):
                s.wait_ge(kld.h, ap_mv)
                sto.inc(s.dma_start(out=aw["adbg"][0:128, 4096:6144], in_=apair), 16)

        if hh == 1:
            @block.sync
            def _(s):
                s.wait_ge(smm.h, smm.n)
                ld_w.inc(s.dma_start(
                    out=wob.rearrange("p (a c) -> p a c", a=8),
                    in_=aw["wot"][hp * 128:(hp + 1) * 128, :].rearrange("p (a c) -> p a c", a=8)), 16)
            wo_ld = ld_w.n

            oeb = soe.n
            @block.tensor
            def _(tn):
                tn.wait_ge(ld_w.h, wo_ld)
                tn.wait_ge(kld.h, ap_mv)
                tn.wait_ge(stp.h, stp.n)
                tn.wait_ge(soe.h, oeb)
                for ic in range(NJ):
                    if ic >= 1:
                        tn.wait_ge(soe.h, oeb + 4 * ic)
                    for nbi in range(4):
                        smm.inc(tn.matmul(PS[nbi][:, 0:512], wob[:, ic * 128:(ic + 1) * 128],
                                          apair[:, nbi * 512:(nbi + 1) * 512],
                                          start=True, stop=True))
            ob = smm.n - 32

            @block.vector
            def _(v):
                cnt = 0
                for ic in range(NJ):
                    for nbi in range(4):
                        cnt += 1
                        v.wait_ge(smm.h, ob + cnt)
                        ins = v.tensor_tensor(out=oacc(ic)[:, nbi * 512:(nbi + 1) * 512],
                                              in0=oacc(ic)[:, nbi * 512:(nbi + 1) * 512],
                                              in1=PS[nbi][:, 0:512], op=AL.add)
                        soe.inc(ins)

    b.fence()

    smm2 = b.sem("a_fm")
    for ic in range(NJ):
        @block.sync
        def _(s):
            s.wait_ge(smm2.h, smm2.n)
            s.wait_ge(sto.h, sto.n)
            kld.inc(s.dma_start(out=qw[:, 0:2048], in_=src_d[ic * 128:(ic + 1) * 128, :]), 16)
        x_ld = kld.n

        @block.scalar
        def _(sc):
            sc.wait_ge(kld.h, x_ld)
            ins = sc.activation(out=ahead[:, 0:2048], in_=oacc(ic), func=AF.Identity,
                                bias=biases[:, 24 + ic: 25 + ic])
            svv.inc(ins)
        o_b = svv.n

        @block.vector
        def _(v):
            v.wait_ge(svv.h, o_b)
            v.wait_ge(kld.h, x_ld)
            ins = v.tensor_tensor(out=qw[:, 0:2048], in0=qw[:, 0:2048], in1=ahead[:, 0:2048], op=AL.add)
            smm2.inc(ins)
        x3_done = smm2.n

        @block.sync
        def _(s):
            s.wait_ge(smm2.h, x3_done)
            sto.inc(s.dma_start(out=dst_d[ic * 128:(ic + 1) * 128, :], in_=qw[:, 0:2048]), 16)

    @block.sync
    def _(s):
        s.wait_ge(sto.h, sto.n)
    b.fence()



def build_program(stage="full"):
    nc = bass.Bass("TRN2", target_bir_lowering=False, debug=False)
    dt = lambda n, shp, k: nc.dram_tensor(n, shp, F32, kind=k).ap()
    x_t = dt("x_t", [E, T], "ExternalInput")
    y_t = dt("y_t", [E, T], "ExternalOutput")
    dbg_t = dt("dbg_t", [128, 8192], "ExternalOutput") if stage == "ffn1dbg" else None
    adbg_t = dt("adbg_t", [128, 12288], "ExternalOutput") if stage == "x3dbg" else None

    w = {}
    w["w1"] = [dt(f"w1_{e}", [E, E], "ExternalInput") for e in range(NEXP)]
    w["w2"] = [dt(f"w2_{e}", [E, E], "ExternalInput") for e in range(NEXP)]
    w["sgu"] = dt("sgu", [E, 2 * E], "ExternalInput")
    w["sd"] = dt("sd", [E, E], "ExternalInput")
    aw = {}
    aw["wqt"] = dt("wqt", [E, E], "ExternalInput")
    aw["wkt"] = dt("wkt", [E, E], "ExternalInput")
    aw["wvt"] = dt("wvt", [E, E], "ExternalInput")
    aw["wot"] = dt("wot", [E, E], "ExternalInput")
    aw["whelp"] = dt("whelp", [8 * 128, 128], "ExternalInput")
    cin = {}
    for name, shp in [("ident", [128, 128]), ("ones128", [128, 1]), ("ones1x128", [1, 128]),
                      ("half1x128", [1, 128]), ("pairmat2", [128, 64]), ("expmat2", [64, 128]),
                      ("selmat", [128, 2]), ("qmask", [128, 4]), ("rw", [128, 64]),
                      ("sgw", [128, 8]), ("biases", [128, 32]), ("bhelp", [128, 8])]:
        cin[name] = dt("c_" + name, shp, "ExternalInput")

    x1_d = dt("x1_d", [E, T], "Internal")
    q_d = dt("q_d", [E, T], "Internal")
    x3_d = dt("x3_d", [E, T], "Internal")
    aw["q_d"] = q_d
    aw["agin"] = [dt(f"agin{g}", [SHARD], "Internal") for g in range(4)]
    aw["agout"] = [dt(f"agout{g}", [4 * SHARD], "Internal") for g in range(4)]

    es = contextlib.ExitStack()
    with es:
        ARENA_W = 52272
        arena = es.enter_context(nc.sbuf_tensor("arena", [128, ARENA_W], F32))
        sb = {}
        for name, shp in [("ident", [128, 128]), ("ones128", [128, 1]), ("ones1x128", [1, 128]),
                          ("half1x128", [1, 128]), ("pairmat2", [128, 64]), ("expmat2", [64, 128]),
                          ("selmat", [128, 2]), ("qmask", [128, 4]), ("rw_sb", [128, 64]),
                          ("sgw_sb", [128, 8]), ("biases_sb", [128, 32]), ("bhelp_sb", [128, 8])]:
            sb[name] = es.enter_context(nc.sbuf_tensor("s_" + name, shp, F32))
        psq = es.enter_context(nc.psum_tensor("psq", [128, 2048], F32))
        psa = es.enter_context(nc.psum_tensor("psa", [128, 2048], F32))
        psum = [(psq if i < 4 else psa)[:, (i % 4) * 512: (i % 4 + 1) * 512] for i in range(8)]
        block = es.enter_context(nc.Block())
        b = B(nc, block, es)

        t = dict(arena=arena, psum=psum, psq=psq, psa=psa)
        for k in sb:
            t[k] = sb[k]

        fmap = {}
        off = 0
        for name, width in [("XH", 8192), ("N1", 8192), ("HB", 8192), ("RACC", 8192),
                            ("CBT", 1024), ("SIGB", 1024), ("GS", 1024), ("GS2", 1024), ("TMP", 1024),
                            ("LG", 1024), ("LGT", 8), ("CBM", 1024), ("CBM2", 1024), ("SIGROW", 1024),
                            ("SML", 40), ("MUBS", 1024), ("RBS", 1024), ("ROWS", 3072),
                            ("WB", 5120)]:
            fmap[name] = off
            off += width
        assert off <= ARENA_W, off
        t["fmap"] = fmap
        t["rowoff"] = fmap["ROWS"]
        t["mub"] = arena[:, fmap["MUBS"]: fmap["MUBS"] + 1024]
        t["rb"] = arena[:, fmap["RBS"]: fmap["RBS"] + 1024]

        ldc = b.sem("ldc0")

        @block.sync
        def _(s):
            for name, key, rowlim in [("ident", "ident", 128), ("ones128", "ones128", 128),
                                      ("ones1x128", "ones1x128", 1), ("half1x128", "half1x128", 1),
                                      ("pairmat2", "pairmat2", 128), ("expmat2", "expmat2", 64),
                                      ("selmat", "selmat", 128), ("qmask", "qmask", 128),
                                      ("biases", "biases_sb", 128), ("bhelp", "bhelp_sb", 128)]:
                ldc.inc(s.dma_start(out=sb[key][0:rowlim, :], in_=cin[name]), 16)
            ldc.inc(s.dma_start(out=sb["rw_sb"][:, :], in_=cin["rw"]), 16)
            ldc.inc(s.dma_start(out=sb["sgw_sb"][:, :], in_=cin["sgw"]), 16)

        for name in ("tensor", "vector", "scalar", "gpsimd"):
            def f(e):
                e.wait_ge(ldc.h, ldc.n)
            getattr(block, name)(f)

        if stage == "ffn1dbg":
            emit_ffn(b, t, x_t, y_t, w, dbg_d=dbg_t)
        else:
            emit_ffn(b, t, x_t, x1_d if stage != "ffn1" else y_t, w)
        if stage not in ("ffn1", "ffn1dbg"):
            aw["adbg"] = adbg_t if stage == "x3dbg" else None
            emit_attention(b, t, x1_d, x3_d if stage not in ("x3", "x3dbg") else y_t, aw)
            if stage != "x3":
                emit_ffn(b, t, x3_d, y_t, w)

        snap = [(s_.h, s_.n) for s_ in b._sems.values() if s_.n > 0]

        @block.sync
        def _(s):
            for h, n in snap:
                s.wait_ge(h, n)
    return nc



def _consts(quar):
    c = {}
    c["ident"] = np.eye(128, dtype=np.float32)
    c["ones128"] = np.ones((128, 1), dtype=np.float32)
    c["ones1x128"] = np.ones((1, 128), dtype=np.float32)
    c["half1x128"] = np.full((1, 128), 0.5, dtype=np.float32)
    pm = np.zeros((128, 64), dtype=np.float32)
    for u in range(128):
        if u < 64:
            pm[u, u % 32] = 1.0
        else:
            pm[u, 32 + (u - 64) % 32] = 1.0
    c["pairmat2"] = pm
    em = np.zeros((64, 128), dtype=np.float32)
    for u in range(128):
        if u < 64:
            em[u % 32, u] = 1.0
        else:
            em[32 + (u - 64) % 32, u] = 1.0
    c["expmat2"] = em
    sel = np.zeros((128, 2), dtype=np.float32)
    sel[0:64, 0] = 1.0
    sel[64:128, 1] = 1.0
    c["selmat"] = sel
    qm = np.zeros((128, 4), dtype=np.float32)
    qm[:, quar] = 1.0
    c["qmask"] = qm
    return c


def kernel(L0_attn_x, L0_attn_Wq, L0_attn_bq, L0_attn_Wk, L0_attn_bk, L0_attn_Wv,
           L0_attn_bv, L0_attn_Wo, L0_attn_bo, L0_attn_Whel, L0_attn_bhel,
           L0_ffn_router_w, L0_ffn_w1, L0_ffn_w2, L0_ffn_shared_gate_up,
           L0_ffn_shared_down, L0_ffn_shared_gate_w, _stage="full"):
    x = np.ascontiguousarray(np.asarray(L0_attn_x, dtype=np.float32)).reshape(2 * S, E)

    perm = np.array([2 * (ep % 32) + (ep // 32) for ep in range(64)], dtype=np.int64)
    whel = np.asarray(L0_attn_Whel, dtype=np.float32)
    whelT = whel.transpose(0, 2, 1)[:, :, perm]
    whelp = np.zeros((8 * 128, 128), dtype=np.float32)
    for hp in range(8):
        whelp[hp * 128: hp * 128 + 64, 0:64] = whelT[2 * hp]
        whelp[hp * 128 + 64: hp * 128 + 128, 64:128] = whelT[2 * hp + 1]
    bhel = np.asarray(L0_attn_bhel, dtype=np.float32)[:, perm]
    bhelp = np.zeros((128, 8), dtype=np.float32)
    for hp in range(8):
        bhelp[0:64, hp] = bhel[2 * hp]
        bhelp[64:128, hp] = bhel[2 * hp + 1]

    biases = np.zeros((128, 32), dtype=np.float32)
    for hp in range(8):
        biases[:, hp] = np.asarray(L0_attn_bq, dtype=np.float32)[hp * 128:(hp + 1) * 128]
        biases[:, 8 + hp] = np.asarray(L0_attn_bk, dtype=np.float32)[hp * 128:(hp + 1) * 128]
        biases[:, 16 + hp] = np.asarray(L0_attn_bv, dtype=np.float32)[hp * 128:(hp + 1) * 128]
        biases[:, 24 + hp] = np.asarray(L0_attn_bo, dtype=np.float32)[hp * 128:(hp + 1) * 128]

    shared = {
        "sgu": np.ascontiguousarray(np.asarray(L0_ffn_shared_gate_up, dtype=np.float32)),
        "sd": np.ascontiguousarray(np.asarray(L0_ffn_shared_down, dtype=np.float32)),
        "wqt": np.ascontiguousarray(np.asarray(L0_attn_Wq, dtype=np.float32).T),
        "wkt": np.ascontiguousarray(np.asarray(L0_attn_Wk, dtype=np.float32).T),
        "wvt": np.ascontiguousarray(np.asarray(L0_attn_Wv, dtype=np.float32).T),
        "wot": np.ascontiguousarray(np.asarray(L0_attn_Wo, dtype=np.float32).T),
        "whelp": whelp,
        "c_rw": np.ascontiguousarray(np.asarray(L0_ffn_router_w, dtype=np.float32).reshape(8, 128, 8).transpose(1, 0, 2).reshape(128, 64)),
        "c_sgw": np.ascontiguousarray(np.asarray(L0_ffn_shared_gate_w, dtype=np.float32).reshape(8, 128).T),
        "c_biases": biases,
        "c_bhelp": bhelp,
    }
    w1 = np.asarray(L0_ffn_w1, dtype=np.float32)
    w2 = np.asarray(L0_ffn_w2, dtype=np.float32)
    for e in range(NEXP):
        shared[f"w1_{e}"] = np.ascontiguousarray(w1[e])
        shared[f"w2_{e}"] = np.ascontiguousarray(w2[e])

    nc = build_program(_stage)
    in_maps = []
    for c in range(8):
        quar = c % 4
        m = dict(shared)
        cc = _consts(quar)
        m["c_ident"] = cc["ident"]
        m["c_ones128"] = cc["ones128"]
        m["c_ones1x128"] = cc["ones1x128"]
        m["c_half1x128"] = cc["half1x128"]
        m["c_pairmat2"] = cc["pairmat2"]
        m["c_expmat2"] = cc["expmat2"]
        m["c_selmat"] = cc["selmat"]
        m["c_qmask"] = cc["qmask"]
        m["x_t"] = np.ascontiguousarray(x[c * T:(c + 1) * T, :].T)
        in_maps.append(m)

    res = run_bass_kernel_spmd(nc, in_maps, core_ids=list(range(8)))
    out = np.empty((2 * S, E), dtype=np.float32)
    for c in range(8):
        out[c * T:(c + 1) * T, :] = res.results[c]["y_t"].T
    if _stage == "ffn1dbg":
        return out.reshape(2, S, E), [r["dbg_t"] for r in res.results]
    if _stage == "x3dbg":
        return out.reshape(2, S, E), [r["adbg_t"] for r in res.results]
    return out.reshape(2, S, E)


# version 21 (final)
# speedup vs baseline: 1.1050x; 1.0700x over previous
import contextlib
import numpy as np
import concourse.bass as bass
import concourse.mybir as mybir
from concourse.bass_utils import run_bass_kernel_spmd

F32 = mybir.dt.float32
BF16 = mybir.dt.bfloat16
AF = mybir.ActivationFunctionType
AL = mybir.AluOpType
AX = mybir.AxisListType.X

E, T, S, H, HD = 1024, 2048, 8192, 16, 64
NJ = 8
NEXP = 8

OFF_KHT = 0
OFF_KN = 131072
OFF_QN = 262144
OFF_VB = 393216
OFF_ST = 526336
HE = 526400
SHARD = 4 * HE


class Sem:
    def __init__(self, nc, es, name):
        self.h = es.enter_context(nc.semaphore(name))
        self.n = 0

    def inc(self, instr, k=1):
        instr.then_inc(self.h, k)
        self.n += k
        return instr


class B:
    def __init__(self, nc, block, es):
        self.nc, self.block, self.es = nc, block, es
        self._sems = {}

    def sem(self, name):
        if name not in self._sems:
            self._sems[name] = Sem(self.nc, self.es, name)
        return self._sems[name]

    def fence(self, exclude=("a_cc",)):
        snap = [(s.h, s.n) for k, s in self._sems.items() if s.n > 0 and k not in exclude]
        for name in ("tensor", "vector", "scalar"):
            def f(e, snap=snap):
                for h, n in snap:
                    e.wait_ge(h, n)
            getattr(self.block, name)(f)



def emit_ln(b, t, src_jc, xh, sq, n1, nb):
    block = b.block
    PS = t["psum"]
    W = nb * 512
    ld = b.sem("ln_ld")
    ssq = b.sem("ln_sq")
    sst = b.sem("ln_st")
    srow = b.sem("ln_row")
    sbc = b.sem("ln_bc")
    snm = b.sem("ln_nm")
    ones128 = t["ones128"]
    ones1 = t["ones1x128"]
    A = t["arena"]
    R = t["rowoff"]
    murow = A[0:1, R: R + W]
    varow = A[0:1, R + 1024: R + 1024 + W]
    mu2 = A[0:1, R + 2048: R + 2048 + W]
    mub = t["mub"][:, 0:W]
    rb = t["rb"][:, 0:W]

    @block.sync
    def _(s):
        for nm in ("f_to", "a_to"):
            ss = b.sem(nm)
            s.wait_ge(ss.h, ss.n)
        s.wait_ge(snm.h, snm.n)
        s.wait_ge(sst.h, sst.n)
        for jc in range(NJ):
            ld.inc(s.dma_start(out=xh(jc), in_=src_jc(jc)), 16)
    ld_goal = ld.n

    @block.scalar
    def _(sc):
        sc.wait_ge(ld.h, ld_goal)
        for jc in range(NJ):
            ins = sc.activation(out=sq(jc), in_=xh(jc), func=AF.Square)
        ssq.inc(ins)
    ssq_goal = ssq.n

    @block.tensor
    def _(tn):
        tn.wait_ge(ld.h, ld_goal)
        for nbi in range(nb):
            for jc in range(NJ):
                ins = tn.matmul(PS[nbi % 2][0:1, 0:512], ones128[:, 0:1],
                                xh(jc)[:, nbi * 512:(nbi + 1) * 512],
                                start=(jc == 0), stop=(jc == NJ - 1))
            sst.inc(ins)
        tn.wait_ge(ssq.h, ssq_goal)
        for nbi in range(nb):
            for jc in range(NJ):
                ins = tn.matmul(PS[2 + nbi % 2][0:1, 0:512], ones128[:, 0:1],
                                sq(jc)[:, nbi * 512:(nbi + 1) * 512],
                                start=(jc == 0), stop=(jc == NJ - 1))
            sst.inc(ins)
    base = sst.n - 2 * nb

    @block.vector
    def _(v):
        for nbi in range(nb):
            v.wait_ge(sst.h, base + nbi + 1)
            v.tensor_scalar(out=murow[:, nbi * 512:(nbi + 1) * 512],
                            in0=PS[nbi % 2][0:1, 0:512], scalar1=1.0 / E,
                            scalar2=None, op0=AL.mult)
        for nbi in range(nb):
            v.wait_ge(sst.h, base + nb + nbi + 1)
            ins = v.tensor_scalar(out=varow[:, nbi * 512:(nbi + 1) * 512],
                                  in0=PS[2 + nbi % 2][0:1, 0:512], scalar1=1.0 / E,
                                  scalar2=None, op0=AL.mult)
        srow.inc(ins)
        v.wait_ge(srow.h, srow.n)
        v.tensor_tensor(out=mu2, in0=murow, in1=murow, op=AL.mult)
        ins = v.tensor_tensor(out=varow, in0=varow, in1=mu2, op=AL.subtract)
        srow.inc(ins)
        v.wait_ge(srow.h, srow.n)
        ins = v.tensor_scalar(out=varow, in0=varow, scalar1=1e-5, scalar2=None, op0=AL.add)
        srow.inc(ins)
        v.wait_ge(srow.h, srow.n)
        ins = v.reciprocal(out=varow, in_=varow)
        srow.inc(ins)
    srow_goal = srow.n

    @block.scalar
    def _(sc):
        sc.wait_ge(srow.h, srow_goal)
        ins = sc.activation(out=varow, in_=varow, func=AF.Sqrt)
        srow.inc(ins)
    srow_goal2 = srow.n

    @block.tensor
    def _(tn):
        tn.wait_ge(srow.h, srow_goal2)
        for nbi in range(nb):
            sbc.inc(tn.matmul(PS[4 + nbi % 2][:, 0:512], ones1[0:1, :],
                              murow[:, nbi * 512:(nbi + 1) * 512], start=True, stop=True))
            sbc.inc(tn.matmul(PS[6 + nbi % 2][:, 0:512], ones1[0:1, :],
                              varow[:, nbi * 512:(nbi + 1) * 512], start=True, stop=True))
    bbase = sbc.n - 2 * nb

    @block.scalar
    def _(sc):
        for nbi in range(nb):
            sc.wait_ge(sbc.h, bbase + 2 * nbi + 2)
            sc.copy(out=mub[:, nbi * 512:(nbi + 1) * 512], in_=PS[4 + nbi % 2][:, 0:512])
            ins = sc.copy(out=rb[:, nbi * 512:(nbi + 1) * 512], in_=PS[6 + nbi % 2][:, 0:512])
        sbc.inc(ins)
    sbc_goal = sbc.n

    @block.vector
    def _(v):
        v.wait_ge(sbc.h, sbc_goal)
        for jc in range(NJ):
            v.tensor_tensor(out=n1(jc), in0=xh(jc), in1=mub, op=AL.subtract)
        for jc in range(NJ):
            ins = v.tensor_tensor(out=n1(jc), in0=n1(jc), in1=rb, op=AL.mult)
        snm.inc(ins)



def emit_ffn(b, t, src_d, dst_d, w, dbg_d=None):
    block = b.block
    A = t["arena"]
    PS = t["psum"]
    M = t["fmap"]
    HT = 1024
    NB = 2
    ld_w = b.sem("f_w")
    ldh = b.sem("f_wh")
    swl = b.sem("f_wl")
    smm = b.sem("f_mm")
    sev = b.sem("f_ev")
    scb = b.sem("f_cb")
    sra = b.sem("f_ra")
    sgu = b.sem("f_gu")
    stp = b.sem("f_tp")
    svv = b.sem("f_vv")
    sso = b.sem("f_so")
    sto = b.sem("f_to")
    ones1 = t["ones1x128"]
    half1 = t["half1x128"]
    ident = t["ident"]

    xh = lambda jc: A[:, M["XH"] + jc * HT: M["XH"] + (jc + 1) * HT]
    n1 = lambda jc: A[:, M["N1"] + jc * HT: M["N1"] + (jc + 1) * HT]
    hb = lambda fc: A[:, M["HB"] + fc * HT: M["HB"] + (fc + 1) * HT]
    racc = lambda ic: A[:, M["RACC"] + ic * HT: M["RACC"] + (ic + 1) * HT]
    cbt = A[:, M["CBT"]: M["CBT"] + HT]
    sigb = A[:, M["SIGB"]: M["SIGB"] + HT]
    gs = A[:, M["GS"]: M["GS"] + HT]
    tmp = A[:, M["TMP"]: M["TMP"] + 512]
    lg = A[0:8, M["LG"]: M["LG"] + HT]
    lgt = A[:, M["LGT"]: M["LGT"] + 8]
    cbm = A[0:8, M["CBM"]: M["CBM"] + HT]
    cbm2 = A[0:1, M["CBM2"]: M["CBM2"] + HT]
    sigrow = A[0:1, M["ROWS"]: M["ROWS"] + HT]
    m1c = A[:, M["SML"]: M["SML"] + 1]
    m2c = A[:, M["SML"] + 1: M["SML"] + 2]
    eqt = A[:, M["SML"] + 8: M["SML"] + 16]
    mskt = A[:, M["SML"] + 16: M["SML"] + 24]
    smet = A[:, M["SML"] + 24: M["SML"] + 32]
    smes = A[:, M["SML"] + 32: M["SML"] + 33]
    wb = lambda p: A[:, M["WB"] + p * 1024: M["WB"] + (p + 1) * 1024]

    @block.sync
    def _(s):
        for nm in ("a_to", "a_fm", "a_oe"):
            ss = b.sem(nm)
            s.wait_ge(ss.h, ss.n)

    for hh in range(2):
        c0 = hh * HT
        b.fence()
        emit_ln(b, t, lambda jc, c0=c0: src_d[jc * 128:(jc + 1) * 128, c0:c0 + HT],
                xh, hb, n1, NB)
        b.fence()

        @block.tensor
        def _(tn):
            for nbi in range(NB):
                for jc in range(NJ):
                    ins = tn.matmul(PS[nbi][0:8, 0:512], t["rw_sb"][:, jc * 8:(jc + 1) * 8],
                                    n1(jc)[:, nbi * 512:(nbi + 1) * 512],
                                    start=(jc == 0), stop=(jc == NJ - 1))
                smm.inc(ins)
        mbase = smm.n - NB

        @block.scalar
        def _(sc):
            for nbi in range(NB):
                sc.wait_ge(smm.h, mbase + nbi + 1)
                ins = sc.copy(out=lg[:, nbi * 512:(nbi + 1) * 512], in_=PS[nbi][0:8, 0:512])
            sev.inc(ins)
        lg_done = sev.n

        for tt in range(8):
            @block.tensor
            def _(tn):
                tn.wait_ge(sev.h, lg_done)
                tn.wait_ge(stp.h, stp.n)
                tn.wait_ge(scb.h, scb.n)
                stp.inc(tn.transpose(PS[4][0:128, 0:8], lg[:, tt * 128:(tt + 1) * 128],
                                     ident[0:8, 0:8]))
            tp1 = stp.n

            @block.vector
            def _(v):
                v.wait_ge(stp.h, tp1)
                ins = v.tensor_copy(out=lgt, in_=PS[4][0:128, 0:8])
                stp.inc(ins)
                v.wait_ge(stp.h, stp.n)
                ins = v.reduce_max(m1c, lgt, axis=AX)
                svv.inc(ins)
            g1 = svv.n

            @block.scalar
            def _(sc):
                sc.wait_ge(svv.h, g1)
                ins = sc.mul(m2c, m1c, -1.0)
                svv.inc(ins)
                sc.wait_ge(svv.h, svv.n)
                ins = sc.activation(out=smet, in_=lgt, func=AF.Exp, bias=m2c[:, 0:1])
                svv.inc(ins)
            g2 = svv.n

            @block.vector
            def _(v):
                v.wait_ge(svv.h, g2)
                ins = v.reduce_sum(smes, smet, axis=AX)
                svv.inc(ins)
                v.wait_ge(svv.h, svv.n)
                ins = v.reciprocal(out=smes, in_=smes)
                svv.inc(ins)
                v.wait_ge(svv.h, svv.n)
                ins = v.tensor_scalar(out=smet, in0=smet, scalar1=smes[:, 0:1], scalar2=None, op0=AL.mult)
                svv.inc(ins)
                v.wait_ge(svv.h, svv.n)
                ins = v.reduce_max(m1c, smet, axis=AX)
                svv.inc(ins)
                v.wait_ge(svv.h, svv.n)
                ins = v.tensor_scalar(out=eqt, in0=smet, scalar1=m1c[:, 0:1], scalar2=None, op0=AL.is_equal)
                svv.inc(ins)
                v.wait_ge(svv.h, svv.n)
                ins = v.scalar_tensor_tensor(out=mskt, in0=eqt, scalar=-1e9, in1=smet, op0=AL.mult, op1=AL.add)
                svv.inc(ins)
                v.wait_ge(svv.h, svv.n)
                ins = v.reduce_max(m2c, mskt, axis=AX)
                svv.inc(ins)
                v.wait_ge(svv.h, svv.n)
                ins = v.tensor_scalar(out=eqt, in0=smet, scalar1=m2c[:, 0:1], scalar2=None, op0=AL.is_ge)
                svv.inc(ins)
                v.wait_ge(svv.h, svv.n)
                ins = v.tensor_tensor(out=lgt, in0=smet, in1=eqt, op=AL.mult)
                svv.inc(ins)
            g3 = svv.n

            @block.tensor
            def _(tn):
                tn.wait_ge(svv.h, g3)
                tn.wait_ge(stp.h, stp.n)
                stp.inc(tn.transpose(PS[4][0:8, 0:128], lgt, ident[0:128, 0:128]))
            tp2 = stp.n

            @block.scalar
            def _(sc):
                sc.wait_ge(stp.h, tp2)
                ins = sc.copy(out=cbm[:, tt * 128:(tt + 1) * 128], in_=PS[4][0:8, 0:128])
                scb.inc(ins)
        cbm_ready = scb.n
        if dbg_d is not None and hh == 0:
            snm_now = b.sem("ln_nm").n
            @block.sync
            def _(s):
                s.wait_ge(b.sem("ln_nm").h, snm_now)
                s.wait_ge(scb.h, cbm_ready)
                sto.inc(s.dma_start(out=dbg_d[0:128, 0:1024], in_=n1(0)), 16)
                sto.inc(s.dma_start(out=dbg_d[0:8, 1024:2048], in_=cbm), 16)
        b.fence()

        AB = A[:, :].bitcast(BF16)
        b16 = lambda c0, w: AB[:, 2 * c0: 2 * c0 + 2 * w]
        n1hi = lambda jc: b16(M["XH"] + jc * 512, 512)
        n1lo = lambda jc: b16(M["XH"] + 4096 + jc * 512, 512)
        hshi = lambda fc: b16(M["HB"] + fc * 512, 512)
        hslo = lambda fc: b16(M["HB"] + 4096 + fc * 512, 512)
        wstage = lambda k: A[:, M["WB"] + k * 1024: M["WB"] + (k + 1) * 1024]
        whi1 = lambda p: b16(M["WB"] + 2048 + p * 512, 512)
        wlo1 = lambda p: b16(M["WB"] + 4096 + p * 512, 512)
        stemp = lambda fc: A[:, (M["TMP"] if fc % 2 == 0 else M["GS"]): (M["TMP"] if fc % 2 == 0 else M["GS"]) + 1024]
        ptemp = A[:, M["GS2"]: M["GS2"] + 1024]
        shs = b.sem("f_hs")

        snm2 = b.sem("ln_nm")
        n1r = snm2.n
        @block.vector
        def _(v):
            v.wait_ge(snm2.h, n1r)
            for jc in range(NJ):
                v.tensor_copy(out=n1hi(jc), in_=n1(jc))
            for jc in range(NJ):
                ins = v.tensor_tensor(out=n1lo(jc), in0=n1(jc), in1=n1hi(jc), op=AL.subtract)
            shs.inc(ins)
        n1split = shs.n

        scm = b.sem("f_cm")
        for e in range(NEXP):
            @block.sync
            def _(s):
                s.wait_ge(scb.h, cbm_ready)
                s.wait_ge(stp.h, stp.n)
                scm.inc(s.dma_start(out=cbm2, in_=cbm[e:e + 1, :]), 16)
            cb_ld = scm.n

            @block.tensor
            def _(tn):
                tn.wait_ge(scm.h, cb_ld)
                tn.wait_ge(scb.h, scb.n)
                tn.wait_ge(sev.h, sev.n)
                tn.wait_ge(sgu.h, sgu.n)
                for nbi in range(NB):
                    stp.inc(tn.matmul(PS[2 + nbi][:, 0:512], ones1[0:1, :],
                                      cbm2[0:1, nbi * 512:(nbi + 1) * 512],
                                      start=True, stop=True))
            cb_mm = stp.n

            @block.scalar
            def _(sc):
                sc.wait_ge(stp.h, cb_mm)
                sc.wait_ge(smm.h, smm.n)
                for nbi in range(NB):
                    ins = sc.copy(out=cbt[:, nbi * 512:(nbi + 1) * 512], in_=PS[2 + nbi][:, 0:512])
                scb.inc(ins)
            cbt_ready = scb.n

            sev_at = {}
            sev_base = sev.n
            sgu_base = sgu.n
            for k in range(4):
                sev_at[2 * k + 1] = sev_base + k + 1
            hbase_at = {}

            def emit_hs(fco):
                @block.vector
                def _(v, fco=fco):
                    v.wait_ge(sev.h, sev_at[fco])
                    v.wait_ge(scb.h, cbt_ready)
                    for f2 in (fco - 1, fco):
                        v.tensor_tensor(out=ptemp, in0=stemp(f2), in1=cbt, op=AL.mult)
                        sgu.inc(v.tensor_copy(out=hshi(f2), in_=ptemp))
                        ins = v.tensor_tensor(out=hslo(f2), in0=ptemp, in1=hshi(f2), op=AL.subtract)
                        shs.inc(ins)
            for fc in range(NJ):
                wpar = (e * NJ + fc) % 2
                pbank = (fc % 2) * 2
                @block.sync
                def _(s):
                    s.wait_ge(swl.h, max(0, swl.n - 1))
                    ld_w.inc(s.dma_start(
                        out=wstage(fc % 2).rearrange("p (a c) -> p a c", a=8),
                        in_=w["w1"][e].rearrange("(jc p) c -> p jc c", p=128)[:, :, fc * 128:(fc + 1) * 128]), 16)
                w1_ld = ld_w.n

                @block.gpsimd
                def _(g):
                    if e * NJ + fc >= 2:
                        g.wait_ge(smm.h, smm.n - NB)
                    else:
                        for nm in ("a_to", "a_vv", "a_oe"):
                            ss = b.sem(nm)
                            g.wait_ge(ss.h, ss.n)
                        g.wait_ge(smm.h, smm.n)
                    ldh.inc(g.dma_start(
                        out=whi1(wpar).rearrange("p (a c) -> p a c", a=8),
                        in_=w["w1"][e].rearrange("(jc p) c -> p jc c", p=128)[:, :, fc * 128:(fc + 1) * 128]), 16)
                w1_hi = ldh.n

                @block.vector
                def _(v):
                    v.wait_ge(ld_w.h, w1_ld)
                    v.wait_ge(ldh.h, w1_hi)
                    if e * NJ + fc >= 2:
                        v.wait_ge(smm.h, smm.n - NB)
                    ins = v.tensor_tensor(out=wlo1(wpar), in0=wstage(fc % 2), in1=whi1(wpar), op=AL.subtract)
                    swl.inc(ins)
                w1_split = swl.n

                @block.tensor
                def _(tn):
                    tn.wait_ge(swl.h, w1_split)
                    if fc >= 2:
                        tn.wait_ge(sev.h, sev_at[fc - 1] if fc % 2 == 0 else sev_at[fc - 2])
                    else:
                        tn.wait_ge(sev.h, sev.n)
                        tn.wait_ge(scb.h, cbt_ready)
                        tn.wait_ge(sra.h, sra.n)
                    for nbi in range(NB):
                        for jc in range(NJ):
                            ins = tn.matmul(PS[pbank + nbi][:, 0:512], whi1(wpar)[:, jc * 128:(jc + 1) * 128],
                                            n1hi(jc)[:, nbi * 512:(nbi + 1) * 512],
                                            start=(jc == 0), stop=False)
                            ins = tn.matmul(PS[pbank + nbi][:, 0:512], whi1(wpar)[:, jc * 128:(jc + 1) * 128],
                                            n1lo(jc)[:, nbi * 512:(nbi + 1) * 512],
                                            start=False, stop=False)
                            ins = tn.matmul(PS[pbank + nbi][:, 0:512], wlo1(wpar)[:, jc * 128:(jc + 1) * 128],
                                            n1hi(jc)[:, nbi * 512:(nbi + 1) * 512],
                                            start=False, stop=(jc == NJ - 1))
                        smm.inc(ins)
                hbase = smm.n - NB

                if fc % 2 == 1:
                    hbase_at[fc] = hbase
                if fc >= 2 and fc % 2 == 0:
                    emit_hs(fc - 1)

            w2_split_at = {}

            def w2_prep(ic):
                wpar = (e * NJ + ic) % 2

                @block.sync
                def _(s, ic=ic):
                    s.wait_ge(swl.h, max(0, swl.n - 1))
                    ld_w.inc(s.dma_start(
                        out=wstage(ic % 2).rearrange("p (a c) -> p a c", a=8),
                        in_=w["w2"][e].rearrange("(fc p) c -> p fc c", p=128)[:, :, ic * 128:(ic + 1) * 128]), 16)
                w2_ld = ld_w.n

                @block.gpsimd
                def _(g, ic=ic, wpar=wpar):
                    if e * NJ + ic >= 2:
                        g.wait_ge(smm.h, smm.n - NB)
                    ldh.inc(g.dma_start(
                        out=whi1(2 + wpar).rearrange("p (a c) -> p a c", a=8),
                        in_=w["w2"][e].rearrange("(fc p) c -> p fc c", p=128)[:, :, ic * 128:(ic + 1) * 128]), 16)
                w2_hi = ldh.n

                @block.vector
                def _(v, ic=ic, wpar=wpar, w2_ld=w2_ld, w2_hi=w2_hi):
                    v.wait_ge(ld_w.h, w2_ld)
                    v.wait_ge(ldh.h, w2_hi)
                    if e * NJ + ic >= 2:
                        v.wait_ge(smm.h, smm.n - NB)
                    ins = v.tensor_tensor(out=wlo1(2 + wpar), in0=wstage(ic % 2), in1=whi1(2 + wpar), op=AL.subtract)
                    swl.inc(ins)
                w2_split_at[ic] = swl.n

            w2_prep(0)
            emit_hs(7)

            @block.scalar
            def _(sc):
                for k in range(4):
                    fc = 2 * k + 1
                    sc.wait_ge(sgu.h, sgu_base + 2 * (k - 1) + 2 if k >= 1 else sgu_base)
                    sc.wait_ge(smm.h, hbase_at[fc] + NB)
                    for f2 in (fc - 1, fc):
                        pb2 = (f2 % 2) * 2
                        for nbi in range(NB):
                            ins = sc.activation(out=stemp(f2)[:, nbi * 512:(nbi + 1) * 512],
                                                in_=PS[pb2 + nbi][:, 0:512], func=AF.Silu)
                    sev.inc(ins)
            assert sev.n == sev_base + 4
            h_scaled = shs.n

            sra_at = {}
            for ic in range(NJ):
                wpar = (e * NJ + ic) % 2
                pbank = 4

                @block.tensor
                def _(tn):
                    tn.wait_ge(swl.h, w2_split_at[ic])
                    tn.wait_ge(shs.h, h_scaled)
                    tn.wait_ge(sev.h, sev.n)
                    if ic >= 1:
                        tn.wait_ge(sra.h, sra_at[ic - 1])
                    else:
                        tn.wait_ge(sra.h, sra.n)
                    for nbi in range(NB):
                        for fc in range(NJ):
                            ins = tn.matmul(PS[pbank + nbi][:, 0:512], whi1(2 + wpar)[:, fc * 128:(fc + 1) * 128],
                                            hshi(fc)[:, nbi * 512:(nbi + 1) * 512],
                                            start=(fc == 0), stop=False)
                            ins = tn.matmul(PS[pbank + nbi][:, 0:512], whi1(2 + wpar)[:, fc * 128:(fc + 1) * 128],
                                            hslo(fc)[:, nbi * 512:(nbi + 1) * 512],
                                            start=False, stop=False)
                            ins = tn.matmul(PS[pbank + nbi][:, 0:512], wlo1(2 + wpar)[:, fc * 128:(fc + 1) * 128],
                                            hshi(fc)[:, nbi * 512:(nbi + 1) * 512],
                                            start=False, stop=(fc == NJ - 1))
                        smm.inc(ins)
                rbase = smm.n - NB
                if ic + 1 < NJ:
                    w2_prep(ic + 1)

                @block.vector
                def _(v):
                    for nbi in range(NB):
                        v.wait_ge(smm.h, rbase + nbi + 1)
                        sl = slice(nbi * 512, (nbi + 1) * 512)
                        if e == 0:
                            ins = v.tensor_copy(out=racc(ic)[:, sl], in_=PS[pbank + nbi][:, 0:512])
                        else:
                            ins = v.tensor_tensor(out=racc(ic)[:, sl], in0=racc(ic)[:, sl],
                                                  in1=PS[pbank + nbi][:, 0:512], op=AL.add)
                    sra.inc(ins)
                sra_at[ic] = sra.n
        @block.tensor
        def _(tn):
            for nbi in range(NB):
                for jc in range(NJ):
                    ins = tn.matmul(PS[6 + nbi][0:1, 0:512], t["sgw_sb"][:, jc:jc + 1],
                                    n1(jc)[:, nbi * 512:(nbi + 1) * 512],
                                    start=(jc == 0), stop=(jc == NJ - 1))
                smm.inc(ins)
        sgbase = smm.n - NB

        @block.scalar
        def _(sc):
            for nbi in range(NB):
                sc.wait_ge(smm.h, sgbase + nbi + 1)
                ins = sc.activation(out=sigrow[:, nbi * 512:(nbi + 1) * 512],
                                    in_=PS[6 + nbi][0:1, 0:512], func=AF.Sigmoid)
            sev.inc(ins)
        sig_done = sev.n

        @block.tensor
        def _(tn):
            tn.wait_ge(sev.h, sig_done)
            for nbi in range(NB):
                stp.inc(tn.matmul(PS[2 + nbi][:, 0:512], half1[0:1, :],
                                  sigrow[:, nbi * 512:(nbi + 1) * 512], start=True, stop=True))
        sgb2 = stp.n

        @block.scalar
        def _(sc):
            sc.wait_ge(stp.h, sgb2)
            for nbi in range(NB):
                ins = sc.copy(out=sigb[:, nbi * 512:(nbi + 1) * 512], in_=PS[2 + nbi][:, 0:512])
            sev.inc(ins)
        b.fence()

        guphi = lambda gp: b16(M["HB"] + gp * 512, 512)
        guplo = lambda gp: b16(M["HB"] + 4096 + gp * 512, 512)
        gutmp = A[:, M["TMP"]: M["TMP"] + 1024]
        gu_done_at = {}
        for gp in range(8):
            for part in range(2):
                gc = gp + 8 * part
                @block.sync
                def _(s):
                    s.wait_ge(swl.h, max(0, swl.n - 1))
                    ld_w.inc(s.dma_start(
                        out=wstage(part).rearrange("p (a c) -> p a c", a=8),
                        in_=w["sgu"].rearrange("(jc p) c -> p jc c", p=128)[:, :, gc * 128:(gc + 1) * 128]), 16)
                g_ld = ld_w.n

                @block.gpsimd
                def _(g):
                    g.wait_ge(smm.h, smm.n)
                    ldh.inc(g.dma_start(
                        out=whi1(part).rearrange("p (a c) -> p a c", a=8),
                        in_=w["sgu"].rearrange("(jc p) c -> p jc c", p=128)[:, :, gc * 128:(gc + 1) * 128]), 16)
                g_hi = ldh.n

                @block.vector
                def _(v):
                    v.wait_ge(ld_w.h, g_ld)
                    v.wait_ge(ldh.h, g_hi)
                    v.wait_ge(smm.h, smm.n)
                    ins = v.tensor_tensor(out=wlo1(part), in0=wstage(part), in1=whi1(part), op=AL.subtract)
                    swl.inc(ins)
                g_split = swl.n

                @block.tensor
                def _(tn):
                    tn.wait_ge(swl.h, g_split)
                    if gp == 0 and part == 0:
                        tn.wait_ge(sra.h, sra.n)
                        tn.wait_ge(sev.h, sev.n)
                    elif part == 0:
                        tn.wait_ge(sgu.h, gu_done_at[gp - 1])
                    for nbi in range(NB):
                        pb = PS[0 + nbi] if part == 0 else PS[2 + nbi]
                        for jc in range(NJ):
                            tn.matmul(pb[:, 0:512], whi1(part)[:, jc * 128:(jc + 1) * 128],
                                      n1hi(jc)[:, nbi * 512:(nbi + 1) * 512],
                                      start=(jc == 0), stop=False)
                            tn.matmul(pb[:, 0:512], whi1(part)[:, jc * 128:(jc + 1) * 128],
                                      n1lo(jc)[:, nbi * 512:(nbi + 1) * 512],
                                      start=False, stop=False)
                            ins = tn.matmul(pb[:, 0:512], wlo1(part)[:, jc * 128:(jc + 1) * 128],
                                            n1hi(jc)[:, nbi * 512:(nbi + 1) * 512],
                                            start=False, stop=(jc == NJ - 1))
                        smm.inc(ins)
            gu_mm = smm.n

            @block.scalar
            def _(sc):
                sc.wait_ge(smm.h, gu_mm)
                for nbi in range(NB):
                    ins = sc.activation(out=gs[:, nbi * 512:(nbi + 1) * 512],
                                        in_=PS[nbi][:, 0:512], func=AF.Silu)
                sev.inc(ins)
            gsil = sev.n

            @block.vector
            def _(v):
                v.wait_ge(sev.h, gsil)
                for nbi in range(NB):
                    ins = v.tensor_tensor(out=gutmp[:, nbi * 512:(nbi + 1) * 512],
                                          in0=gs[:, nbi * 512:(nbi + 1) * 512],
                                          in1=PS[2 + nbi][:, 0:512], op=AL.mult)
                sgu.inc(ins)
                v.wait_ge(sgu.h, sgu.n)
                v.tensor_copy(out=guphi(gp), in_=gutmp)
                ins = v.tensor_tensor(out=guplo(gp), in0=gutmp, in1=guphi(gp), op=AL.subtract)
                shs.inc(ins)
            gu_done_at[gp] = sgu.n
        gup_done = shs.n
        b.fence()

        dso_at = {}
        sxr = b.sem("f_xr")
        xrslot = A[:, M["GS"]: M["GS"] + 1024]
        tmp2 = lambda i_: A[:, M["TMP"] + (i_ % 2) * 512: M["TMP"] + (i_ % 2) * 512 + 512]
        d_split_at = {}

        def d_prep(ic):
            @block.sync
            def _(s, ic=ic):
                s.wait_ge(swl.h, max(0, swl.n - 1))
                ld_w.inc(s.dma_start(
                    out=wstage(ic % 2).rearrange("p (a c) -> p a c", a=8),
                    in_=w["sd"].rearrange("(gc p) c -> p gc c", p=128)[:, :, ic * 128:(ic + 1) * 128]), 16)
            d_ld = ld_w.n

            @block.gpsimd
            def _(g, ic=ic):
                g.wait_ge(smm.h, smm.n if ic == 0 else smm.n - NB)
                ldh.inc(g.dma_start(
                    out=whi1(ic % 2).rearrange("p (a c) -> p a c", a=8),
                    in_=w["sd"].rearrange("(gc p) c -> p gc c", p=128)[:, :, ic * 128:(ic + 1) * 128]), 16)
            d_hi = ldh.n

            @block.vector
            def _(v, ic=ic, d_ld=d_ld, d_hi=d_hi):
                v.wait_ge(ld_w.h, d_ld)
                v.wait_ge(ldh.h, d_hi)
                v.wait_ge(smm.h, smm.n if ic == 0 else smm.n - NB)
                ins = v.tensor_tensor(out=wlo1(ic % 2), in0=wstage(ic % 2), in1=whi1(ic % 2), op=AL.subtract)
                swl.inc(ins)
            d_split_at[ic] = swl.n

        d_prep(0)
        for ic in range(NJ):
            @block.sync
            def _(s):
                if ic > 0:
                    s.wait_ge(sso.h, dso_at[ic - 1])
                else:
                    s.wait_ge(sgu.h, sgu.n)
                    s.wait_ge(sev.h, sev.n)
                sxr.inc(s.dma_start(out=xrslot, in_=src_d[ic * 128:(ic + 1) * 128, c0:c0 + HT]), 16)

            @block.tensor
            def _(tn):
                tn.wait_ge(swl.h, d_split_at[ic])
                tn.wait_ge(shs.h, gup_done)
                if ic >= 1:
                    tn.wait_ge(sso.h, dso_at[ic - 1])
                else:
                    tn.wait_ge(sso.h, sso.n)
                    tn.wait_ge(sgu.h, sgu.n)
                for nbi in range(NB):
                    for gc in range(NJ):
                        tn.matmul(PS[4 + nbi][:, 0:512], whi1(ic % 2)[:, gc * 128:(gc + 1) * 128],
                                  guphi(gc)[:, nbi * 512:(nbi + 1) * 512],
                                  start=(gc == 0), stop=False)
                        tn.matmul(PS[4 + nbi][:, 0:512], whi1(ic % 2)[:, gc * 128:(gc + 1) * 128],
                                  guplo(gc)[:, nbi * 512:(nbi + 1) * 512],
                                  start=False, stop=False)
                        ins = tn.matmul(PS[4 + nbi][:, 0:512], wlo1(ic % 2)[:, gc * 128:(gc + 1) * 128],
                                        guphi(gc)[:, nbi * 512:(nbi + 1) * 512],
                                        start=False, stop=(gc == NJ - 1))
                    smm.inc(ins)
            dbase = smm.n - NB
            if ic + 1 < NJ:
                d_prep(ic + 1)

            xr_goal = sxr.n
            @block.vector
            def _(v):
                v.wait_ge(sxr.h, xr_goal)
                v.wait_ge(smm.h, dbase + NB)
                for nbi in range(NB):
                    sl = slice(nbi * 512, (nbi + 1) * 512)
                    v.tensor_tensor(out=tmp2(ic), in0=PS[4 + nbi][:, 0:512], in1=sigb[:, sl], op=AL.mult)
                    v.scalar_tensor_tensor(out=n1(ic)[:, sl], in0=racc(ic)[:, sl], scalar=0.5,
                                           in1=xrslot[:, sl], op0=AL.mult, op1=AL.add)
                    ins = v.tensor_tensor(out=n1(ic)[:, sl], in0=n1(ic)[:, sl], in1=tmp2(ic), op=AL.add)
                    sso.inc(ins)
            dso_at[ic] = sso.n
            out_done = sso.n

            @block.sync
            def _(s):
                s.wait_ge(sso.h, out_done)
                sto.inc(s.dma_start(out=dst_d[ic * 128:(ic + 1) * 128, c0:c0 + HT],
                                    in_=n1(ic)), 16)
        b.fence()

    @block.sync
    def _(s):
        s.wait_ge(sto.h, sto.n)
    b.fence()



def emit_attention(b, t, src_d, dst_d, aw):
    block = b.block
    A = t["arena"]
    PS = t["psum"]
    ident = t["ident"]
    ones1 = t["ones1x128"]
    pairmat2 = t["pairmat2"]
    expmat2 = t["expmat2"]
    selmat = t["selmat"]
    qmask = t["qmask"]
    biases = t["biases_sb"]
    bhelp = t["bhelp_sb"]

    ld_w = b.sem("a_w")
    smm = b.sem("a_mm")
    sev = b.sem("a_ev")
    svv = b.sem("a_vv")
    stp = b.sem("a_tp")
    pk = b.sem("a_pk")
    cc = b.sem("a_cc")
    kld = b.sem("a_kld")
    ssc = b.sem("a_sc")
    sxp = b.sem("a_xp")
    sav = b.sem("a_av")
    soe = b.sem("a_oe")
    sqm = b.sem("a_qm")
    sto = b.sem("a_to")

    agin, agout = aw["agin"], aw["agout"]
    q_d = aw["q_d"]

    M2_N1, M2_XH, M2_SQ = 0, 16384, 24576
    for hh in range(2):
        c0 = hh * 1024
        b.fence()
        emit_ln(b, t,
                lambda jc, c0=c0: src_d[jc * 128:(jc + 1) * 128, c0:c0 + 1024],
                lambda jc: A[:, M2_XH + jc * 1024: M2_XH + (jc + 1) * 1024],
                lambda jc: A[:, M2_SQ + jc * 1024: M2_SQ + (jc + 1) * 1024],
                lambda jc, c0=c0: A[:, M2_N1 + jc * 2048 + c0: M2_N1 + jc * 2048 + c0 + 1024],
                2)
    b.fence()
    n2 = lambda jc: A[:, M2_N1 + jc * 2048: M2_N1 + (jc + 1) * 2048]

    QP, KP, VP = 16384, 18432, 20480
    HELK, HELQ, SQH = 22528, 24576, 26624
    QNP, RDUP = 28672, 30720
    PLN = 32768
    VT = 36864
    ROWS = 37904
    WB0 = 38416
    WHEL = t["fmap"]["WB"]
    qp = A[:, QP: QP + 2048]
    kp = A[:, KP: KP + 2048]
    vp = A[:, VP: VP + 2048]
    helk = A[:, HELK: HELK + 2048]
    helq = A[:, HELQ: HELQ + 2048]
    sqh = A[:, SQH: SQH + 2048]
    qnp = A[:, QNP: QNP + 2048]
    rdup = A[:, RDUP: RDUP + 2048]
    pln = A[:, PLN: PLN + 4096]
    vt = A[:, VT: VT + 1040]
    rows = A[0:1, ROWS: ROWS + 512]
    wbA = lambda p: A[:, WB0 + p * 1024: WB0 + (p + 1) * 1024]
    whel_sb = A[:, WHEL: WHEL + 1024]
    assert WB0 + 2048 <= t["fmap"]["MUBS"], (WB0, t["fmap"]["MUBS"])
    assert WHEL >= t["fmap"]["WB"]

    ldc = b.sem("a_ldc")

    @block.sync
    def _(s):
        ldc.inc(s.dma_start(out=whel_sb.rearrange("p (a c) -> p a c", a=8),
                            in_=aw["whelp"].rearrange("(hp p) c -> p hp c", p=128)), 16)
    whel_ready = ldc.n

    def proj_pair(hp, wt_d, dst, bias_col, wslot):
        @block.sync
        def _(s):
            s.wait_ge(smm.h, smm.n)
            ld_w.inc(s.dma_start(
                out=wbA(wslot).rearrange("p (a c) -> p a c", a=8),
                in_=wt_d.rearrange("(jc p) c -> p jc c", p=128)[:, :, hp * 128:(hp + 1) * 128]), 16)

        evb = sev.n
        @block.tensor
        def _(tn):
            tn.wait_ge(ld_w.h, ld_w.n)
            tn.wait_ge(sev.h, evb)
            for nbi in range(4):
                if nbi >= 2:
                    tn.wait_ge(sev.h, evb + nbi - 1)
                for jc in range(NJ):
                    ins = tn.matmul(PS[nbi % 2][:, 0:512], wbA(wslot)[:, jc * 128:(jc + 1) * 128],
                                    n2(jc)[:, nbi * 512:(nbi + 1) * 512],
                                    start=(jc == 0), stop=(jc == NJ - 1))
                smm.inc(ins)
        pbase = smm.n - 4

        @block.scalar
        def _(sc):
            for nbi in range(4):
                sc.wait_ge(smm.h, pbase + nbi + 1)
                ins = sc.activation(out=dst[:, nbi * 512:(nbi + 1) * 512], in_=PS[nbi % 2][:, 0:512],
                                    func=AF.Identity, bias=biases[:, bias_col:bias_col + 1])
                sev.inc(ins)

    def hel_pair(hp, src, dsttile):
        evb = sev.n
        @block.tensor
        def _(tn):
            tn.wait_ge(ldc.h, whel_ready)
            tn.wait_ge(sev.h, evb)
            for nbi in range(4):
                if nbi >= 2:
                    tn.wait_ge(sev.h, evb + nbi - 1)
                stp.inc(tn.matmul(PS[2 + nbi % 2][:, 0:512], whel_sb[:, hp * 128:(hp + 1) * 128],
                                  src[:, nbi * 512:(nbi + 1) * 512], start=True, stop=True))
        hbase = stp.n - 4

        @block.scalar
        def _(sc):
            for nbi in range(4):
                sc.wait_ge(stp.h, hbase + nbi + 1)
                ins = sc.activation(out=dsttile[:, nbi * 512:(nbi + 1) * 512],
                                    in_=PS[2 + nbi % 2][:, 0:512], func=AF.Identity,
                                    bias=bhelp[:, hp:hp + 1])
                sev.inc(ins)

    def l2norm_pair(srctile):
        @block.scalar
        def _(sc):
            sc.wait_ge(sev.h, sev.n)
            sc.wait_ge(svv.h, svv.n)
            ins = sc.activation(out=sqh, in_=srctile, func=AF.Square)
            svv.inc(ins)
        sq_done = svv.n

        svb = svv.n
        @block.tensor
        def _(tn):
            tn.wait_ge(svv.h, sq_done)
            for nbi in range(4):
                if nbi >= 2:
                    tn.wait_ge(svv.h, svb + nbi - 1)
                stp.inc(tn.matmul(PS[4 + nbi % 2][0:64, 0:512], pairmat2[:, :],
                                  sqh[:, nbi * 512:(nbi + 1) * 512], start=True, stop=True))
        psbase = stp.n - 4

        @block.vector
        def _(v):
            for nbi in range(4):
                v.wait_ge(stp.h, psbase + nbi + 1)
                ins = v.tensor_scalar(out=qnp[0:64, nbi * 512:(nbi + 1) * 512],
                                      in0=PS[4 + nbi % 2][0:64, 0:512],
                                      scalar1=1e-24, scalar2=None, op0=AL.max)
                svv.inc(ins)
            v.wait_ge(svv.h, svv.n)
            ins = v.reciprocal(out=qnp[0:64, :], in_=qnp[0:64, :])
            svv.inc(ins)
        rc_done = svv.n

        @block.scalar
        def _(sc):
            sc.wait_ge(svv.h, rc_done)
            ins = sc.activation(out=qnp[0:64, :], in_=qnp[0:64, :], func=AF.Sqrt)
            svv.inc(ins)
        r_done = svv.n

        svb2 = svv.n
        @block.tensor
        def _(tn):
            tn.wait_ge(svv.h, r_done)
            for nbi in range(4):
                if nbi >= 2:
                    tn.wait_ge(svv.h, svb2 + nbi - 1)
                stp.inc(tn.matmul(PS[6 + nbi % 2][:, 0:512], expmat2[:, :],
                                  qnp[0:64, nbi * 512:(nbi + 1) * 512], start=True, stop=True))
        exbase = stp.n - 4

        @block.vector
        def _(v):
            for nbi in range(4):
                v.wait_ge(stp.h, exbase + nbi + 1)
                ins = v.tensor_copy(out=rdup[:, nbi * 512:(nbi + 1) * 512],
                                    in_=PS[6 + nbi % 2][:, 0:512])
                svv.inc(ins)
            v.wait_ge(svv.h, svv.n)
            ins = v.tensor_tensor(out=srctile, in0=srctile, in1=rdup, op=AL.mult)
            svv.inc(ins)
        return svv.n

    def deinterleave(srctile, done_val):
        @block.vector
        def _(v):
            v.wait_ge(svv.h, done_val)
            v.wait_ge(pk.h, pk.n)
            for band in range(4):
                r0 = band * 32
                for par in range(2):
                    ins = v.tensor_copy(out=pln[r0:r0 + 32, par * 1024:(par + 1) * 1024],
                                        in_=srctile[r0:r0 + 32, par::2])
                    svv.inc(ins)
        return svv.n


    for hp in range(8):
        b.fence()
        proj_pair(hp, aw["wqt"], qp, 0 + hp, 0)
        proj_pair(hp, aw["wkt"], kp, 8 + hp, 1)
        proj_pair(hp, aw["wvt"], vp, 16 + hp, 2)
        qkv_done = sev.n

        @block.sync
        def _(s):
            s.wait_ge(sev.h, qkv_done)
            pk.inc(s.dma_start(out=q_d[hp * 128:(hp + 1) * 128, :], in_=qp), 16)
            if aw.get("adbg") is not None and hp == 0:
                pk.inc(s.dma_start(out=aw["adbg"][0:128, 6656:8704], in_=n2(0)), 16)
                pk.inc(s.dma_start(out=aw["adbg"][0:128, 8704:10752], in_=qp), 16)
                pk.inc(s.dma_start(out=aw["adbg"][0:128, 10752:12288], in_=kp[:, 0:1536]), 16)

        for hh in range(2):
            head = hp * 2 + hh
            g, hl = head // 4, head % 4
            sh = hl * HE
            @block.gpsimd
            def _(gp):
                gp.wait_ge(sev.h, qkv_done)
                pk.inc(gp.dma_start(
                    out=agin[g][sh + OFF_KHT: sh + OFF_KHT + 131072].rearrange("(a c) -> a c", a=64),
                    in_=kp[hh * 64:(hh + 1) * 64, :]), 16)

        for side, tile in ((0, helk), (1, helq)):
            hel_pair(hp, kp if side == 0 else qp, tile)
            dv = l2norm_pair(tile)
            if aw.get("adbg") is not None and hp == 0 and side == 1:
                @block.sync
                def _(s):
                    s.wait_ge(svv.h, dv)
                    sto.inc(s.dma_start(out=aw["adbg"][0:128, 0:2048], in_=helq), 16)
            dv2 = deinterleave(tile, dv)
            if aw.get("adbg") is not None and hp == 0 and side == 1:
                @block.sync
                def _(s):
                    s.wait_ge(svv.h, dv2)
                    sto.inc(s.dma_start(out=aw["adbg"][0:128, 2048:6144], in_=pln), 16)
            for hh in range(2):
                head = hp * 2 + hh
                g, hl = head // 4, head % 4
                sh = hl * HE
                @block.gpsimd
                def _(gp):
                    gp.wait_ge(svv.h, dv2)
                    off = OFF_KN if side == 0 else OFF_QN
                    for q in range(4):
                        comp, par = q // 2, q % 2
                        src = pln[hh * 64 + comp * 32: hh * 64 + comp * 32 + 32,
                                  par * 1024:(par + 1) * 1024]
                        pk.inc(gp.dma_start(
                            out=agin[g][sh + off + q * 32768: sh + off + (q + 1) * 32768].rearrange("(a c) -> a c", a=32),
                            in_=src), 16)

        for hh in range(2):
            head = hp * 2 + hh
            g, hl = head // 4, head % 4
            sh = hl * HE

            @block.vector
            def _(v):
                v.wait_ge(pk.h, pk.n)
                ins = v.memset(vt[:, 64::65], 1.0)
                svv.inc(ins)
            vt_ones = svv.n

            for tb in range(16):
                @block.tensor
                def _(tn):
                    tn.wait_ge(sev.h, qkv_done)
                    tn.wait_ge(stp.h, stp.n)
                    stp.inc(tn.transpose(PS[2][:, 0:128], vp[:, tb * 128:(tb + 1) * 128], ident[:, :]))
                tdone = stp.n

                @block.vector
                def _(v):
                    v.wait_ge(stp.h, tdone)
                    v.wait_ge(svv.h, vt_ones)
                    ins = v.tensor_copy(out=vt[:, tb * 65: tb * 65 + 64],
                                        in_=PS[2][:, hh * 64: hh * 64 + 64])
                    stp.inc(ins)
            vt_done = stp.n

            @block.scalar
            def _(sc):
                sc.wait_ge(sev.h, qkv_done)
                sc.wait_ge(svv.h, svv.n)
                ins = sc.activation(out=sqh, in_=kp, func=AF.Square)
                svv.inc(ins)
            ksq = svv.n

            @block.tensor
            def _(tn):
                tn.wait_ge(svv.h, ksq)
                tn.wait_ge(stp.h, stp.n)
                for nbi in range(4):
                    stp.inc(tn.matmul(PS[4 + nbi][0:1, 0:512], selmat[:, hh:hh + 1],
                                      sqh[:, nbi * 512:(nbi + 1) * 512], start=True, stop=True))
            kbase = stp.n - 4

            @block.vector
            def _(v):
                for nbi in range(4):
                    v.wait_ge(stp.h, kbase + nbi + 1)
                    ins = v.reduce_max(rows[:, 16 + nbi: 17 + nbi], PS[4 + nbi][0:1, 0:512], axis=AX)
                    svv.inc(ins)
                v.wait_ge(svv.h, svv.n)
                ins = v.reduce_max(rows[:, 20:21], rows[:, 16:20], axis=AX)
                svv.inc(ins)
            kmx = svv.n

            @block.scalar
            def _(sc):
                sc.wait_ge(svv.h, kmx)
                ins = sc.activation(out=rows[:, 21:22], in_=rows[:, 20:21], func=AF.Sqrt)
                svv.inc(ins)
            kmx2 = svv.n

            @block.gpsimd
            def _(gp):
                gp.wait_ge(svv.h, kmx2)
                gp.wait_ge(stp.h, vt_done)
                pk.inc(gp.dma_start(
                    out=agin[g][sh + OFF_VB: sh + OFF_VB + 1040 * 128].rearrange("(a c) -> a c", a=128),
                    in_=vt), 16)
                pk.inc(gp.dma_start(
                    out=agin[g][sh + OFF_ST: sh + OFF_ST + 1].rearrange("(a c) -> a c", a=1),
                    in_=rows[:, 21:22]), 16)

        if hp % 2 == 1:
            g = hp // 2
            @block.gpsimd
            def _(gp):
                gp.wait_ge(pk.h, pk.n)
                ins = gp.collective_compute(
                    "AllGather", AL.bypass,
                    replica_groups=[[0, 1, 2, 3], [4, 5, 6, 7]],
                    ins=[agin[g].opt()],
                    outs=[agout[g].opt()],
                )
                cc.inc(ins, 1)

    pack_goals = [(smm.h, smm.n), (stp.h, stp.n), (svv.h, svv.n), (sev.h, sev.n), (pk.h, pk.n)]
    b.fence()

    KT, VTA, QT, PT, OACC = 0, 8192, 12352, 14400, 18496
    APAIR, QW, AHEAD = 34880, 36928, 45120
    ROWS2, RZ, WOB, NEGM = 47168, 47696, 45120 + 0, 47680
    AHEAD = 45120
    ROWS2 = 47168
    NEGM = 47680
    RZ = 47696
    WOB = 49744
    kt = lambda p: A[:, KT + p * 4096: KT + (p + 1) * 4096]
    vta = lambda p: A[:, VTA + p * 2080: VTA + (p + 1) * 2080]
    qt = A[:, QT: QT + 2048]
    pt = lambda p: A[:, PT + p * 2048: PT + (p + 1) * 2048]
    oacc = lambda ic: A[:, OACC + ic * 2048: OACC + (ic + 1) * 2048]
    apair = A[:, APAIR: APAIR + 2048]
    qw = A[:, QW: QW + 8192]
    ahead = A[:, AHEAD: AHEAD + 2048]
    rows2 = A[0:1, ROWS2: ROWS2 + 512]
    negm = A[:, NEGM: NEGM + 1]
    rz = A[:, RZ: RZ + 2048]
    wob = A[:, WOB: WOB + 1024]

    def ag_off(g, r, hl, field):
        return r * SHARD + hl * HE + field

    def assemble_kv(eng, head, th, pbuf):
        g, hl = head // 4, head % 4
        ag = agout[g]
        for i in range(2):
            r = 2 * th + i
            kld.inc(eng.dma_start(
                out=kt(pbuf)[0:64, i * 2048:(i + 1) * 2048],
                in_=ag[ag_off(g, r, hl, OFF_KHT): ag_off(g, r, hl, OFF_KHT) + 131072].rearrange("(a c) -> a c", a=64)), 16)
        for par in range(2):
            for r4 in range(4):
                o = OFF_KN + (2 * th + par) * 32768
                src = ag[ag_off(g, r4, hl, o): ag_off(g, r4, hl, o) + 32768].rearrange("(a c) -> a c", a=32)
                kld.inc(eng.dma_start(
                    out=kt(pbuf)[64 + 32 * par: 96 + 32 * par, r4 * 1024:(r4 + 1) * 1024],
                    in_=src), 16)
        for i in range(2):
            r = 2 * th + i
            kld.inc(eng.dma_start(
                out=vta(pbuf)[:, i * 1040:(i + 1) * 1040],
                in_=ag[ag_off(g, r, hl, OFF_VB): ag_off(g, r, hl, OFF_VB) + 1040 * 128].rearrange("(a c) -> a c", a=128)), 16)

    @block.vector
    def _(v):
        v.wait_ge(pk.h, pk.n)
        for ic in range(NJ):
            ins = v.memset(oacc(ic), 0.0)
        soe.inc(ins)

    for head in range(H):
        g, hl = head // 4, head % 4
        hp, hh = head // 2, head % 2
        sc_head0 = ssc.n
        xp_head0 = sxp.n
        av_head0 = sav.n

        @block.sync
        def _(s):
            s.wait_ge(cc.h, g + 1)
            for hh_, nn_ in pack_goals:
                s.wait_ge(hh_, nn_)
            s.wait_ge(ssc.h, ssc.n)
            s.wait_ge(sav.h, sav.n)
            assemble_kv(s, head, 0, 0)
        kld_h0 = kld.n

        @block.sync
        def _(s):
            s.wait_ge(cc.h, g + 1)
            for hh_, nn_ in pack_goals:
                s.wait_ge(hh_, nn_)
            s.wait_ge(sqm.h, sqm.n)
            s.wait_ge(ssc.h, ssc.n)
            kld.inc(s.dma_start(out=qt[0:64, :], in_=q_d[head * 64:(head + 1) * 64, :]), 16)
            for wsel in range(4):
                p_pl, s2 = wsel // 2, wsel % 2
                for par in range(2):
                    for i in range(2):
                        r = 2 * s2 + i
                        o = OFF_QN + (2 * p_pl + par) * 32768
                        src = agout[g][ag_off(g, r, hl, o): ag_off(g, r, hl, o) + 32768].rearrange("(a c) -> a c", a=32)
                        kld.inc(s.dma_start(
                            out=qw[64 + 32 * par: 96 + 32 * par,
                                   wsel * 2048 + i * 1024: wsel * 2048 + (i + 1) * 1024],
                            in_=src), 16)
            stat_src = agout[g].rearrange("(r x) -> r x", r=4)[:, hl * HE + OFF_ST: hl * HE + OFF_ST + 1]
            kld.inc(s.dma_start(out=rows2[0:1, 0:4], in_=stat_src, allow_slow_non_contiguous=True), 16)
        kld_q = kld.n

        @block.vector
        def _(v):
            v.wait_ge(kld.h, kld_q)
            for wsel in range(4):
                if wsel == 0:
                    ins = v.tensor_scalar(out=qt[64:128, :], in0=qw[64:128, 0:2048],
                                          scalar1=qmask[64:128, 0:1], scalar2=None, op0=AL.mult)
                else:
                    ins = v.scalar_tensor_tensor(out=qt[64:128, :],
                                                 in0=qw[64:128, wsel * 2048:(wsel + 1) * 2048],
                                                 scalar=qmask[64:128, wsel:wsel + 1],
                                                 in1=qt[64:128, :], op0=AL.mult, op1=AL.add)
                sqm.inc(ins)
                v.wait_ge(sqm.h, sqm.n)
        qsel_done = sqm.n

        @block.scalar
        def _(sc):
            sc.wait_ge(sqm.h, qsel_done)
            ins = sc.mul(qt[0:64, :], qt[0:64, :], 0.0625)
            sqm.inc(ins)
            ins = sc.mul(qt[64:128, :], qt[64:128, :], 1.0 / 64.0)
            sqm.inc(ins)
            sc.wait_ge(sqm.h, sqm.n)
            ins = sc.activation(out=qw[0:64, 0:2048], in_=qt[0:64, :], func=AF.Square)
            sqm.inc(ins)
        qsq_done = sqm.n

        @block.tensor
        def _(tn):
            tn.wait_ge(sqm.h, qsq_done)
            tn.wait_ge(soe.h, soe.n)
            tn.wait_ge(stp.h, stp.n)
            for nbi in range(4):
                stp.inc(tn.matmul(PS[4 + nbi][0:1, 0:512], t["ones128"][0:64, 0:1],
                                  qw[0:64, nbi * 512:(nbi + 1) * 512], start=True, stop=True))
        qs_base = stp.n - 4

        @block.vector
        def _(v):
            for nbi in range(4):
                v.wait_ge(stp.h, qs_base + nbi + 1)
                ins = v.reduce_max(rows2[:, 8 + nbi: 9 + nbi], PS[4 + nbi][0:1, 0:512], axis=AX)
                svv.inc(ins)
            v.wait_ge(svv.h, svv.n)
            ins = v.reduce_max(rows2[:, 12:13], rows2[:, 8:12], axis=AX)
            svv.inc(ins)
            v.wait_ge(svv.h, svv.n)
            ins = v.reduce_max(rows2[:, 13:14], rows2[:, 0:4], axis=AX)
            svv.inc(ins)
        qk_red = svv.n

        @block.scalar
        def _(sc):
            sc.wait_ge(svv.h, qk_red)
            ins = sc.activation(out=rows2[:, 14:15], in_=rows2[:, 12:13], func=AF.Sqrt)
            svv.inc(ins)
        qmx = svv.n

        @block.vector
        def _(v):
            v.wait_ge(svv.h, qmx)
            ins = v.tensor_tensor(out=rows2[:, 15:16], in0=rows2[:, 14:15],
                                  in1=rows2[:, 13:14], op=AL.mult)
            svv.inc(ins)
            v.wait_ge(svv.h, svv.n)
            ins = v.tensor_scalar(out=rows2[:, 16:17], in0=rows2[:, 15:16],
                                  scalar1=-1.0, scalar2=-1.5, op0=AL.mult, op1=AL.add)
            svv.inc(ins)
        negm_row = svv.n

        @block.tensor
        def _(tn):
            tn.wait_ge(svv.h, negm_row)
            tn.wait_ge(stp.h, stp.n)
            stp.inc(tn.matmul(PS[4][:, 0:1], ones1[0:1, :], rows2[:, 16:17], start=True, stop=True))
        negm_mm = stp.n

        @block.scalar
        def _(sc):
            sc.wait_ge(stp.h, negm_mm)
            ins = sc.copy(out=negm, in_=PS[4][:, 0:1])
            sqm.inc(ins)
        negm_done = sqm.n

        if aw.get("adbg") is not None and head == 0:
            @block.sync
            def _(s):
                s.wait_ge(sqm.h, negm_done)
                s.wait_ge(kld.h, kld_h0)
                sto.inc(s.dma_start(out=aw["adbg"][0:128, 0:2048], in_=qt), 16)
                sto.inc(s.dma_start(out=aw["adbg"][0:128, 2048:4096], in_=kt(0)[:, 0:2048]), 16)
                sto.inc(s.dma_start(out=aw["adbg"][0:1, 6144:6145], in_=negm[0:1, 0:1]), 16)

        @block.sync
        def _(s):
            s.wait_ge(sav.h, av_head0 + 4 * 16)
            assemble_kv(s, head, 1, 1)
        kld_h1 = kld.n

        sc_head0 = ssc.n

        @block.tensor
        def _(tn):
            for tb in range(64):
                th = tb // 32
                tbl = tb % 32
                if tb == 0:
                    tn.wait_ge(kld.h, kld_q)
                if tb == 32:
                    tn.wait_ge(kld.h, kld_h1)
                if tb == 0:
                    tn.wait_ge(sqm.h, negm_done)
                if tb >= 1:
                    tn.wait_ge(sxp.h, xp_head0 + 4 * tb)
                if tb >= 2:
                    tn.wait_ge(sav.h, av_head0 + 4 * (tb - 1))
                kslice = kt(th)[:, tbl * 128:(tbl + 1) * 128]
                for sb_i in range(4):
                    ssc.inc(tn.matmul(PS[sb_i][:, 0:512], kslice,
                                      qt[:, sb_i * 512:(sb_i + 1) * 512], start=True, stop=True))
                if tb >= 1:
                    pth = (tb - 1) // 32
                    vtile = vta(pth)[:, ((tb - 1) % 32) * 65: ((tb - 1) % 32) * 65 + 65]
                    for sb_i in range(4):
                        sav.inc(tn.matmul(PS[4 + sb_i][0:65, 0:512], vtile,
                                          pt((tb - 1) % 2)[:, sb_i * 512:(sb_i + 1) * 512],
                                          start=(tb - 1 == 0), stop=(tb - 1 == 63)))

        @block.scalar
        def _(sc):
            sc.wait_ge(sqm.h, negm_done)
            for tb in range(64):
                if tb >= 2:
                    sc.wait_ge(sav.h, av_head0 + 4 * (tb - 1))
                sc.wait_ge(ssc.h, sc_head0 + 4 * tb + 4)
                sc.activation(out=pt(tb % 2)[:, 0:1024], in_=t["psq"][:, 0:1024],
                              func=AF.Exp, bias=negm[:, 0:1])
                sxp.inc(sc.activation(out=pt(tb % 2)[:, 1024:2048], in_=t["psq"][:, 1024:2048],
                                      func=AF.Exp, bias=negm[:, 0:1]), 4)

        @block.tensor
        def _(tn):
            tn.wait_ge(sxp.h, sxp.n)
            vtile = vta(1)[:, 31 * 65: 31 * 65 + 65]
            for sb_i in range(4):
                sav.inc(tn.matmul(PS[4 + sb_i][0:65, 0:512], vtile,
                                  pt(63 % 2)[:, sb_i * 512:(sb_i + 1) * 512],
                                  start=False, stop=True))
        av_done = sav.n

        @block.vector
        def _(v):
            v.wait_ge(sav.h, av_done)
            for sb_i in range(4):
                ins = v.tensor_copy(out=rz[64:65, sb_i * 512:(sb_i + 1) * 512],
                                    in_=PS[4 + sb_i][64:65, 0:512])
                svv.inc(ins)
        z_ev = svv.n

        @block.sync
        def _(s):
            s.wait_ge(svv.h, z_ev)
            kld.inc(s.dma_start(out=rz[0:1, :], in_=rz[64:65, :]), 16)
        z_mv = kld.n

        @block.vector
        def _(v):
            v.wait_ge(kld.h, z_mv)
            ins = v.reciprocal(out=rz[0:1, :], in_=rz[0:1, :])
            svv.inc(ins)
        z_rc = svv.n

        @block.tensor
        def _(tn):
            tn.wait_ge(svv.h, z_rc)
            tn.wait_ge(stp.h, stp.n)
            tn.wait_ge(sxp.h, sxp.n)
            for nbi in range(4):
                stp.inc(tn.matmul(PS[nbi][0:64, 0:512], ones1[0:1, 0:64],
                                  rz[0:1, nbi * 512:(nbi + 1) * 512], start=True, stop=True))
        rzb_mm = stp.n - 4

        @block.scalar
        def _(sc):
            for nbi in range(4):
                sc.wait_ge(stp.h, rzb_mm + nbi + 1)
                ins = sc.copy(out=rz[0:64, nbi * 512:(nbi + 1) * 512], in_=PS[nbi][0:64, 0:512])
            svv.inc(ins)
        rzb_done = svv.n

        @block.vector
        def _(v):
            v.wait_ge(svv.h, rzb_done)
            for sb_i in range(4):
                ins = v.tensor_tensor(out=ahead[0:64, sb_i * 512:(sb_i + 1) * 512],
                                      in0=PS[4 + sb_i][0:64, 0:512],
                                      in1=rz[0:64, sb_i * 512:(sb_i + 1) * 512], op=AL.mult)
                soe.inc(ins)
        ah_done = soe.n

        @block.sync
        def _(s):
            s.wait_ge(soe.h, ah_done)
            kld.inc(s.dma_start(out=apair[hh * 64:(hh + 1) * 64, :], in_=ahead[0:64, :]), 16)
        ap_mv = kld.n

        if aw.get("adbg") is not None and head == 1:
            @block.sync
            def _(s):
                s.wait_ge(kld.h, ap_mv)
                sto.inc(s.dma_start(out=aw["adbg"][0:128, 4096:6144], in_=apair), 16)

        if hh == 0:
            @block.sync
            def _(s):
                s.wait_ge(smm.h, smm.n)
                ld_w.inc(s.dma_start(
                    out=wob.rearrange("p (a c) -> p a c", a=8),
                    in_=aw["wot"][hp * 128:(hp + 1) * 128, :].rearrange("p (a c) -> p a c", a=8)), 16)
            wo_ld_pair = ld_w.n
        if hh == 1:
            wo_ld = wo_ld_pair

            oeb = soe.n
            @block.tensor
            def _(tn):
                tn.wait_ge(ld_w.h, wo_ld)
                tn.wait_ge(kld.h, ap_mv)
                tn.wait_ge(stp.h, stp.n)
                tn.wait_ge(soe.h, oeb)
                for ic in range(NJ):
                    if ic >= 1:
                        tn.wait_ge(soe.h, oeb + 4 * ic)
                    for nbi in range(4):
                        smm.inc(tn.matmul(PS[nbi][:, 0:512], wob[:, ic * 128:(ic + 1) * 128],
                                          apair[:, nbi * 512:(nbi + 1) * 512],
                                          start=True, stop=True))
            ob = smm.n - 32

            @block.vector
            def _(v):
                cnt = 0
                for ic in range(NJ):
                    for nbi in range(4):
                        cnt += 1
                        v.wait_ge(smm.h, ob + cnt)
                        ins = v.tensor_tensor(out=oacc(ic)[:, nbi * 512:(nbi + 1) * 512],
                                              in0=oacc(ic)[:, nbi * 512:(nbi + 1) * 512],
                                              in1=PS[nbi][:, 0:512], op=AL.add)
                        soe.inc(ins)

    b.fence()

    smm2 = b.sem("a_fm")
    for ic in range(NJ):
        @block.sync
        def _(s):
            s.wait_ge(smm2.h, smm2.n)
            s.wait_ge(sto.h, sto.n)
            kld.inc(s.dma_start(out=qw[:, 0:2048], in_=src_d[ic * 128:(ic + 1) * 128, :]), 16)
        x_ld = kld.n

        @block.scalar
        def _(sc):
            sc.wait_ge(kld.h, x_ld)
            ins = sc.activation(out=ahead[:, 0:2048], in_=oacc(ic), func=AF.Identity,
                                bias=biases[:, 24 + ic: 25 + ic])
            svv.inc(ins)
        o_b = svv.n

        @block.vector
        def _(v):
            v.wait_ge(svv.h, o_b)
            v.wait_ge(kld.h, x_ld)
            ins = v.tensor_tensor(out=qw[:, 0:2048], in0=qw[:, 0:2048], in1=ahead[:, 0:2048], op=AL.add)
            smm2.inc(ins)
        x3_done = smm2.n

        @block.sync
        def _(s):
            s.wait_ge(smm2.h, x3_done)
            sto.inc(s.dma_start(out=dst_d[ic * 128:(ic + 1) * 128, :], in_=qw[:, 0:2048]), 16)

    @block.sync
    def _(s):
        s.wait_ge(sto.h, sto.n)
    b.fence()



def build_program(stage="full"):
    nc = bass.Bass("TRN2", target_bir_lowering=False, debug=False)
    dt = lambda n, shp, k: nc.dram_tensor(n, shp, F32, kind=k).ap()
    x_t = dt("x_t", [E, T], "ExternalInput")
    y_t = dt("y_t", [E, T], "ExternalOutput")
    dbg_t = dt("dbg_t", [128, 8192], "ExternalOutput") if stage == "ffn1dbg" else None
    adbg_t = dt("adbg_t", [128, 12288], "ExternalOutput") if stage == "x3dbg" else None

    w = {}
    w["w1"] = [dt(f"w1_{e}", [E, E], "ExternalInput") for e in range(NEXP)]
    w["w2"] = [dt(f"w2_{e}", [E, E], "ExternalInput") for e in range(NEXP)]
    w["sgu"] = dt("sgu", [E, 2 * E], "ExternalInput")
    w["sd"] = dt("sd", [E, E], "ExternalInput")
    aw = {}
    aw["wqt"] = dt("wqt", [E, E], "ExternalInput")
    aw["wkt"] = dt("wkt", [E, E], "ExternalInput")
    aw["wvt"] = dt("wvt", [E, E], "ExternalInput")
    aw["wot"] = dt("wot", [E, E], "ExternalInput")
    aw["whelp"] = dt("whelp", [8 * 128, 128], "ExternalInput")
    cin = {}
    for name, shp in [("ident", [128, 128]), ("ones128", [128, 1]), ("ones1x128", [1, 128]),
                      ("half1x128", [1, 128]), ("pairmat2", [128, 64]), ("expmat2", [64, 128]),
                      ("selmat", [128, 2]), ("qmask", [128, 4]), ("rw", [128, 64]),
                      ("sgw", [128, 8]), ("biases", [128, 32]), ("bhelp", [128, 8])]:
        cin[name] = dt("c_" + name, shp, "ExternalInput")

    x1_d = dt("x1_d", [E, T], "Internal")
    q_d = dt("q_d", [E, T], "Internal")
    x3_d = dt("x3_d", [E, T], "Internal")
    aw["q_d"] = q_d
    aw["agin"] = [dt(f"agin{g}", [SHARD], "Internal") for g in range(4)]
    aw["agout"] = [dt(f"agout{g}", [4 * SHARD], "Internal") for g in range(4)]

    es = contextlib.ExitStack()
    with es:
        ARENA_W = 52272
        arena = es.enter_context(nc.sbuf_tensor("arena", [128, ARENA_W], F32))
        sb = {}
        for name, shp in [("ident", [128, 128]), ("ones128", [128, 1]), ("ones1x128", [1, 128]),
                          ("half1x128", [1, 128]), ("pairmat2", [128, 64]), ("expmat2", [64, 128]),
                          ("selmat", [128, 2]), ("qmask", [128, 4]), ("rw_sb", [128, 64]),
                          ("sgw_sb", [128, 8]), ("biases_sb", [128, 32]), ("bhelp_sb", [128, 8])]:
            sb[name] = es.enter_context(nc.sbuf_tensor("s_" + name, shp, F32))
        psq = es.enter_context(nc.psum_tensor("psq", [128, 2048], F32))
        psa = es.enter_context(nc.psum_tensor("psa", [128, 2048], F32))
        psum = [(psq if i < 4 else psa)[:, (i % 4) * 512: (i % 4 + 1) * 512] for i in range(8)]
        block = es.enter_context(nc.Block())
        b = B(nc, block, es)

        t = dict(arena=arena, psum=psum, psq=psq, psa=psa)
        for k in sb:
            t[k] = sb[k]

        fmap = {}
        off = 0
        for name, width in [("XH", 8192), ("N1", 8192), ("HB", 8192), ("RACC", 8192),
                            ("CBT", 1024), ("SIGB", 1024), ("GS", 1024), ("GS2", 1024), ("TMP", 1024),
                            ("LG", 1024), ("LGT", 8), ("CBM", 1024), ("CBM2", 1024),
                            ("SML", 40), ("MUBS", 1024), ("RBS", 1024), ("ROWS", 3072),
                            ("WB", 6144)]:
            fmap[name] = off
            off += width
        assert off <= ARENA_W, off
        t["fmap"] = fmap
        t["rowoff"] = fmap["ROWS"]
        t["mub"] = arena[:, fmap["MUBS"]: fmap["MUBS"] + 1024]
        t["rb"] = arena[:, fmap["RBS"]: fmap["RBS"] + 1024]

        ldc = b.sem("ldc0")

        @block.sync
        def _(s):
            for name, key, rowlim in [("ident", "ident", 128), ("ones128", "ones128", 128),
                                      ("ones1x128", "ones1x128", 1), ("half1x128", "half1x128", 1),
                                      ("pairmat2", "pairmat2", 128), ("expmat2", "expmat2", 64),
                                      ("selmat", "selmat", 128), ("qmask", "qmask", 128),
                                      ("biases", "biases_sb", 128), ("bhelp", "bhelp_sb", 128)]:
                ldc.inc(s.dma_start(out=sb[key][0:rowlim, :], in_=cin[name]), 16)
            ldc.inc(s.dma_start(out=sb["rw_sb"][:, :], in_=cin["rw"]), 16)
            ldc.inc(s.dma_start(out=sb["sgw_sb"][:, :], in_=cin["sgw"]), 16)

        for name in ("tensor", "vector", "scalar", "gpsimd"):
            def f(e):
                e.wait_ge(ldc.h, ldc.n)
            getattr(block, name)(f)

        if stage == "ffn1dbg":
            emit_ffn(b, t, x_t, y_t, w, dbg_d=dbg_t)
        else:
            emit_ffn(b, t, x_t, x1_d if stage != "ffn1" else y_t, w)
        if stage not in ("ffn1", "ffn1dbg"):
            aw["adbg"] = adbg_t if stage == "x3dbg" else None
            emit_attention(b, t, x1_d, x3_d if stage not in ("x3", "x3dbg") else y_t, aw)
            if stage != "x3":
                emit_ffn(b, t, x3_d, y_t, w)

        snap = [(s_.h, s_.n) for s_ in b._sems.values() if s_.n > 0]

        @block.sync
        def _(s):
            for h, n in snap:
                s.wait_ge(h, n)
    return nc



def _consts(quar):
    c = {}
    c["ident"] = np.eye(128, dtype=np.float32)
    c["ones128"] = np.ones((128, 1), dtype=np.float32)
    c["ones1x128"] = np.ones((1, 128), dtype=np.float32)
    c["half1x128"] = np.full((1, 128), 0.5, dtype=np.float32)
    pm = np.zeros((128, 64), dtype=np.float32)
    for u in range(128):
        if u < 64:
            pm[u, u % 32] = 1.0
        else:
            pm[u, 32 + (u - 64) % 32] = 1.0
    c["pairmat2"] = pm
    em = np.zeros((64, 128), dtype=np.float32)
    for u in range(128):
        if u < 64:
            em[u % 32, u] = 1.0
        else:
            em[32 + (u - 64) % 32, u] = 1.0
    c["expmat2"] = em
    sel = np.zeros((128, 2), dtype=np.float32)
    sel[0:64, 0] = 1.0
    sel[64:128, 1] = 1.0
    c["selmat"] = sel
    qm = np.zeros((128, 4), dtype=np.float32)
    qm[:, quar] = 1.0
    c["qmask"] = qm
    return c


def kernel(L0_attn_x, L0_attn_Wq, L0_attn_bq, L0_attn_Wk, L0_attn_bk, L0_attn_Wv,
           L0_attn_bv, L0_attn_Wo, L0_attn_bo, L0_attn_Whel, L0_attn_bhel,
           L0_ffn_router_w, L0_ffn_w1, L0_ffn_w2, L0_ffn_shared_gate_up,
           L0_ffn_shared_down, L0_ffn_shared_gate_w, _stage="full"):
    x = np.ascontiguousarray(np.asarray(L0_attn_x, dtype=np.float32)).reshape(2 * S, E)

    perm = np.array([2 * (ep % 32) + (ep // 32) for ep in range(64)], dtype=np.int64)
    whel = np.asarray(L0_attn_Whel, dtype=np.float32)
    whelT = whel.transpose(0, 2, 1)[:, :, perm]
    whelp = np.zeros((8 * 128, 128), dtype=np.float32)
    for hp in range(8):
        whelp[hp * 128: hp * 128 + 64, 0:64] = whelT[2 * hp]
        whelp[hp * 128 + 64: hp * 128 + 128, 64:128] = whelT[2 * hp + 1]
    bhel = np.asarray(L0_attn_bhel, dtype=np.float32)[:, perm]
    bhelp = np.zeros((128, 8), dtype=np.float32)
    for hp in range(8):
        bhelp[0:64, hp] = bhel[2 * hp]
        bhelp[64:128, hp] = bhel[2 * hp + 1]

    biases = np.zeros((128, 32), dtype=np.float32)
    for hp in range(8):
        biases[:, hp] = np.asarray(L0_attn_bq, dtype=np.float32)[hp * 128:(hp + 1) * 128]
        biases[:, 8 + hp] = np.asarray(L0_attn_bk, dtype=np.float32)[hp * 128:(hp + 1) * 128]
        biases[:, 16 + hp] = np.asarray(L0_attn_bv, dtype=np.float32)[hp * 128:(hp + 1) * 128]
        biases[:, 24 + hp] = np.asarray(L0_attn_bo, dtype=np.float32)[hp * 128:(hp + 1) * 128]

    shared = {
        "sgu": np.ascontiguousarray(np.asarray(L0_ffn_shared_gate_up, dtype=np.float32)),
        "sd": np.ascontiguousarray(np.asarray(L0_ffn_shared_down, dtype=np.float32)),
        "wqt": np.ascontiguousarray(np.asarray(L0_attn_Wq, dtype=np.float32).T),
        "wkt": np.ascontiguousarray(np.asarray(L0_attn_Wk, dtype=np.float32).T),
        "wvt": np.ascontiguousarray(np.asarray(L0_attn_Wv, dtype=np.float32).T),
        "wot": np.ascontiguousarray(np.asarray(L0_attn_Wo, dtype=np.float32).T),
        "whelp": whelp,
        "c_rw": np.ascontiguousarray(np.asarray(L0_ffn_router_w, dtype=np.float32).reshape(8, 128, 8).transpose(1, 0, 2).reshape(128, 64)),
        "c_sgw": np.ascontiguousarray(np.asarray(L0_ffn_shared_gate_w, dtype=np.float32).reshape(8, 128).T),
        "c_biases": biases,
        "c_bhelp": bhelp,
    }
    w1 = np.asarray(L0_ffn_w1, dtype=np.float32)
    w2 = np.asarray(L0_ffn_w2, dtype=np.float32)
    for e in range(NEXP):
        shared[f"w1_{e}"] = np.ascontiguousarray(w1[e])
        shared[f"w2_{e}"] = np.ascontiguousarray(w2[e])

    nc = build_program(_stage)
    in_maps = []
    for c in range(8):
        quar = c % 4
        m = dict(shared)
        cc = _consts(quar)
        m["c_ident"] = cc["ident"]
        m["c_ones128"] = cc["ones128"]
        m["c_ones1x128"] = cc["ones1x128"]
        m["c_half1x128"] = cc["half1x128"]
        m["c_pairmat2"] = cc["pairmat2"]
        m["c_expmat2"] = cc["expmat2"]
        m["c_selmat"] = cc["selmat"]
        m["c_qmask"] = cc["qmask"]
        m["x_t"] = np.ascontiguousarray(x[c * T:(c + 1) * T, :].T)
        in_maps.append(m)

    res = run_bass_kernel_spmd(nc, in_maps, core_ids=list(range(8)))
    out = np.empty((2 * S, E), dtype=np.float32)
    for c in range(8):
        out[c * T:(c + 1) * T, :] = res.results[c]["y_t"].T
    if _stage == "ffn1dbg":
        return out.reshape(2, S, E), [r["dbg_t"] for r in res.results]
    if _stage == "x3dbg":
        return out.reshape(2, S, E), [r["adbg_t"] for r in res.results]
    return out.reshape(2, S, E)
